# version 12
# baseline (speedup 1.0000x reference)
import numpy as np
import ml_dtypes
import concourse.bass as bass
import concourse.mybir as mybir
from concourse.bass_utils import run_bass_kernel_spmd

F32 = mybir.dt.float32
BF16 = mybir.dt.bfloat16
I32 = mybir.dt.int32
AF = mybir.ActivationFunctionType
ALU = mybir.AluOpType
AX = mybir.AxisListType

SEQ, D, NH = 4096, 1024, 4
QB = 512
NQB = SEQ // QB
NT = SEQ // 128
NGRP = SEQ // 1024

ENGINES = ("sync", "tensor", "vector", "scalar", "gpsimd")


class Prog:
    def __init__(self):
        self.steps = {e: [] for e in ENGINES}
        self.cnt = {}
        self.basic_ev = set()

    def add(self, eng, fn, waits=(), sig=None, n=1, basic=False):
        ev = None
        if sig is not None:
            c = self.cnt.get(id(sig), 0) + n
            self.cnt[id(sig)] = c
            ev = (sig, c)
            if basic:
                self.basic_ev.add((id(sig), c))
        self.steps[eng].append((fn, [w for w in waits if w is not None],
                                (sig, n) if sig is not None else None))
        return ev

    def barrier(self, sig, waits=()):
        return [self.add(e, lambda q: q.nop(), waits=waits, sig=sig)
                for e in ENGINES]

    def emit(self, block):
        def mk(eng_name):
            def body(q):
                maxw = {}
                own = {}
                el = getattr(self, "elide", {})
                for fn, waits, sig in self.steps[eng_name]:
                    for (sem, val) in waits:
                        if (el.get(id(sem)) == eng_name
                                and own.get(id(sem), 0) >= val
                                and (eng_name != "vector"
                                     or (id(sem), val) in self.basic_ev)):
                            continue
                        if maxw.get(id(sem), -1) >= val:
                            continue
                        q.wait_ge(sem, val)
                        maxw[id(sem)] = val
                    instr = fn(q)
                    if sig is not None:
                        if instr is None:
                            instr = q.nop()
                        instr.then_inc(sig[0], sig[1])
                        own[id(sig[0])] = own.get(id(sig[0]), 0) + sig[1]
            return body
        for e in ENGINES:
            getattr(block, e)(mk(e))


def build_program(stage=99, dbg=False, dwait=3, dmask=7):
    nc = bass.Bass("TRN2", target_bir_lowering=False, detect_race_conditions=False)

    x_h = nc.dram_tensor("x", [SEQ, D], F32, kind="ExternalInput").ap()
    wq_h = nc.dram_tensor("wq", [256, 16], F32, kind="ExternalInput").ap()
    wk_h = nc.dram_tensor("wk", [256, 16], F32, kind="ExternalInput").ap()
    wv_h = nc.dram_tensor("wv", [256, 16], F32, kind="ExternalInput").ap()
    wi_h = nc.dram_tensor("wi", [4, 3072], F32, kind="ExternalInput").ap()
    bi_h = nc.dram_tensor("bi", [4, 1], F32, kind="ExternalInput").ap()
    wf_h = nc.dram_tensor("wf", [4, 3072], F32, kind="ExternalInput").ap()
    bf_h = nc.dram_tensor("bf", [4, 1], F32, kind="ExternalInput").ap()
    lnw_h = nc.dram_tensor("lnw", [1, D], F32, kind="ExternalInput").ap()
    l1w_h = nc.dram_tensor("l1w", [1, D], F32, kind="ExternalInput").ap()
    l1b_h = nc.dram_tensor("l1b", [1, D], F32, kind="ExternalInput").ap()
    w1_h = nc.dram_tensor("w1", [4096, 1024], F32, kind="ExternalInput").ap()
    b1_h = nc.dram_tensor("b1", [1, 4096], F32, kind="ExternalInput").ap()
    l2w_h = nc.dram_tensor("l2w", [1, 4096], F32, kind="ExternalInput").ap()
    l2b_h = nc.dram_tensor("l2b", [1, 4096], F32, kind="ExternalInput").ap()
    w2_h = nc.dram_tensor("w2", [1024, 4096], F32, kind="ExternalInput").ap()
    b2_h = nc.dram_tensor("b2", [1, D], F32, kind="ExternalInput").ap()
    out_h = nc.dram_tensor("out", [SEQ, D], F32, kind="ExternalOutput").ap()

    KS = "ExternalOutput" if dbg else "Internal"
    xbf_scr = nc.dram_tensor("xbf_scr", [SEQ, D], BF16, kind=KS).ap()
    A_scr = nc.dram_tensor("A_scr", [NH, 256, 256], F32, kind=KS)
    Bv_scr = nc.dram_tensor("Bv_scr", [NH, 256, 256], F32, kind=KS)
    xpost_scr = nc.dram_tensor("xpost_scr", [SEQ, D], F32, kind=KS).ap()
    t1_scr = nc.dram_tensor("t1_scr", [4096, 1024], BF16, kind=KS).ap()
    t2_scr = nc.dram_tensor("t2_scr", [1024, 4096], BF16, kind=KS).ap()
    grow_scr = nc.dram_tensor("grow_scr", [1, 64], F32, kind=KS)
    msc_scr = nc.dram_tensor("msc_scr", [1, 4], F32, kind=KS)
    dbg_stt = nc.dram_tensor("dbg_stt", [4, 128, 64], F32, kind=KS)
    dbg_acc = nc.dram_tensor("dbg_acc", [4, 128, 257], F32, kind=KS)
    dbg_po = nc.dram_tensor("dbg_po", [4, 128, 257], F32, kind=KS)
    dbg_gcol = nc.dram_tensor("dbg_gcol", [128, 512], F32, kind=KS)
    dbg_rcol = nc.dram_tensor("dbg_rcol", [128, 64], F32, kind=KS)
    dbg_gw = nc.dram_tensor("dbg_gw", [4, 3 * SEQ], F32, kind=KS)
    dbg_hat = nc.dram_tensor("dbg_hat", [4, 128, 1024], F32, kind=KS)
    dbg_st2 = nc.dram_tensor("dbg_st2", [4, 4, 128, 64], F32, kind=KS)

    ident_bf_h = nc.inline_tensor(
        np.eye(128, dtype=np.float32).astype(ml_dtypes.bfloat16), name="identb_h").ap()
    ident_f_h = nc.inline_tensor(np.eye(128, dtype=np.float32), name="identf_h").ap()
    ones_row_h = nc.inline_tensor(np.ones((1, 128), np.float32), name="onesr_h").ap()
    ii = np.arange(128)[:, None]
    jj = np.arange(512)[None, :]
    masks_np = np.concatenate(
        [(jj - ii - 128 * r >= 0).astype(np.float32) for r in range(4)], axis=1)
    masks_h = nc.inline_tensor(masks_np, name="masks_h").ap()
    cv = np.zeros((128, 12), np.float32)
    cv[:, 0] = 1e-5
    cv[:, 1] = -0.5
    cv[:, 2] = 1e-8
    cv[:, 3] = 1.0 / 256
    cv[:, 4] = 1.0 / 1024
    cv[:, 5] = 1.0 / 4096
    cv[:, 6] = 127.0
    cv[:, 7] = 1.0
    cv[:, 8] = 1e-30
    cv[:, 9] = 0.5
    cv[:, 10] = -34.53877639491068
    cv[:, 11] = 1e-18
    consts_h = nc.inline_tensor(cv, name="consts_h").ap()

    ARENA_BYTES = 212096
    _arena = nc.alloc_sbuf_tensor("arena", [128, ARENA_BYTES], mybir.dt.uint8)
    BASE = nc.lookup_mloc(_arena).addr

    def T(name, shape, dt, offset):
        return nc.alloc_sbuf_tensor_at(name, shape, dt, offset=BASE + offset)
    consts = T("consts", [128, 12], F32, offset=0)
    dummy4 = T("dummy4", [128, 4], F32, offset=64)
    identb = T("identb", [128, 128], BF16, offset=96)
    identf = T("identf", [128, 128], F32, offset=352)
    onesr = T("onesr", [1, 128], F32, offset=864)
    bstats = T("bstats", [128, 64], F32, offset=1376)
    fcol = T("fcol", [128, 8], F32, offset=1632)
    mrep = T("mrep", [128, 2], F32, offset=1664)
    wq_sb = T("wq_sb", [128, 32], F32, offset=1696)
    wk_sb = T("wk_sb", [128, 32], F32, offset=1824)
    wv_sb = T("wv_sb", [128, 32], F32, offset=1952)
    a16 = T("a16", [128, 32], F32, offset=2080)
    bv16 = T("bv16", [128, 32], F32, offset=2208)
    wgf_all = T("wgf_all", [128, 64], BF16, offset=2336)
    biasg_i = T("biasg_i", [4, 1], F32, offset=2464)
    biasg_f = T("biasg_f", [4, 1], F32, offset=2496)
    A_sb = [[T(f"A_sb{h}_{c}", [128, 256], BF16, offset=2528 + (2 * h + c) * 512)
             for c in range(2)] for h in range(NH)]
    Bv_sb = [[T(f"Bv_sb{h}_{c}", [128, 256], BF16, offset=6624 + (2 * h + c) * 512)
              for c in range(2)] for h in range(NH)]
    gbc_mh = T("gbc_mh", [128, D], F32, offset=10720)
    masks = T("masks", [128, 2048], F32, offset=14816)
    gcol = T("gcol", [128, 512], F32, offset=23008)
    stt_sb = T("stt_sb", [128, 64], F32, offset=25120)
    g2bc = T("g2bc", [128, 4096], BF16, offset=1696)
    b2bc_s = T("b2bc_s", [128, 4096], BF16, offset=9888)
    g1bc = T("g1bc", [128, D], BF16, offset=18080)
    b1bc_s = T("b1bc_s", [128, D], BF16, offset=20128)
    bb2 = T("bb2", [128, D], BF16, offset=22176)
    XF = 25376
    x_fm = [T(f"x_fm{c}", [128, SEQ], BF16, offset=XF + c * 8192) for c in range(8)]
    vt = T("vt", [128, NT * 1028], BF16, offset=XF + 65536)
    W1t = [T(f"W1t{c}", [128, 4096], BF16, offset=XF + c * 8192) for c in range(8)]
    W2t = [T(f"W2t{c}", [128, 1024], BF16, offset=XF + 65536 + c * 2048)
           for c in range(32)]
    R = XF + 65536 + 65792
    gw = T("gw", [4, 3 * SEQ], F32, offset=R)
    wif = T("wif", [8, 3072], F32, offset=R)
    wgs = T("wgs", [8, 1024], F32, offset=R + 12288)
    lnw_sb = T("lnw_sb", [1, D], F32, offset=R + 16384)
    gsm = T("gsm", [4, 64], F32, offset=R + 49152)
    grow = T("grow", [1, 64], F32, offset=R + 49408)
    ones4k = T("ones4k", [4, 1024], F32, offset=R + 49536)
    rcol = T("rcol", [128, 32], F32, offset=R + 53632)
    grep = T("grep", [128, 32], F32, offset=R + 53760)
    zrow = T("zrow", [128, 512], F32, offset=R)
    qf = [[T(f"qf{h}_{c}", [128, QB], BF16, offset=R + (2 * h + c) * 1024)
           for c in range(2)] for h in range(NH)]
    s_sb = [T(f"s_sb{i}", [128, QB], BF16, offset=R + 8192 + i * 1024) for i in range(2)]
    acc = [T(f"acc{i}", [128, 257], F32, offset=R + 10240 + i * 1056) for i in range(4)]
    hatt = [T(f"hatt{i}", [128, D], F32, offset=R + 14464 + i * 4096) for i in range(4)]
    xin = [T(f"xin{i}", [128, D], F32, offset=R + 30848 + i * 4096) for i in range(2)]
    sqscr = T("sqscr", [128, D], F32, offset=R + 39040)
    wstage = [T(f"wstage{i}", [128, 1024], F32, offset=R + i * 4096) for i in range(2)]
    wtern = T("wtern", [128, 1024], BF16, offset=R + 8192)
    wint = T("wint", [128, 1024], I32, offset=R + 10240)
    wabs = T("wabs", [128, 1024], F32, offset=R + 10240)
    wsum = T("wsum", [128, 16], F32, offset=R + 43136)
    xp = [T(f"xp{i}", [128, D], F32, offset=R + i * 4096) for i in range(2)]
    q1 = [T(f"q1_{i}", [128, D], BF16, offset=R + 8192 + i * 2048) for i in range(2)]
    h1 = T("h1", [128, 4096], BF16, offset=R + 12288)
    q2 = T("q2", [128, 4096], BF16, offset=R + 20480)
    sqb = T("sqb", [128, 4096], BF16, offset=R + 28672)
    h2b = T("h2b", [128, 4096], BF16, offset=R + 28672)
    scr32 = T("scr32", [128, 1024], F32, offset=R + 36864)
    hln = T("hln", [128, D], F32, offset=R + 36864)
    ost = T("ost", [128, D], F32, offset=R + 36864)
    bb1 = T("bb1", [128, 4096], BF16, offset=R + 40960)
    scri = T("scri", [128, 512], I32, offset=R + 49152)
    yf_s = T("yf_s", [128, 512], F32, offset=R + 51200)
    adj_s = T("adj_s", [128, 512], F32, offset=R + 53248)
    cbf = T("cbf", [128, 512], BF16, offset=24224)
    vecs = T("vecs", [1, 4096], F32, offset=R + 12288)
    assert BASE + ARENA_BYTES <= 229344 - 500, BASE

    P = Prog()
    from contextlib import ExitStack
    with ExitStack() as _stk:
        sL = _stk.enter_context(nc.semaphore("sL"))
        sLS = _stk.enter_context(nc.semaphore("sLS"))
        sLA = _stk.enter_context(nc.semaphore("sLA"))
        sLG = _stk.enter_context(nc.semaphore("sLG"))
        sT = _stk.enter_context(nc.semaphore("sT"))
        sV = _stk.enter_context(nc.semaphore("sV"))
        sA = _stk.enter_context(nc.semaphore("sA"))
        sG = _stk.enter_context(nc.semaphore("sG"))
        sO = _stk.enter_context(nc.semaphore("sO"))
        sW = _stk.enter_context(nc.semaphore("sW"))
        import os as _os
        _emask = int(_os.environ.get("KELIDE", "15"))
        P.elide = {}
        for _bit, (_sem, _eng) in enumerate([(sV, "vector"), (sT, "tensor"),
                                             (sA, "scalar"), (sG, "gpsimd")]):
            if _emask & (1 << _bit):
                P.elide[id(_sem)] = _eng
        pq = _stk.enter_context(nc.psum_tensor("pq", [128, 512], F32))
        ps0 = _stk.enter_context(nc.psum_tensor("ps0", [128, 512], F32))
        ps1 = _stk.enter_context(nc.psum_tensor("ps1", [128, 512], F32))
        po0 = _stk.enter_context(nc.psum_tensor("po0", [128, 512], F32))
        po1 = _stk.enter_context(nc.psum_tensor("po1", [128, 512], F32))
        po2 = _stk.enter_context(nc.psum_tensor("po2", [128, 512], F32))
        po3 = _stk.enter_context(nc.psum_tensor("po3", [128, 512], F32))
        ptr = _stk.enter_context(nc.psum_tensor("ptr", [128, 1024], BF16))
        block = _stk.enter_context(nc.Block())
        ps = [ps0, ps1]
        po = [po0, po1, po2, po3]

        _dma_sem = {"sync": sLS, "scalar": sLA, "gpsimd": sLG}
        _orig_add = P.add

        def _add(eng, fn, waits=(), sig=None, n=1, basic=False):
            if sig is sL:
                sig = _dma_sem[eng]
            return _orig_add(eng, fn, waits, sig, n, basic)
        P.add = _add

        def _cut():
            for s_ in (sLS, sLA, sLG, sT, sV, sA, sG, sO, sW):
                v = P.cnt.get(id(s_), 0)
                if v:
                    P.add("sync", lambda q, s_=s_, v=v: q.wait_ge(s_, v))
            P.emit(block)
            return True

        def uround(src_ap, scale_ap, dst_ap, waits, clamp=False):
            w = src_ap.shape[1]
            e_1 = P.add("vector", lambda q: q.tensor_scalar(
                out=yf_s[:, 0:w], in0=src_ap, scalar1=scale_ap,
                scalar2=1023.5, op0=ALU.mult, op1=ALU.add), waits=waits, sig=sV, basic=True)
            e_2 = P.add("vector", lambda q: q.tensor_scalar(
                out=scri[:, 0:w], in0=yf_s[:, 0:w],
                scalar1=0.5, scalar2=None, op0=ALU.add), waits=[e_1], sig=sV)
            e_3 = P.add("vector", lambda q: q.tensor_tensor(
                out=adj_s[:, 0:w], in0=yf_s[:, 0:w],
                in1=scri[:, 0:w], op=ALU.is_ge), waits=[e_2], sig=sV, basic=True)
            e_4 = P.add("vector", lambda q: q.scalar_tensor_tensor(
                out=dst_ap, in0=scri[:, 0:w], scalar=-1024.0,
                in1=adj_s[:, 0:w], op0=ALU.add, op1=ALU.add),
                waits=[e_3], sig=sV, basic=True)
            if clamp:
                e_4 = P.add("vector", lambda q: q.tensor_scalar(
                    out=dst_ap, in0=dst_ap, scalar1=-1.0, scalar2=1.0,
                    op0=ALU.max, op1=ALU.min), waits=[e_4], sig=sV, basic=True)
            return e_4

        def uround_bs(src_ap, scale_ap, dst_ap, waits):
            w = src_ap.shape[1]
            e_1 = P.add("vector", lambda q: q.tensor_scalar(
                out=yf_s[:, 0:w], in0=src_ap, scalar1=scale_ap,
                scalar2=1024.5, op0=ALU.mult, op1=ALU.add), waits=waits, sig=sV, basic=True)
            e_2 = P.add("vector", lambda q: q.tensor_scalar(
                out=scri[:, 0:w], in0=yf_s[:, 0:w],
                scalar1=1.0, scalar2=None, op0=ALU.mult), waits=[e_1], sig=sV)
            e_3 = P.add("vector", lambda q: q.scalar_tensor_tensor(
                out=cbf[:, 0:w], in0=scri[:, 0:w], scalar=-1.0,
                in1=yf_s[:, 0:w], op0=ALU.mult, op1=ALU.add), waits=[e_2], sig=sV, basic=True)
            ev = P.add("vector", lambda q: q.tensor_scalar(
                out=dst_ap, in0=cbf[:, 0:w], scalar1=1.0,
                scalar2=16.0, op0=ALU.mult, op1=ALU.add), waits=[e_3], sig=sV, basic=True)
            for B in (16.0, 8.0, 4.0, 2.0, 1.0):
                ea = P.add("vector", lambda q, B=B: q.tensor_scalar(
                    out=cbf[:, 0:w], in0=dst_ap, scalar1=B, scalar2=B,
                    op0=ALU.is_ge, op1=ALU.mult), waits=[ev], sig=sV, basic=True)
                ev = P.add("vector", lambda q: q.tensor_tensor(
                    out=dst_ap, in0=dst_ap, in1=cbf[:, 0:w], op=ALU.subtract),
                    waits=[ea], sig=sV, basic=True)
            ev = P.add("vector", lambda q: q.scalar_tensor_tensor(
                out=dst_ap, in0=yf_s[:, 0:w], scalar=-1024.0,
                in1=dst_ap, op0=ALU.add, op1=ALU.subtract), waits=[ev], sig=sV, basic=True)
            return ev

        ldc = P.add("sync", lambda q: q.dma_start(out=consts[:], in_=consts_h), sig=sL, n=16)
        ldib = P.add("sync", lambda q: q.dma_start(out=identb[:], in_=ident_bf_h), sig=sL, n=16)
        ldif = P.add("sync", lambda q: q.dma_start(out=identf[:], in_=ident_f_h), sig=sL, n=16)
        ldor = P.add("sync", lambda q: q.dma_start(out=onesr[:], in_=ones_row_h), sig=sL, n=16)
        ldmk = P.add("sync", lambda q: q.dma_start(out=masks[:], in_=masks_h), sig=sL, n=16)
        for cc in range(2):
            ldwq = P.add("sync", lambda q, cc=cc: q.dma_start(
                out=wq_sb[:, 16 * cc:16 * cc + 16],
                in_=wq_h[128 * cc:128 * cc + 128, :]), sig=sL, n=16)
            ldwk = P.add("sync", lambda q, cc=cc: q.dma_start(
                out=wk_sb[:, 16 * cc:16 * cc + 16],
                in_=wk_h[128 * cc:128 * cc + 128, :]), sig=sL, n=16)
            ldwv = P.add("sync", lambda q, cc=cc: q.dma_start(
                out=wv_sb[:, 16 * cc:16 * cc + 16],
                in_=wv_h[128 * cc:128 * cc + 128, :]), sig=sL, n=16)
        ldwi = P.add("sync", lambda q: q.dma_start(out=wif[0:4, :], in_=wi_h), sig=sL, n=16)
        ldwf = P.add("sync", lambda q: q.dma_start(out=wif[4:8, :], in_=wf_h), sig=sL, n=16)
        ldbi = P.add("sync", lambda q: q.dma_start(out=biasg_i[:], in_=bi_h), sig=sL, n=16)
        ldbf = P.add("sync", lambda q: q.dma_start(out=biasg_f[:], in_=bf_h), sig=sL, n=16)
        ldln = None

        if stage <= 11:
            if _cut():
                return nc
        ldx = [None] * NT
        stxbf = [None] * NT
        for t in range(NT):
            ldx[t] = P.add("sync", lambda q, t=t: q.dma_start(
                out=xin[t % 2][:], in_=x_h[t * 128:(t + 1) * 128, :]),
                waits=[stxbf[t - 2]] if t >= 2 else [], sig=sL, n=16)
            stxbf[t] = P.add("gpsimd", lambda q, t=t: q.dma_start(
                out=xbf_scr[t * 128:(t + 1) * 128, :], in_=xin[t % 2][:]),
                waits=[ldx[t]], sig=sL, n=16)

        if stage <= 12:
            if _cut():
                return nc
        a16_evs = []
        for c in range(2):
            for i in range(4):
                for i2 in range(4):
                    def f(q, c=c, i=i, i2=i2):
                        return q.tensor_tensor(
                            out=dummy4[:], in0=wq_sb[:, 16 * c + i:16 * c + i + 13:4],
                            in1=wk_sb[:, 16 * c + i2:16 * c + i2 + 13:4], op=ALU.mult)
                    ev = P.add("vector", f, waits=[ldwq, ldwk], sig=sV)
                    def f2(q, c=c, i=i, i2=i2):
                        col = 16 * c + 4 * i + i2
                        return q.reduce_sum(a16[:, col:col + 1], dummy4[:], AX.X)
                    a16_evs.append(P.add("vector", f2, waits=[ev], sig=sV))
        a16_evs.append(P.add("vector", lambda q: q.tensor_scalar(
            out=a16[:], in0=a16[:], scalar1=1.0 / 16.0, scalar2=None, op0=ALU.mult),
            waits=[a16_evs[-1]], sig=sV))
        bv_evs = []
        for c in range(2):
            for i in range(4):
                for o in range(4):
                    def f(q, c=c, i=i, o=o):
                        return q.tensor_copy(
                            bv16[:, 16 * c + 4 * i + o:16 * c + 4 * i + o + 1],
                            wv_sb[:, 16 * c + 4 * o + i:16 * c + 4 * o + i + 1])
                    bv_evs.append(P.add("vector", f, waits=[ldwv], sig=sV))

        if stage <= 13:
            if _cut():
                return nc
        zm = P.add("gpsimd", lambda q: q.memset(zrow[:], 0.0), sig=sG)
        zevs = []
        for h in range(NH):
            for dst in (A_scr, Bv_scr):
                def fz(q, h=h, dst=dst):
                    return q.dma_start(
                        out=bass.AP(dst, h * 65536, [[512, 128], [1, 512]]),
                        in_=zrow[:])
                zevs.append(P.add("gpsimd", fz, waits=[zm], sig=sL, n=16))
        scat = []
        for h in range(NH):
            c, poff = h // 2, (h % 2) * 64
            def fa(q, h=h, c=c, poff=poff):
                return q.dma_start(
                    out=bass.AP(A_scr, h * 65536, [[1028, 64], [256, 4], [1, 4]]),
                    in_=a16[poff:poff + 64, 16 * c:16 * c + 16])
            def fb(q, h=h, c=c, poff=poff):
                return q.dma_start(
                    out=bass.AP(Bv_scr, h * 65536, [[1028, 64], [256, 4], [1, 4]]),
                    in_=bv16[poff:poff + 64, 16 * c:16 * c + 16])
            scat.append(P.add("gpsimd", fa, waits=[a16_evs[-1], zevs[-1]], sig=sL, n=16))
            scat.append(P.add("gpsimd", fb, waits=[bv_evs[-1], zevs[-1]], sig=sL, n=16))
        if stage <= 14:
            if _cut():
                return nc
        ldA = []
        for h in range(NH):
            for c in range(2):
                ldA.append(P.add("gpsimd", lambda q, h=h, c=c: q.dma_start(
                    out=A_sb[h][c][:], in_=A_scr.ap()[h, 128 * c:128 * c + 128, :]),
                    waits=[scat[-1]], sig=sL, n=16))
                ldA.append(P.add("gpsimd", lambda q, h=h, c=c: q.dma_start(
                    out=Bv_sb[h][c][:], in_=Bv_scr.ap()[h, 128 * c:128 * c + 128, :]),
                    waits=[scat[-1]], sig=sL, n=16))

        if stage <= 15:
            if _cut():
                return nc
        g1 = P.add("vector", lambda q: q.tensor_tensor(
            out=wgs[:], in0=wif[:, 0:1024], in1=wif[:, 1024:2048], op=ALU.add),
            waits=[ldwi, ldwf], sig=sV)
        g2 = P.add("vector", lambda q: q.tensor_tensor(
            out=wgs[:], in0=wgs[:], in1=wif[:, 2048:3072], op=ALU.add),
            waits=[g1], sig=sV)
        if stage <= 17:
            if _cut():
                return nc
        wg_evs = []
        prev = None
        for c in range(8):
            tev = P.add("tensor", lambda q, c=c: q.transpose(
                pq[:, 0:8], wgs[:, 128 * c:128 * c + 128], identf[0:8, 0:8]),
                waits=[g2, ldif] + ([prev] if prev else []), sig=sT)
            prev = P.add("vector", lambda q, c=c: q.tensor_copy(
                wgf_all[:, 8 * c:8 * c + 8], pq[:, 0:8]), waits=[tev], sig=sV)
            wg_evs.append(prev)

        if stage <= 18:
            if _cut():
                return nc
        gmh_ld = P.add("sync", lambda q: q.dma_start(
            out=gbc_mh[:], in_=lnw_h[0:1, :].to_broadcast((128, D))), sig=sL, n=16)
        gmh = [P.add("vector", lambda q: q.tensor_scalar(
            out=gbc_mh[:], in0=gbc_mh[:], scalar1=1.0, scalar2=None, op0=ALU.add),
            waits=[gmh_ld], sig=sV)]

        if stage <= 16:
            if _cut():
                return nc
        xfm_evs = []
        for c in range(8):
            xfm_evs.append(P.add("sync", lambda q, c=c: q.dma_start_transpose(
                x_fm[c][:], xbf_scr[:, 128 * c:128 * c + 128]),
                waits=[stxbf[NT - 1]], sig=sL, n=16))

        if stage <= 21:
            if _cut():
                return nc
        gate_ev = []
        for blk in range(8):
            last_f = None
            for c in range(8):
                def fi(q, blk=blk, c=c):
                    return q.matmul(pq[0:4, :], wgf_all[:, 8 * c:8 * c + 4],
                                    x_fm[c][:, 512 * blk:512 * blk + 512],
                                    start=(c == 0), stop=(c == 7))
                def ff(q, blk=blk, c=c):
                    return q.matmul(po1[0:4, :], wgf_all[:, 8 * c + 4:8 * c + 8],
                                    x_fm[c][:, 512 * blk:512 * blk + 512],
                                    start=(c == 0), stop=(c == 7))
                w = [xfm_evs[-1], wg_evs[-1], gmh[-1], ldA[-1]] if c == 0 else []
                if c == 0 and gate_ev:
                    w.append(gate_ev[-1])
                P.add("tensor", fi, waits=w, sig=None)
                last_f = P.add("tensor", ff, sig=sT if c == 7 else None)
            gate_ev.append(P.add("vector", lambda q, blk=blk: q.tensor_copy(
                gw[0:4, 512 * blk:512 * blk + 512], pq[0:4, :]),
                waits=[last_f] + ([scat[-1]] if blk == 0 else []), sig=sV))
            gate_ev.append(P.add("vector", lambda q, blk=blk: q.tensor_copy(
                gw[0:4, SEQ + 512 * blk:SEQ + 512 * blk + 512], po1[0:4, :]),
                waits=[last_f], sig=sV))
        gb1 = P.add("vector", lambda q: q.tensor_scalar(
            out=gw[0:4, 0:SEQ], in0=gw[0:4, 0:SEQ], scalar1=biasg_i[:],
            scalar2=None, op0=ALU.add), waits=[gate_ev[-1], ldbi], sig=sV)
        gb2 = P.add("vector", lambda q: q.tensor_scalar(
            out=gw[0:4, SEQ:2 * SEQ], in0=gw[0:4, SEQ:2 * SEQ], scalar1=biasg_f[:],
            scalar2=None, op0=ALU.add), waits=[gate_ev[-1], ldbf], sig=sV)

        on4 = P.add("gpsimd", lambda q: q.memset(ones4k[:], 1.0),
                    waits=[zevs[-1]], sig=sG)
        sp1 = P.add("scalar", lambda q: q.activation(
            gw[0:4, SEQ:2 * SEQ], gw[0:4, SEQ:2 * SEQ], AF.Exp, scale=-1.0),
            waits=[gb2], sig=sA)
        sp2 = P.add("scalar", lambda q: q.activation(
            gw[0:4, SEQ:2 * SEQ], gw[0:4, SEQ:2 * SEQ], AF.Ln, bias=1.0),
            waits=[sp1], sig=sA)
        ev = None
        for ck in range(4):
            ev = P.add("vector", lambda q, ck=ck: q.tensor_tensor_scan(
                out=gw[0:4, 2 * SEQ + 1024 * ck:2 * SEQ + 1024 * ck + 1024],
                data0=ones4k[:], data1=gw[0:4, SEQ + 1024 * ck:SEQ + 1024 * ck + 1024],
                initial=0.0, op0=ALU.mult, op1=ALU.add),
                waits=[sp2, on4] if ck == 0 else [ev], sig=sV)
            if ck > 0:
                ev = P.add("vector", lambda q, ck=ck: q.tensor_scalar(
                    out=gw[0:4, 2 * SEQ + 1024 * ck:2 * SEQ + 1024 * ck + 1024],
                    in0=gw[0:4, 2 * SEQ + 1024 * ck:2 * SEQ + 1024 * ck + 1024],
                    scalar1=gw[0:4, 2 * SEQ + 1024 * ck - 1:2 * SEQ + 1024 * ck],
                    scalar2=None, op0=ALU.add), waits=[ev], sig=sV)
        cs_ev = ev
        u_ev = P.add("vector", lambda q: q.tensor_tensor(
            out=gw[0:4, 0:SEQ], in0=gw[0:4, 0:SEQ], in1=gw[0:4, 2 * SEQ:3 * SEQ],
            op=ALU.add), waits=[cs_ev, gb1], sig=sV)
        ev = None
        for ck in range(4):
            ev = P.add("vector", lambda q, ck=ck: q.tensor_tensor_scan(
                out=gw[0:4, SEQ + 1024 * ck:SEQ + 1024 * ck + 1024],
                data0=ones4k[:], data1=gw[0:4, 1024 * ck:1024 * ck + 1024],
                initial=-3.0e38, op0=ALU.mult, op1=ALU.max),
                waits=[u_ev, on4] if ck == 0 else [ev], sig=sV)
            if ck > 0:
                ev = P.add("vector", lambda q, ck=ck: q.tensor_scalar(
                    out=gw[0:4, SEQ + 1024 * ck:SEQ + 1024 * ck + 1024],
                    in0=gw[0:4, SEQ + 1024 * ck:SEQ + 1024 * ck + 1024],
                    scalar1=gw[0:4, SEQ + 1024 * ck - 1:SEQ + 1024 * ck],
                    scalar2=None, op0=ALU.max), waits=[ev], sig=sV)
        m_ev = ev
        rcop = []
        for g in range(8):
            rcop.append(P.add("vector", lambda q, g=g: q.tensor_copy(
                gsm[0:4, 8 + g:9 + g],
                gw[0:4, SEQ + 512 * g + 511:SEQ + 512 * g + 512]),
                waits=[m_ev], sig=sV))
        eu_evs = []
        for g in range(8):
            eu_evs.append(P.add("vector", lambda q, g=g: q.tensor_scalar(
                out=gw[0:4, 512 * g:512 * g + 512],
                in0=gw[0:4, 512 * g:512 * g + 512],
                scalar1=gsm[0:4, 8 + g:9 + g], scalar2=None, op0=ALU.subtract),
                waits=[rcop[g]] + rcop, sig=sV))
        eu2 = P.add("scalar", lambda q: q.activation(
            gw[0:4, 0:SEQ], gw[0:4, 0:SEQ], AF.Exp), waits=eu_evs, sig=sA)
        rflat = None
        for h_ in range(4):
            rflat = P.add("gpsimd", lambda q, h_=h_: q.dma_start(
                out=grow[0:1, 8 * h_:8 * h_ + 8], in_=gsm[h_:h_ + 1, 8:16]),
                waits=rcop, sig=sL, n=16)
        gd_evs = []
        for g in range(8):
            gp = max(g - 1, 0)
            gd_evs.append(P.add("vector", lambda q, g=g, gp=gp: q.tensor_tensor(
                out=gsm[0:4, 16 + g:17 + g], in0=gsm[0:4, 8 + gp:9 + gp],
                in1=gsm[0:4, 8 + g:9 + g], op=ALU.subtract),
                waits=[rcop[g], rcop[gp]], sig=sV))
        gdflat = None
        for h_ in range(4):
            gdflat = P.add("gpsimd", lambda q, h_=h_: q.dma_start(
                out=grow[0:1, 32 + 8 * h_:32 + 8 * h_ + 8],
                in_=gsm[h_:h_ + 1, 16:24]),
                waits=gd_evs + [rflat], sig=sL, n=16)
        gdexp = P.add("scalar", lambda q: q.activation(
            grow[:, 32:64], grow[:, 32:64], AF.Exp), waits=[gdflat], sig=sA)
        st_grow = P.add("gpsimd", lambda q: q.dma_start(
            out=grow_scr.ap(), in_=grow[:]), waits=[gdexp, rflat], sig=sL, n=16)
        rcol_ev = P.add("sync", lambda q: q.dma_start(
            out=rcol[:], in_=grow_scr.ap()[0:1, 0:32].to_broadcast((128, 32))),
            waits=[st_grow], sig=sL, n=16)
        grep_ev = P.add("sync", lambda q: q.dma_start(
            out=grep[:], in_=grow_scr.ap()[0:1, 32:64].to_broadcast((128, 32))),
            waits=[st_grow], sig=sL, n=16)
        grp_evs = [grep_ev] * 8

        if stage <= 22:
            if _cut():
                return nc
        gcol_evs = [None] * NT
        prevc = [None, None]
        for t in range(NT):
            bank = pq if t % 2 == 0 else po1
            tlast = None
            for qi_ in range(3):
                tlast = P.add("tensor", lambda q, t=t, bank=bank, qi_=qi_: q.transpose(
                    bank[:, 4 * qi_:4 * qi_ + 4],
                    gw[0:4, SEQ * qi_ + 128 * t:SEQ * qi_ + 128 * t + 128],
                    identf[0:4, 0:4]),
                    waits=([eu2, rcol_ev, grep_ev] if qi_ == 0 else []) +
                          ([prevc[t % 2]] if (qi_ == 0 and prevc[t % 2]) else []),
                    sig=sT if qi_ == 2 else None)
            gcol_evs[t] = P.add("vector", lambda q, t=t, bank=bank: q.tensor_copy(
                gcol[:, 16 * t:16 * t + 12], bank[:, 0:12]), waits=[tlast], sig=sV)
            prevc[t % 2] = gcol_evs[t]

        if stage <= 23:
            if _cut():
                return nc
        if dbg and (dmask & 4):
            P.add("sync", lambda q: q.dma_start(out=dbg_gcol.ap(), in_=gcol[:]),
                  waits=[gcol_evs[NT - 1], gcol_evs[NT - 2]], sig=sL, n=16)
            P.add("sync", lambda q: q.dma_start(out=dbg_rcol.ap()[:, 0:32], in_=rcol[:]),
                  waits=[rcol_ev], sig=sL, n=16)
            P.add("sync", lambda q: q.dma_start(out=dbg_rcol.ap()[:, 32:64], in_=grep[:]),
                  waits=[grep_ev], sig=sL, n=16)
            P.add("sync", lambda q: q.dma_start(out=dbg_gw.ap(), in_=gw[:]),
                  waits=[eu2], sig=sL, n=16)
        bank_free = {0: gcol_evs[NT - 2], 1: gcol_evs[NT - 1]}
        vt_last = None
        for t in range(NT):
            for pair in range(2):
                bank = pq if pair == 0 else po1
                last = None
                for hh in range(2):
                    h = 2 * pair + hh
                    for c in range(2):
                        def f(q, h=h, c=c, hh=hh, t=t, bank=bank):
                            return q.matmul(
                                bank[:, 256 * hh:256 * hh + 256],
                                x_fm[2 * h + c][:, 128 * t:128 * t + 128],
                                Bv_sb[h][c][:], start=(c == 0), stop=(c == 1))
                        w = [ldA[-1], xfm_evs[-1], bank_free[pair]] \
                            if (hh == 0 and c == 0) else []
                        last = P.add("tensor", f, waits=w,
                                     sig=sT if (hh == 1 and c == 1) else None)
                ev_sc = None
                for hh in range(2):
                    h = 2 * pair + hh
                    ev_sc = P.add("vector", lambda q, h=h, hh=hh, t=t, bank=bank:
                                  q.tensor_scalar(
                                      out=vt[:, 1028 * t + 257 * h:1028 * t + 257 * h + 256],
                                      in0=bank[:, 256 * hh:256 * hh + 256],
                                      scalar1=gcol[:, 16 * t + h:16 * t + h + 1],
                                      scalar2=None, op0=ALU.mult),
                                  waits=[last, gcol_evs[t]], sig=sV)
                    P.add("vector", lambda q, h=h, t=t: q.tensor_copy(
                        vt[:, 1028 * t + 257 * h + 256:1028 * t + 257 * h + 257],
                        gcol[:, 16 * t + h:16 * t + h + 1]),
                        waits=[gcol_evs[t]], sig=sV)
                bank_free[pair] = ev_sc
                vt_last = ev_sc
        vt_all = vt_last

        if stage <= 24:
            if _cut():
                return nc
        xpost_st = [None] * NT
        ps_free = [None, None]
        po_free = [None] * 4
        q_bank_free = {0: bank_free[0], 1: bank_free[1]}
        q_bank_free = {0: bank_free[0], 1: None}
        sct = 0
        for qb in range(NQB):
            qf_evs = []
            rnd = 0
            for h in range(NH):
                for m in range(2):
                    bank = pq if rnd % 2 == 0 else po0
                    bsel = rnd % 2
                    last = None
                    for c in range(2):
                        def f(q, h=h, m=m, c=c, qb=qb, bank=bank):
                            return q.matmul(
                                bank[:], A_sb[h][c][:, 128 * m:128 * m + 128],
                                x_fm[2 * h + c][:, QB * qb:QB * qb + QB],
                                start=(c == 0), stop=(c == 1))
                        w = []
                        if c == 0:
                            if qb == 0 and rnd == 0:
                                w.append(vt_all)
                            if q_bank_free[bsel] is not None:
                                w.append(q_bank_free[bsel])
                            if bsel == 1 and po_free[0] is not None:
                                w.append(po_free[0])
                        last = P.add("tensor", f, waits=w, sig=sT if c == 1 else None)
                    ev = P.add("vector", lambda q, h=h, m=m, bank=bank: q.tensor_copy(
                        qf[h][m][:], bank[:]), waits=[last], sig=sV)
                    q_bank_free[bsel] = ev
                    qf_evs.append(ev)
                    rnd += 1
            qf_ready = qf_evs[-1]
            if dbg and (dmask & 8) and qb == 0:
                P.add("gpsimd", lambda q: q.dma_start(
                    out=dbg_po.ap()[0][:, 0:257],
                    in_=vt[:, 0:257]), waits=[vt_all], sig=sL, n=16)
                P.add("gpsimd", lambda q: q.dma_start(
                    out=dbg_po.ap()[1][:, 0:256],
                    in_=qf[0][0][:, 0:256]), waits=[qf_ready], sig=sL, n=16)
            po_free[0] = q_bank_free[1]

            acc_ev = {h: [None] * 4 for h in range(NH)}

            def emit_out(job, sev, sb):
                h, gi, jt, s_, e_ = job
                grp_g = jt // 4
                last_out = None
                for ic in range(4):
                    def f(q, sb=sb, ic=ic, h=h, jt=jt, s=s_, e=e_):
                        return q.matmul(
                            po[ic][:, 0:257],
                            s_sb[sb][:, 128 * ic:128 * ic + 128],
                            vt[:, 1028 * jt + 257 * h:1028 * jt + 257 * h + 257],
                            start=s, stop=e)
                    w = [sev]
                    if s_ and po_free[ic] is not None:
                        w.append(po_free[ic])
                    last_out = P.add("tensor", f, waits=w,
                                     sig=sT if ic == 3 else None)
                if not e_:
                    return last_out
                for ic in range(4):
                    if gi == 0:
                        ev = P.add("vector", lambda q, ic=ic: q.tensor_copy(
                            acc[ic][:], po[ic][:, 0:257]), waits=[last_out], sig=sV)
                    else:
                        def f2(q, ic=ic, h=h, grp_g=grp_g):
                            return q.scalar_tensor_tensor(
                                out=acc[ic][:], in0=acc[ic][:],
                                scalar=grep[:, 8 * h + grp_g:8 * h + grp_g + 1],
                                in1=po[ic][:, 0:257], op0=ALU.mult, op1=ALU.add)
                        ev = P.add("vector", f2,
                                   waits=[last_out, grp_evs[grp_g], acc_ev[h][ic]],
                                   sig=sV)
                    po_free[ic] = ev
                    acc_ev[h][ic] = ev
                return last_out

            for h in range(NH):
                n_jt = 4 * (qb + 1)
                jobs = [(h, jt // 4, jt, jt % 4 == 0, jt % 4 == 3)
                        for jt in range(n_jt)]
                pend = None
                for job in jobs + [None]:
                    if job is not None:
                        _h, gi, jt, s_, e_ = job
                        sb = sct % 2
                        last = None
                        for c in range(2):
                            def f(q, h=h, c=c, jt=jt, sb=sb):
                                return q.matmul(
                                    ps[sb][:], x_fm[2 * h + c][:, 128 * jt:128 * jt + 128],
                                    qf[h][c][:], start=(c == 0), stop=(c == 1))
                            w = [qf_ready] if c == 0 else []
                            if c == 0 and ps_free[sb] is not None:
                                w.append(ps_free[sb])
                            last = P.add("tensor", f, waits=w,
                                         sig=sT if c == 1 else None)
                    if pend is not None:
                        ps_free[pend[2]] = emit_out(pend[0], pend[1], pend[2])
                    if job is not None:
                        diag = (jt >= 4 * qb)
                        if diag:
                            r = jt - 4 * qb
                            sev = P.add("vector", lambda q, sb=sb, r=r: q.tensor_tensor(
                                out=s_sb[sb][:], in0=ps[sb][:],
                                in1=masks[:, 512 * r:512 * r + 512], op=ALU.mult),
                                waits=[last, ldmk], sig=sV)
                        else:
                            sev = P.add("scalar", lambda q, sb=sb: q.copy(
                                s_sb[sb][:], ps[sb][:]), waits=[last], sig=sA)
                        pend = (job, sev, sb)
                        sct += 1
                    else:
                        pend = None

                ep = []
                for ic in range(4):
                    tg = 4 * qb + ic
                    ep.append(P.add("vector", lambda q, ic=ic: q.reduce_sum(
                        stt_sb[:, ic:ic + 1], acc[ic][:, 0:256], AX.X),
                        waits=[acc_ev[h][ic]], sig=sV))
                    ep.append(P.add("scalar", lambda q, ic=ic: q.activation(
                        sqscr[:, 0:256], acc[ic][:, 0:256], AF.Square,
                        accum_out=stt_sb[:, 4 + ic:5 + ic]),
                        waits=[acc_ev[h][ic]], sig=sA))
                    ep.append(P.add("vector", lambda q, ic=ic: q.tensor_copy(
                        stt_sb[:, 8 + ic:9 + ic], acc[ic][:, 256:257]),
                        waits=[acc_ev[h][ic]], sig=sV))
                    gq = qb
                    ep.append(P.add("vector", lambda q, ic=ic, tg=tg, h=h, gq=gq:
                              q.tensor_tensor(
                                  out=stt_sb[:, 12 + ic:13 + ic],
                                  in0=rcol[:, 8 * h + gq:8 * h + gq + 1],
                                  in1=gcol[:, 16 * tg + 4 + h:16 * tg + 5 + h],
                                  op=ALU.subtract),
                              waits=[gcol_evs[tg], rcol_ev], sig=sV))
                    ep.append(P.add("vector", lambda q, ic=ic, tg=tg, h=h: q.tensor_tensor(
                        out=stt_sb[:, 16 + ic:17 + ic],
                        in0=gcol[:, 16 * tg + 8 + h:16 * tg + 9 + h],
                        in1=gcol[:, 16 * tg + 4 + h:16 * tg + 5 + h], op=ALU.subtract),
                        waits=[gcol_evs[tg]], sig=sV))
                e1 = P.add("scalar", lambda q: q.activation(
                    stt_sb[:, 20:28], stt_sb[:, 12:20], AF.Exp), waits=ep, sig=sA)
                e2 = P.add("scalar", lambda q: q.activation(
                    stt_sb[:, 8:12], stt_sb[:, 8:12], AF.Abs), waits=ep, sig=sA)
                e3 = P.add("vector", lambda q: q.tensor_tensor(
                    out=stt_sb[:, 40:44], in0=stt_sb[:, 20:24], in1=stt_sb[:, 8:12],
                    op=ALU.mult), waits=[e1, e2], sig=sV)
                e4 = P.add("vector", lambda q: q.tensor_tensor(
                    out=stt_sb[:, 40:44], in0=stt_sb[:, 40:44], in1=stt_sb[:, 24:28],
                    op=ALU.max), waits=[e3], sig=sV)
                e5 = P.add("vector", lambda q: q.tensor_scalar(
                    out=stt_sb[:, 40:44], in0=stt_sb[:, 40:44],
                    scalar1=consts[:, 2:3], scalar2=None, op0=ALU.add),
                    waits=[e4, ldc], sig=sV)
                e6 = P.add("vector", lambda q: q.reciprocal(
                    stt_sb[:, 44:48], stt_sb[:, 40:44]), waits=[e5], sig=sV)
                e7 = P.add("vector", lambda q: q.tensor_tensor(
                    out=stt_sb[:, 20:24], in0=stt_sb[:, 20:24], in1=stt_sb[:, 44:48],
                    op=ALU.mult), waits=[e6], sig=sV)
                e8 = P.add("vector", lambda q: q.tensor_scalar(
                    out=stt_sb[:, 28:32], in0=stt_sb[:, 0:4], scalar1=consts[:, 3:4],
                    scalar2=None, op0=ALU.mult), waits=ep, sig=sV)
                e9 = P.add("vector", lambda q: q.tensor_tensor(
                    out=stt_sb[:, 48:52], in0=stt_sb[:, 28:32], in1=stt_sb[:, 28:32],
                    op=ALU.mult), waits=[e8], sig=sV)
                e10 = P.add("vector", lambda q: q.tensor_scalar(
                    out=stt_sb[:, 32:36], in0=stt_sb[:, 4:8], scalar1=consts[:, 3:4],
                    scalar2=None, op0=ALU.mult), waits=ep, sig=sV)
                e11 = P.add("vector", lambda q: q.tensor_tensor(
                    out=stt_sb[:, 32:36], in0=stt_sb[:, 32:36], in1=stt_sb[:, 48:52],
                    op=ALU.subtract), waits=[e9, e10], sig=sV)
                e12 = P.add("vector", lambda q: q.tensor_scalar(
                    out=stt_sb[:, 32:36], in0=stt_sb[:, 32:36], scalar1=0.0,
                    scalar2=None, op0=ALU.max), waits=[e11], sig=sV)
                e12a = P.add("vector", lambda q: q.tensor_scalar(
                    out=stt_sb[:, 48:52], in0=stt_sb[:, 32:36], scalar1=1e14,
                    scalar2=None, op0=ALU.mult), waits=[e12], sig=sV)
                e12b = P.add("scalar", lambda q: q.activation(
                    stt_sb[:, 48:52], stt_sb[:, 48:52], AF.Ln),
                    waits=[e12a], sig=sA)
                e12bb = P.add("vector", lambda q: q.tensor_scalar(
                    out=stt_sb[:, 48:52], in0=stt_sb[:, 48:52], scalar1=0.5,
                    scalar2=-16.11809565095832, op0=ALU.mult, op1=ALU.add),
                    waits=[e12b], sig=sV)
                e12c = P.add("scalar", lambda q: q.activation(
                    stt_sb[:, 48:52], stt_sb[:, 48:52], AF.Exp),
                    waits=[e12bb], sig=sA)
                e13 = P.add("vector", lambda q: q.tensor_tensor(
                    out=stt_sb[:, 48:52], in0=stt_sb[:, 20:24], in1=stt_sb[:, 48:52],
                    op=ALU.mult), waits=[e12c, e7], sig=sV)
                e13b = P.add("vector", lambda q: q.tensor_tensor(
                    out=stt_sb[:, 52:56], in0=stt_sb[:, 48:52], in1=stt_sb[:, 48:52],
                    op=ALU.mult), waits=[e13], sig=sV)
                e13c = P.add("vector", lambda q: q.tensor_scalar(
                    out=stt_sb[:, 52:56], in0=stt_sb[:, 52:56], scalar1=consts[:, 0:1],
                    scalar2=None, op0=ALU.add), waits=[e13b], sig=sV)
                e14 = P.add("scalar", lambda q: q.activation(
                    stt_sb[:, 52:56], stt_sb[:, 52:56], AF.Ln), waits=[e13c], sig=sA)
                e15 = P.add("scalar", lambda q: q.activation(
                    stt_sb[:, 52:56], stt_sb[:, 52:56], AF.Exp, scale=consts[:, 1:2]),
                    waits=[e14], sig=sA)
                e16 = P.add("vector", lambda q: q.tensor_tensor(
                    out=stt_sb[:, 36:40], in0=stt_sb[:, 20:24], in1=stt_sb[:, 52:56],
                    op=ALU.mult), waits=[e15], sig=sV)
                if dbg and (dmask & 8) and qb == 0 and h == 0:
                    for icd in range(4):
                        P.add("sync", lambda q, icd=icd: q.dma_start(
                            out=dbg_acc.ap()[icd], in_=acc[icd][:]),
                            waits=[acc_ev[h][icd]], sig=sL, n=16)
                dmp = None
                if dbg and (dmask & 1) and qb >= 6:
                    dmp = P.add("sync", lambda q, h=h, qb=qb: q.dma_start(
                        out=dbg_st2.ap()[qb - 6, h], in_=stt_sb[:]),
                        waits=[e16], sig=sL, n=16)
                if dbg and (dmask & 16) and qb == 0:
                    dmp = P.add("sync", lambda q, h=h: q.dma_start(
                        out=dbg_st2.ap()[0, h], in_=stt_sb[:]),
                        waits=[e16], sig=sL, n=16)
                last_apply = None
                for ic in range(4):
                    def f(q, ic=ic, h=h):
                        return q.tensor_scalar(
                            out=hatt[ic][:, 256 * h:256 * h + 256],
                            in0=acc[ic][:, 0:256],
                            scalar1=stt_sb[:, 28 + ic:29 + ic],
                            scalar2=stt_sb[:, 36 + ic:37 + ic],
                            op0=ALU.subtract, op1=ALU.mult)
                    w = [e16, dmp if (dwait & 1) else None]
                    if h == 0 and qb > 0:
                        w.append(xpost_st[4 * (qb - 1) + ic])
                    last_apply = P.add("vector", f, waits=w, sig=sV)

            for ic in range(4):
                t = 4 * qb + ic
                w = [stxbf[NT - 2 + (t % 2)], eu2, ldA[-1], gcol_evs[NT - 1],
                     gcol_evs[NT - 2]]
                if t >= 2:
                    w.append(xpost_st[t - 2])
                xold = P.add("sync", lambda q, t=t: q.dma_start(
                    out=xin[t % 2][:], in_=x_h[128 * t:128 * t + 128, :]),
                    waits=w, sig=sL, n=16)
                hdmp = None
                if dbg and (dmask & 2) and qb == 7:
                    hdmp = P.add("sync", lambda q, ic=ic: q.dma_start(
                        out=dbg_hat.ap()[ic], in_=hatt[ic][:]),
                        waits=[last_apply], sig=sL, n=16)
                gam = P.add("gpsimd", lambda q, ic=ic: q.tensor_tensor(
                    out=hatt[ic][:], in0=hatt[ic][:], in1=gbc_mh[:], op=ALU.mult),
                    waits=[last_apply, hdmp if (dwait & 2) else None, gmh[-1]], sig=sG)
                r1 = P.add("vector", lambda q, ic=ic, t=t: q.scalar_tensor_tensor(
                    out=hatt[ic][:], in0=hatt[ic][:], scalar=1.0, in1=xin[t % 2][:],
                    op0=ALU.mult, op1=ALU.add,
                    accum_out=stt_sb[:, 52 + ic:53 + ic]), waits=[gam, xold], sig=sV)
                r2 = P.add("scalar", lambda q, ic=ic: q.activation(
                    sqscr[:], hatt[ic][:], AF.Square,
                    accum_out=stt_sb[:, 56 + ic:57 + ic]), waits=[r1], sig=sA)
                r3 = P.add("vector", lambda q, ic=ic: q.tensor_scalar(
                    out=stt_sb[:, 60 + ic:61 + ic], in0=stt_sb[:, 52 + ic:53 + ic],
                    scalar1=consts[:, 4:5], scalar2=None, op0=ALU.mult),
                    waits=[r1], sig=sV)
                r4 = P.add("vector", lambda q, ic=ic: q.tensor_tensor(
                    out=stt_sb[:, 48 + ic:49 + ic], in0=stt_sb[:, 60 + ic:61 + ic],
                    in1=stt_sb[:, 60 + ic:61 + ic], op=ALU.mult), waits=[r3], sig=sV)
                r5 = P.add("vector", lambda q, ic=ic: q.tensor_scalar(
                    out=stt_sb[:, 44 + ic:45 + ic], in0=stt_sb[:, 56 + ic:57 + ic],
                    scalar1=consts[:, 4:5], scalar2=None, op0=ALU.mult),
                    waits=[r2], sig=sV)
                r6 = P.add("vector", lambda q, ic=ic: q.tensor_tensor(
                    out=stt_sb[:, 44 + ic:45 + ic], in0=stt_sb[:, 44 + ic:45 + ic],
                    in1=stt_sb[:, 48 + ic:49 + ic], op=ALU.subtract),
                    waits=[r4, r5], sig=sV)
                r7 = P.add("scalar", lambda q, ic=ic: q.activation(
                    stt_sb[:, 40 + ic:41 + ic], stt_sb[:, 44 + ic:45 + ic],
                    AF.Ln, bias=consts[:, 0:1]), waits=[r6], sig=sA)
                r8 = P.add("scalar", lambda q, ic=ic: q.activation(
                    stt_sb[:, 40 + ic:41 + ic], stt_sb[:, 40 + ic:41 + ic],
                    AF.Exp, scale=consts[:, 1:2]), waits=[r7], sig=sA)
                r9 = P.add("vector", lambda q, ic=ic: q.tensor_scalar(
                    out=hatt[ic][:], in0=hatt[ic][:],
                    scalar1=stt_sb[:, 60 + ic:61 + ic],
                    scalar2=stt_sb[:, 40 + ic:41 + ic],
                    op0=ALU.subtract, op1=ALU.mult), waits=[r8], sig=sV)
                xpost_st[t] = P.add("sync", lambda q, t=t, ic=ic: q.dma_start(
                    out=xpost_scr[128 * t:128 * t + 128, :], in_=hatt[ic][:]),
                    waits=[r9], sig=sL, n=16)

        if stage <= 25:
            if _cut():
                return nc
        bar0 = P.barrier(sW)

        tern_done = []
        mrep_evs = []
        for wi_, (src, dst, nch, nsub) in enumerate(
                [(w1_h, t1_scr, 32, 1), (w2_h, t2_scr, 8, 4)]):
            prev_use = {}
            last_acc = None
            for ch in range(nch):
                for sb_ in range(nsub):
                    k = ch * nsub + sb_
                    def fl(q, src=src, ch=ch, sb_=sb_, k=k):
                        return q.dma_start(
                            out=wstage[k % 2][:],
                            in_=src[128 * ch:128 * ch + 128, 1024 * sb_:1024 * sb_ + 1024])
                    w = list(bar0) if k == 0 else []
                    if k % 2 in prev_use:
                        w.append(prev_use[k % 2])
                    lde = P.add("scalar", fl, waits=w, sig=sL, n=16)
                    abe = P.add("scalar", lambda q, k=k: q.activation(
                        wabs[:], wstage[k % 2][:], AF.Abs, accum_out=wsum[:, 0:1]),
                        waits=[lde] + ([last_acc] if last_acc else list(bar0)), sig=sA)
                    def fac(q, wi_=wi_, k=k):
                        if k == 0:
                            return q.tensor_copy(wsum[:, 2 + wi_:3 + wi_], wsum[:, 0:1])
                        return q.tensor_tensor(
                            out=wsum[:, 2 + wi_:3 + wi_], in0=wsum[:, 2 + wi_:3 + wi_],
                            in1=wsum[:, 0:1], op=ALU.add)
                    last_acc = P.add("vector", fac, waits=[abe], sig=sV)
                    prev_use[k % 2] = last_acc
            t1e = P.add("tensor", lambda q, wi_=wi_: q.matmul(
                pq[0:1, 0:1], consts[:, 7:8], wsum[:, 2 + wi_:3 + wi_],
                start=True, stop=True), waits=[last_acc] + bar0, sig=sT)
            t2e = P.add("vector", lambda q: q.tensor_copy(
                wsum[0:1, 8:9], pq[0:1, 0:1]), waits=[t1e], sig=sV)
            t3e = P.add("vector", lambda q: q.reciprocal(
                wsum[0:1, 9:10], wsum[0:1, 8:9]), waits=[t2e], sig=sV)
            t4e = P.add("vector", lambda q: q.tensor_scalar(
                out=wsum[0:1, 10:11], in0=wsum[0:1, 9:10],
                scalar1=float(4096 * 1024), scalar2=None, op0=ALU.mult),
                waits=[t3e], sig=sV)
            t4b = P.add("vector", lambda q: q.tensor_scalar(
                out=wsum[0:1, 11:12], in0=wsum[0:1, 8:9],
                scalar1=1.0 / float(4096 * 1024), scalar2=None, op0=ALU.mult),
                waits=[t2e], sig=sV)
            t5e = P.add("gpsimd", lambda q: q.dma_start(
                out=msc_scr.ap()[0:1, 0:2], in_=wsum[0:1, 10:12]),
                waits=[t4e, t4b], sig=sL, n=16)
            t6e = P.add("sync", lambda q: q.dma_start(
                out=wsum[:, 12:13],
                in_=msc_scr.ap()[0:1, 0:1].to_broadcast((128, 1))),
                waits=[t5e], sig=sL, n=16)
            t6b = P.add("sync", lambda q, wi_=wi_: q.dma_start(
                out=mrep[:, wi_:wi_ + 1],
                in_=msc_scr.ap()[0:1, 1:2].to_broadcast((128, 1))),
                waits=[t5e], sig=sL, n=16)
            mrep_evs.append(t6b)
            prev2 = {}
            stv = None
            qe_prev = None
            for ch in range(nch):
                for sb_ in range(nsub):
                    k = ch * nsub + sb_
                    def fl(q, src=src, ch=ch, sb_=sb_, k=k):
                        return q.dma_start(
                            out=wstage[k % 2][:],
                            in_=src[128 * ch:128 * ch + 128, 1024 * sb_:1024 * sb_ + 1024])
                    w = [t6e] + ([prev2[k % 2]] if k % 2 in prev2 else [])
                    lde = P.add("scalar", fl, waits=w, sig=sL, n=16)
                    ce = None
                    for hf in range(2):
                        w0 = [lde] + ([qe_prev] if qe_prev else []) \
                            + ([stv] if (stv and hf == 0) else []) \
                            + ([xpost_st[NT - 1]] if k == 0 and hf == 0 else [])
                        ce = uround(wstage[k % 2][:, 512 * hf:512 * hf + 512],
                                    wsum[:, 12:13],
                                    wtern[:, 512 * hf:512 * hf + 512],
                                    w0, clamp=True)
                    qe_prev = ce
                    prev2[k % 2] = ce
                    stv = P.add("scalar", lambda q, dst=dst, ch=ch, sb_=sb_: q.dma_start(
                        out=dst[128 * ch:128 * ch + 128, 1024 * sb_:1024 * sb_ + 1024],
                        in_=wtern[:]), waits=[ce], sig=sL, n=16)
            tern_done.append(stv)

        if stage <= 26:
            if _cut():
                return nc
        barA = P.barrier(sW, waits=[xpost_st[NT - 1], tern_done[-1]])

        wt_evs = []
        for c in range(8):
            wt_evs.append(P.add("sync", lambda q, c=c: q.dma_start_transpose(
                W1t[c][:], t1_scr[:, 128 * c:128 * c + 128]), waits=barA, sig=sL, n=16))
        for c in range(32):
            wt_evs.append(P.add("sync", lambda q, c=c: q.dma_start_transpose(
                W2t[c][:], t2_scr[:, 128 * c:128 * c + 128]), waits=barA, sig=sL, n=16))

        bc_evs = []
        for (hsrc, dstt, width) in [
                (l1w_h, g1bc, 1024), (l1b_h, b1bc_s, 1024), (b2_h, bb2, 1024),
                (l2w_h, g2bc, 4096), (l2b_h, b2bc_s, 4096), (b1_h, bb1, 4096)]:
            bc_evs.append(P.add("gpsimd", lambda q, hsrc=hsrc, dstt=dstt, width=width:
                           q.dma_start(out=dstt[:],
                                       in_=hsrc[0:1, :].to_broadcast((128, width))),
                           waits=barA, sig=sL, n=16))
        bc_all = bc_evs[-1]

        def heron(col_t, col_s, iters, waits, seed_half):
            if seed_half:
                ev = P.add("vector", lambda q, a=col_t, b=col_s: q.tensor_scalar(
                    out=bstats[:, b:b + 1], in0=bstats[:, a:a + 1],
                    scalar1=0.5, scalar2=0.5, op0=ALU.mult, op1=ALU.add),
                    waits=waits, sig=sV)
            else:
                ev = P.add("vector", lambda q, a=col_t, b=col_s: q.tensor_scalar(
                    out=bstats[:, b:b + 1], in0=bstats[:, a:a + 1],
                    scalar1=0.0, scalar2=1.0, op0=ALU.mult, op1=ALU.add),
                    waits=waits, sig=sV)
            for _ in range(iters):
                ev = P.add("vector", lambda q, b=col_s: q.reciprocal(
                    bstats[:, b + 1:b + 2], bstats[:, b:b + 1]), waits=[ev], sig=sV)
                ev = P.add("vector", lambda q, a=col_t, b=col_s: q.tensor_tensor(
                    out=bstats[:, b + 1:b + 2], in0=bstats[:, a:a + 1],
                    in1=bstats[:, b + 1:b + 2], op=ALU.mult), waits=[ev], sig=sV)
                ev = P.add("vector", lambda q, b=col_s: q.tensor_tensor(
                    out=bstats[:, b:b + 1], in0=bstats[:, b:b + 1],
                    in1=bstats[:, b + 1:b + 2], op=ALU.add), waits=[ev], sig=sV)
                ev = P.add("vector", lambda q, b=col_s: q.tensor_scalar(
                    out=bstats[:, b:b + 1], in0=bstats[:, b:b + 1],
                    scalar1=0.5, scalar2=None, op0=ALU.mult), waits=[ev], sig=sV)
            ev = P.add("vector", lambda q, b=col_s: q.reciprocal(
                bstats[:, b:b + 1], bstats[:, b:b + 1]), waits=[ev], sig=sV)
            return ev

        def rsqrt_nr(col_t, col_s, waits):
            ct, cs = col_t, col_s
            ev = P.add("vector", lambda q: q.tensor_scalar(
                out=scri[:, 0:1],
                in0=bstats[:, ct:ct + 1].bitcast(I32),
                scalar1=1, scalar2=None, op0=ALU.logical_shift_right),
                waits=waits, sig=sV)
            ev = P.add("vector", lambda q: q.tensor_scalar(
                out=bstats[:, cs:cs + 1], in0=scri[:, 0:1].bitcast(F32),
                scalar1=1.2665333395337613e19, scalar2=None, op0=ALU.mult),
                waits=[ev], sig=sV)
            ev = P.add("vector", lambda q: q.reciprocal(
                bstats[:, cs + 1:cs + 2], bstats[:, cs:cs + 1]),
                waits=[ev], sig=sV)
            for _ in range(4):
                ev = P.add("vector", lambda q: q.tensor_tensor(
                    out=bstats[:, cs:cs + 1], in0=bstats[:, cs + 1:cs + 2],
                    in1=bstats[:, cs + 1:cs + 2], op=ALU.mult), waits=[ev], sig=sV)
                ev = P.add("vector", lambda q: q.tensor_tensor(
                    out=bstats[:, cs:cs + 1], in0=bstats[:, cs:cs + 1],
                    in1=bstats[:, ct:ct + 1], op=ALU.mult), waits=[ev], sig=sV)
                ev = P.add("vector", lambda q: q.tensor_scalar(
                    out=bstats[:, cs:cs + 1], in0=bstats[:, cs:cs + 1],
                    scalar1=-0.5, scalar2=1.5, op0=ALU.mult, op1=ALU.add),
                    waits=[ev], sig=sV)
                ev = P.add("vector", lambda q: q.tensor_tensor(
                    out=bstats[:, cs + 1:cs + 2], in0=bstats[:, cs + 1:cs + 2],
                    in1=bstats[:, cs:cs + 1], op=ALU.mult), waits=[ev], sig=sV)
            ev = P.add("vector", lambda q: q.tensor_scalar(
                out=bstats[:, cs:cs + 1], in0=bstats[:, cs + 1:cs + 2],
                scalar1=1.0, scalar2=None, op0=ALU.mult), waits=[ev], sig=sV)
            return ev

        xp_free = [None, None]
        q1_free = [None, None]
        h1_free = None
        q2_free = None
        scr_free = None
        ptr_free = None
        pb_po_free = list(po_free)
        pb_ps_free = list(ps_free)
        out_st = [None] * NT

        st_fap = [None] * NT
        st_s2 = [None] * NT
        st_mmh1 = [None] * NT
        st_q1c = [None] * NT
        st_gel = [None] * NT
        st_mmh0 = [None] * NT
        st_p1 = [None] * NT
        st_q2e = [None] * NT
        st_q2c = [None] * NT
        st_mm2 = [None] * NT
        st_f1e = [None] * NT
        st_f2e = [None] * NT
        st_fr2 = [None] * NT
        st_qb16 = [None] * NT
        out_st = [None] * NT

        def S1(t):
            pr = t % 2
            lxp = P.add("sync", lambda q, t=t, pr=pr: q.dma_start(
                out=xp[pr][:], in_=xpost_scr[128 * t:128 * t + 128, :]),
                waits=[xpost_st[t]] + barA +
                      ([st_fap[t - 2]] if t >= 2 else []), sig=sL, n=16)
            b1e = P.add("vector", lambda q, pr=pr: q.reduce_sum(
                bstats[:, 0:1], xp[pr][:], AX.X), waits=[lxp] + barA, sig=sV)
            b2e = P.add("scalar", lambda q, pr=pr: q.activation(
                scr32[:], xp[pr][:], AF.Square, accum_out=bstats[:, 1:2]),
                waits=[lxp], sig=sA)
            b3e = P.add("vector", lambda q: q.tensor_scalar(
                out=bstats[:, 2:3], in0=bstats[:, 0:1], scalar1=consts[:, 4:5],
                scalar2=None, op0=ALU.mult), waits=[b1e], sig=sV)
            b4e = P.add("vector", lambda q: q.tensor_tensor(
                out=bstats[:, 3:4], in0=bstats[:, 2:3], in1=bstats[:, 2:3],
                op=ALU.mult), waits=[b3e], sig=sV)
            b5e = P.add("vector", lambda q: q.tensor_scalar(
                out=bstats[:, 4:5], in0=bstats[:, 1:2], scalar1=consts[:, 4:5],
                scalar2=None, op0=ALU.mult), waits=[b2e], sig=sV)
            b6e = P.add("vector", lambda q: q.tensor_tensor(
                out=bstats[:, 4:5], in0=bstats[:, 4:5], in1=bstats[:, 3:4],
                op=ALU.subtract), waits=[b4e, b5e], sig=sV)
            b7e = P.add("vector", lambda q: q.tensor_scalar(
                out=bstats[:, 5:6], in0=bstats[:, 4:5], scalar1=consts[:, 0:1],
                scalar2=None, op0=ALU.add), waits=[b6e], sig=sV)
            rstd1 = heron(5, 6, 2, [b7e], seed_half=False)
            a1 = P.add("vector", lambda q, pr=pr: q.tensor_scalar(
                out=hln[:], in0=xp[pr][:], scalar1=bstats[:, 2:3],
                scalar2=bstats[:, 6:7], op0=ALU.subtract, op1=ALU.mult),
                waits=[rstd1, b2e] +
                      ([st_gel[t - 1][-1]] if t >= 1 else []), sig=sV, basic=True)
            a2 = P.add("vector", lambda q: q.tensor_tensor(
                out=hln[:], in0=hln[:], in1=g1bc[:], op=ALU.mult),
                waits=[a1, bc_all], sig=sV, basic=True)
            a3 = P.add("vector", lambda q: q.tensor_tensor(
                out=hln[:], in0=hln[:], in1=b1bc_s[:], op=ALU.add),
                waits=[a2], sig=sV, basic=True)
            qm = P.add("vector", lambda q: q.tensor_reduce(
                bstats[:, 8:9], hln[:], AX.X, ALU.max, apply_absolute_value=True),
                waits=[a3], sig=sV)
            qs1 = P.add("vector", lambda q: q.tensor_scalar(
                out=bstats[:, 9:10], in0=bstats[:, 8:9], scalar1=consts[:, 0:1],
                scalar2=None, op0=ALU.max), waits=[qm], sig=sV)
            qs2 = P.add("vector", lambda q: q.reciprocal(
                bstats[:, 10:11], bstats[:, 9:10]), waits=[qs1], sig=sV)
            qs3 = P.add("vector", lambda q: q.tensor_scalar(
                out=bstats[:, 11:12], in0=bstats[:, 10:11], scalar1=consts[:, 6:7],
                scalar2=None, op0=ALU.mult), waits=[qs2], sig=sV)
            f1a = P.add("vector", lambda q, pr=pr: q.tensor_scalar(
                out=fcol[:, pr:pr + 1], in0=bstats[:, 9:10], scalar1=1.0 / 127.0,
                scalar2=None, op0=ALU.mult), waits=[qs1], sig=sV)
            f1e = P.add("vector", lambda q, pr=pr: q.tensor_tensor(
                out=fcol[:, pr:pr + 1], in0=fcol[:, pr:pr + 1], in1=mrep[:, 0:1],
                op=ALU.mult), waits=[f1a, mrep_evs[0]], sig=sV)
            st_f1e[t] = f1e
            qprev = None
            for cc in range(2):
                w0 = [qs3] + ([qprev] if qprev else [])
                if cc == 0 and t >= 2 and out_st[t - 2] is not None:
                    w0.append(out_st[t - 2])
                qprev = uround_bs(hln[:, 512 * cc:512 * cc + 512], bstats[:, 11:12],
                                  q1[pr][:, 512 * cc:512 * cc + 512], w0)
            st_qb16[t] = qprev
            tr_evs = []
            for g4 in range(2):
                tes = []
                for cc in range(4):
                    c = 4 * g4 + cc
                    te = P.add("tensor", lambda q, c=c, cc=cc, pr=pr: q.transpose(
                        ptr[:, 128 * cc:128 * cc + 128],
                        q1[pr][:, 128 * c:128 * c + 128], identb[:]),
                        waits=[qprev, ldib] + (tr_evs[-1:] if g4 > 0 else
                              ([st_q1c[t - 1]] if t >= 1 else [])),
                        sig=sT)
                    tes.append(te)
                tr_evs.append(P.add("scalar", lambda q, g4=g4, pr=pr: q.activation(
                    q1[pr][:, 512 * g4:512 * g4 + 512], ptr[:, 0:512], AF.Copy),
                    waits=tes, sig=sA))
            st_q1c[t] = tr_evs[-1]

        def MM1H0(t):
            pr = t % 2
            mm_last = {}
            for k in range(8):
                for hc2 in range(4):
                    def f(q, k=k, hc2=hc2, pr=pr):
                        return q.matmul(
                            po[hc2][:], q1[pr][:, 128 * k:128 * k + 128],
                            W1t[k][:, 512 * hc2:512 * hc2 + 512],
                            start=(k == 0), stop=(k == 7))
                    w = []
                    if k == 0:
                        w = [st_q1c[t]]
                        if t == 0:
                            w.append(wt_evs[7])
                        if pb_po_free[hc2] is not None:
                            w.append(pb_po_free[hc2])
                    ev = P.add("tensor", f, waits=w, sig=sT if k == 7 else None)
                    if k == 7:
                        mm_last[hc2] = ev
            st_mmh0[t] = mm_last

        def S2(t):
            pr = t % 2
            gels = []

            def evac_gelu(hc2, hc, mm_ev):
                sl = 512 * (hc % 2)
                w = [mm_ev, st_f1e[t], bc_all]
                if hc == 0:
                    w.append(st_qb16[t])
                    if t >= 1:
                        w.append(st_gel[t - 1][6])
                elif hc == 1 and t >= 1:
                    w.append(st_gel[t - 1][7])
                if len(gels) >= 2:
                    w.append(gels[-2])
                e1 = P.add("vector", lambda q, hc2=hc2, hc=hc, sl=sl, pr=pr:
                           q.scalar_tensor_tensor(
                               out=scr32[:, sl:sl + 512], in0=po[hc2][:],
                               scalar=fcol[:, pr:pr + 1],
                               in1=bb1[:, 512 * hc:512 * hc + 512],
                               op0=ALU.mult, op1=ALU.add), waits=w, sig=sV)
                pb_po_free[hc2] = e1
                gw_ = [e1]
                if hc == 0 and t >= 1 and st_p1[t - 1] is not None:
                    gw_.append(st_p1[t - 1])
                ge = P.add("scalar", lambda q, hc=hc, sl=sl: q.activation(
                    h1[:, 512 * hc:512 * hc + 512], scr32[:, sl:sl + 512],
                    AF.Gelu, accum_out=bstats[:, 16 + hc:17 + hc]),
                    waits=gw_, sig=sA)
                gels.append(ge)

            for hc2 in range(4):
                evac_gelu(hc2, hc2, st_mmh0[t][hc2])
            st_s2[t] = (gels, evac_gelu)

        def MM1H1(t):
            pr = t % 2
            mm_last = {}
            for k in range(8):
                for hc2 in range(4):
                    hc = 4 + hc2
                    def f(q, k=k, hc2=hc2, hc=hc, pr=pr):
                        return q.matmul(
                            po[hc2][:], q1[pr][:, 128 * k:128 * k + 128],
                            W1t[k][:, 512 * hc:512 * hc + 512],
                            start=(k == 0), stop=(k == 7))
                    w = []
                    if k == 0:
                        w = [pb_po_free[hc2]]
                    ev = P.add("tensor", f, waits=w, sig=sT if k == 7 else None)
                    if k == 7:
                        mm_last[hc2] = ev
            st_mmh1[t] = mm_last

        def S2b(t):
            gels, evac_gelu = st_s2[t]
            for hc2 in range(4):
                evac_gelu(hc2, 4 + hc2, st_mmh1[t][hc2])
            st_gel[t] = gels

        def S3(t):
            pr = t % 2
            l2s1 = P.add("vector", lambda q: q.reduce_sum(
                bstats[:, 24:25], bstats[:, 16:24], AX.X), waits=st_gel[t], sig=sV)
            l2s2 = P.add("scalar", lambda q: q.activation(
                sqb[:], h1[:], AF.Square, accum_out=bstats[:, 25:26]),
                waits=st_gel[t] + ([st_q2e[t - 1]] if t >= 1 else []), sig=sA)
            l2s3 = P.add("vector", lambda q: q.tensor_scalar(
                out=bstats[:, 26:27], in0=bstats[:, 24:25], scalar1=consts[:, 5:6],
                scalar2=None, op0=ALU.mult), waits=[l2s1], sig=sV)
            l2s4 = P.add("vector", lambda q: q.tensor_tensor(
                out=bstats[:, 27:28], in0=bstats[:, 26:27], in1=bstats[:, 26:27],
                op=ALU.mult), waits=[l2s3], sig=sV)
            l2s5 = P.add("vector", lambda q: q.tensor_scalar(
                out=bstats[:, 28:29], in0=bstats[:, 25:26], scalar1=consts[:, 5:6],
                scalar2=None, op0=ALU.mult), waits=[l2s2], sig=sV)
            l2s6 = P.add("vector", lambda q: q.tensor_tensor(
                out=bstats[:, 28:29], in0=bstats[:, 28:29], in1=bstats[:, 27:28],
                op=ALU.subtract), waits=[l2s4, l2s5], sig=sV)
            l2s7 = P.add("vector", lambda q: q.tensor_scalar(
                out=bstats[:, 29:30], in0=bstats[:, 28:29], scalar1=consts[:, 0:1],
                scalar2=1e-6, op0=ALU.add, op1=ALU.max), waits=[l2s6], sig=sV)
            rstd2 = rsqrt_nr(29, 30, [l2s7])
            p1 = P.add("vector", lambda q: q.tensor_scalar(
                out=h2b[:], in0=h1[:], scalar1=bstats[:, 26:27],
                scalar2=bstats[:, 30:31], op0=ALU.subtract, op1=ALU.mult),
                waits=[rstd2, l2s2] +
                      ([st_fr2[t - 1]] if t >= 1 and st_fr2[t - 1] else []), sig=sV, basic=True)
            st_p1[t] = p1
            p2 = P.add("vector", lambda q: q.tensor_tensor(
                out=h2b[:], in0=h2b[:], in1=g2bc[:], op=ALU.mult),
                waits=[p1, bc_all], sig=sV, basic=True)
            p3 = P.add("vector", lambda q: q.tensor_tensor(
                out=h2b[:], in0=h2b[:], in1=b2bc_s[:], op=ALU.add),
                waits=[p2], sig=sV, basic=True)
            q2m = P.add("vector", lambda q: q.tensor_reduce(
                bstats[:, 32:33], h2b[:], AX.X, ALU.max, apply_absolute_value=True),
                waits=[p3], sig=sV)
            q2s1 = P.add("vector", lambda q: q.tensor_scalar(
                out=bstats[:, 33:34], in0=bstats[:, 32:33], scalar1=consts[:, 0:1],
                scalar2=None, op0=ALU.max), waits=[q2m], sig=sV)
            q2s2 = P.add("vector", lambda q: q.reciprocal(
                bstats[:, 34:35], bstats[:, 33:34]), waits=[q2s1], sig=sV)
            q2s3 = P.add("vector", lambda q: q.tensor_scalar(
                out=bstats[:, 35:36], in0=bstats[:, 34:35], scalar1=consts[:, 6:7],
                scalar2=None, op0=ALU.mult), waits=[q2s2], sig=sV)
            f2a = P.add("vector", lambda q, pr=pr: q.tensor_scalar(
                out=fcol[:, 2 + pr:3 + pr], in0=bstats[:, 33:34], scalar1=1.0 / 127.0,
                scalar2=None, op0=ALU.mult), waits=[q2s1], sig=sV)
            f2e = P.add("vector", lambda q, pr=pr: q.tensor_tensor(
                out=fcol[:, 2 + pr:3 + pr], in0=fcol[:, 2 + pr:3 + pr],
                in1=mrep[:, 1:2], op=ALU.mult), waits=[f2a, mrep_evs[1]], sig=sV)
            st_f2e[t] = f2e
            qprev = None
            for cc in range(8):
                w0 = [q2s3] + ([qprev] if qprev else [])
                if cc == 0 and t >= 1:
                    w0.append(st_mm2[t - 1][1])
                qprev = uround_bs(h2b[:, 512 * cc:512 * cc + 512], bstats[:, 35:36],
                                  q2[:, 512 * cc:512 * cc + 512], w0)
            st_q2e[t] = qprev
            q2f_evs = []
            for g4 in range(8):
                tes = []
                for cc in range(4):
                    c = 4 * g4 + cc
                    te = P.add("tensor", lambda q, c=c, cc=cc: q.transpose(
                        ptr[:, 512 + 128 * cc:512 + 128 * cc + 128],
                        q2[:, 128 * c:128 * c + 128], identb[:]),
                        waits=[qprev, ldib] + (q2f_evs[-1:] if g4 > 0 else
                              ([st_q2c[t - 1][-1]] if t >= 1 else [])),
                        sig=sT)
                    tes.append(te)
                q2f_evs.append(P.add("scalar", lambda q, g4=g4: q.activation(
                    q2[:, 512 * g4:512 * g4 + 512], ptr[:, 512:1024], AF.Copy),
                    waits=tes, sig=sA))
            st_q2c[t] = q2f_evs

        def S4a(t):
            mm_last = {}
            for c in range(32):
                for nn in range(2):
                    def f(q, c=c, nn=nn):
                        return q.matmul(
                            ps[nn][:], q2[:, 128 * c:128 * c + 128],
                            W2t[c][:, 512 * nn:512 * nn + 512],
                            start=(c == 0), stop=(c == 31))
                    w = [st_q2c[t][c // 4]]
                    if c == 0:
                        if t == 0:
                            w.append(wt_evs[-1])
                        if pb_ps_free[nn] is not None:
                            w.append(pb_ps_free[nn])
                    ev = P.add("tensor", f, waits=w, sig=sT if c == 31 else None)
                    if c == 31:
                        mm_last[nn] = ev
            st_mm2[t] = mm_last

        def S4b(t):
            pr = t % 2
            ev1 = {}
            for nn in range(2):
                ev1[nn] = P.add("vector", lambda q, nn=nn, pr=pr:
                                q.scalar_tensor_tensor(
                                    out=xp[pr][:, 512 * nn:512 * nn + 512],
                                    in0=ps[nn][:], scalar=fcol[:, 2 + pr:3 + pr],
                                    in1=xp[pr][:, 512 * nn:512 * nn + 512],
                                    op0=ALU.mult, op1=ALU.add),
                                waits=[st_mm2[t][nn], st_f2e[t]], sig=sV)
                pb_ps_free[nn] = ev1[nn]
            frb = P.add("vector", lambda q, pr=pr: q.scalar_tensor_tensor(
                out=xp[pr][:], in0=xp[pr][:], scalar=1.0, in1=bb2[:],
                op0=ALU.mult, op1=ALU.add, accum_out=bstats[:, 40:41]),
                waits=[ev1[0], ev1[1], bc_all], sig=sV)
            fr2 = P.add("scalar", lambda q, pr=pr: q.activation(
                sqb[:, 0:1024], xp[pr][:], AF.Square, accum_out=bstats[:, 41:42]),
                waits=[frb], sig=sA)
            st_fr2[t] = fr2
            fr3 = P.add("vector", lambda q: q.tensor_scalar(
                out=bstats[:, 42:43], in0=bstats[:, 40:41], scalar1=consts[:, 4:5],
                scalar2=None, op0=ALU.mult), waits=[frb], sig=sV)
            fr4 = P.add("vector", lambda q: q.tensor_tensor(
                out=bstats[:, 43:44], in0=bstats[:, 42:43], in1=bstats[:, 42:43],
                op=ALU.mult), waits=[fr3], sig=sV)
            fr5 = P.add("vector", lambda q: q.tensor_scalar(
                out=bstats[:, 44:45], in0=bstats[:, 41:42], scalar1=consts[:, 4:5],
                scalar2=None, op0=ALU.mult), waits=[fr2], sig=sV)
            fr6 = P.add("vector", lambda q: q.tensor_tensor(
                out=bstats[:, 44:45], in0=bstats[:, 44:45], in1=bstats[:, 43:44],
                op=ALU.subtract), waits=[fr4, fr5], sig=sV)
            fr7 = P.add("vector", lambda q: q.tensor_scalar(
                out=bstats[:, 45:46], in0=bstats[:, 44:45], scalar1=consts[:, 0:1],
                scalar2=1e-6, op0=ALU.add, op1=ALU.max), waits=[fr6], sig=sV)
            rstdf = rsqrt_nr(45, 46, [fr7])
            fap = P.add("vector", lambda q, pr=pr: q.tensor_scalar(
                out=q1[pr][:], in0=xp[pr][:], scalar1=bstats[:, 42:43],
                scalar2=bstats[:, 46:47], op0=ALU.subtract, op1=ALU.mult),
                waits=[rstdf, fr2], sig=sV)
            st_fap[t] = fap
            out_st[t] = P.add("gpsimd", lambda q, t=t, pr=pr: q.dma_start(
                out=out_h[128 * t:128 * t + 128, :], in_=q1[pr][:]),
                waits=[fap], sig=sO, n=16)

        S1(0)
        MM1H0(0)
        S2(0)
        MM1H1(0)
        for t in range(NT):
            S2b(t)
            if t >= 1:
                S4b(t - 1)
            if t + 1 < NT:
                S1(t + 1)
                MM1H0(t + 1)
            S3(t)
            S4a(t)
            if t + 1 < NT:
                S2(t + 1)
                MM1H1(t + 1)
        S4b(NT - 1)

        fin = P.cnt[id(sO)]
        P.add("sync", lambda q: q.wait_ge(sO, fin))
        for s_ in (sLS, sLA, sLG):
            v = P.cnt.get(id(s_), 0)
            if v:
                P.add("sync", lambda q, s_=s_, v=v: q.wait_ge(s_, v))
        P.emit(block)
    return nc


_NC_CACHE = {}


def kernel(L0_attn_x, L0_attn_wq, L0_attn_wk, L0_attn_wv, L0_attn_wi, L0_attn_bi,
           L0_attn_wf, L0_attn_bf, L0_attn_ln_w, L0_ffn_ln1_w, L0_ffn_ln1_b,
           L0_ffn_w1, L0_ffn_b1, L0_ffn_ln2_w, L0_ffn_ln2_b, L0_ffn_w2, L0_ffn_b2):
    if "nc" not in _NC_CACHE:
        _NC_CACHE["nc"] = build_program()
    nc = _NC_CACHE["nc"]
    x = np.ascontiguousarray(np.asarray(L0_attn_x, dtype=np.float32))
    com = {
        "wq": np.ascontiguousarray(np.asarray(L0_attn_wq, np.float32).reshape(256, 16)),
        "wk": np.ascontiguousarray(np.asarray(L0_attn_wk, np.float32).reshape(256, 16)),
        "wv": np.ascontiguousarray(np.asarray(L0_attn_wv, np.float32).reshape(256, 16)),
        "wi": np.ascontiguousarray(np.asarray(L0_attn_wi, np.float32)),
        "bi": np.ascontiguousarray(np.asarray(L0_attn_bi, np.float32).reshape(4, 1)),
        "wf": np.ascontiguousarray(np.asarray(L0_attn_wf, np.float32)),
        "bf": np.ascontiguousarray(np.asarray(L0_attn_bf, np.float32).reshape(4, 1)),
        "lnw": np.ascontiguousarray(np.asarray(L0_attn_ln_w, np.float32).reshape(1, D)),
        "l1w": np.ascontiguousarray(np.asarray(L0_ffn_ln1_w, np.float32).reshape(1, D)),
        "l1b": np.ascontiguousarray(np.asarray(L0_ffn_ln1_b, np.float32).reshape(1, D)),
        "w1": np.ascontiguousarray(np.asarray(L0_ffn_w1, np.float32)),
        "b1": np.ascontiguousarray(np.asarray(L0_ffn_b1, np.float32).reshape(1, 4096)),
        "l2w": np.ascontiguousarray(np.asarray(L0_ffn_ln2_w, np.float32).reshape(1, 4096)),
        "l2b": np.ascontiguousarray(np.asarray(L0_ffn_ln2_b, np.float32).reshape(1, 4096)),
        "w2": np.ascontiguousarray(np.asarray(L0_ffn_w2, np.float32)),
        "b2": np.ascontiguousarray(np.asarray(L0_ffn_b2, np.float32).reshape(1, D)),
    }
    in_maps = [dict(com, x=np.ascontiguousarray(x[b])) for b in range(8)]
    res = run_bass_kernel_spmd(nc, in_maps, core_ids=list(range(8)))
    out = np.stack([res.results[b]["out"] for b in range(8)], axis=0)
    return out


# version 17
# speedup vs baseline: 1.0501x; 1.0501x over previous
import numpy as np
import ml_dtypes
import concourse.bass as bass
import concourse.mybir as mybir
from concourse.bass_utils import run_bass_kernel_spmd

F32 = mybir.dt.float32
BF16 = mybir.dt.bfloat16
I32 = mybir.dt.int32
AF = mybir.ActivationFunctionType
ALU = mybir.AluOpType
AX = mybir.AxisListType

SEQ, D, NH = 4096, 1024, 4
QB = 512
NQB = SEQ // QB
NT = SEQ // 128
NGRP = SEQ // 1024

ENGINES = ("sync", "tensor", "vector", "scalar", "gpsimd")


class Prog:
    def __init__(self):
        self.steps = {e: [] for e in ENGINES}
        self.cnt = {}
        self.basic_ev = set()

    def add(self, eng, fn, waits=(), sig=None, n=1, basic=False):
        ev = None
        if sig is not None:
            c = self.cnt.get(id(sig), 0) + n
            self.cnt[id(sig)] = c
            ev = (sig, c)
            if basic:
                self.basic_ev.add((id(sig), c))
        self.steps[eng].append((fn, [w for w in waits if w is not None],
                                (sig, n) if sig is not None else None))
        return ev

    def barrier(self, sig, waits=()):
        return [self.add(e, lambda q: q.nop(), waits=waits, sig=sig)
                for e in ENGINES]

    def emit(self, block):
        def mk(eng_name):
            def body(q):
                maxw = {}
                own = {}
                el = getattr(self, "elide", {})
                for fn, waits, sig in self.steps[eng_name]:
                    for (sem, val) in waits:
                        if (el.get(id(sem)) == eng_name
                                and own.get(id(sem), 0) >= val
                                and (eng_name != "vector"
                                     or (id(sem), val) in self.basic_ev)):
                            continue
                        if maxw.get(id(sem), -1) >= val:
                            continue
                        q.wait_ge(sem, val)
                        maxw[id(sem)] = val
                    instr = fn(q)
                    if sig is not None:
                        if instr is None:
                            instr = q.nop()
                        instr.then_inc(sig[0], sig[1])
                        own[id(sig[0])] = own.get(id(sig[0]), 0) + sig[1]
            return body
        for e in ENGINES:
            getattr(block, e)(mk(e))


def build_program(stage=99, dbg=False, dwait=3, dmask=7):
    nc = bass.Bass("TRN2", target_bir_lowering=False, detect_race_conditions=False)

    x_h = nc.dram_tensor("x", [SEQ, D], F32, kind="ExternalInput").ap()
    wq_h = nc.dram_tensor("wq", [256, 16], F32, kind="ExternalInput").ap()
    wk_h = nc.dram_tensor("wk", [256, 16], F32, kind="ExternalInput").ap()
    wv_h = nc.dram_tensor("wv", [256, 16], F32, kind="ExternalInput").ap()
    wi_h = nc.dram_tensor("wi", [4, 3072], F32, kind="ExternalInput").ap()
    bi_h = nc.dram_tensor("bi", [4, 1], F32, kind="ExternalInput").ap()
    wf_h = nc.dram_tensor("wf", [4, 3072], F32, kind="ExternalInput").ap()
    bf_h = nc.dram_tensor("bf", [4, 1], F32, kind="ExternalInput").ap()
    lnw_h = nc.dram_tensor("lnw", [1, D], F32, kind="ExternalInput").ap()
    l1w_h = nc.dram_tensor("l1w", [1, D], F32, kind="ExternalInput").ap()
    l1b_h = nc.dram_tensor("l1b", [1, D], F32, kind="ExternalInput").ap()
    w1_h = nc.dram_tensor("w1", [4096, 1024], F32, kind="ExternalInput").ap()
    b1_h = nc.dram_tensor("b1", [1, 4096], F32, kind="ExternalInput").ap()
    l2w_h = nc.dram_tensor("l2w", [1, 4096], F32, kind="ExternalInput").ap()
    l2b_h = nc.dram_tensor("l2b", [1, 4096], F32, kind="ExternalInput").ap()
    w2_h = nc.dram_tensor("w2", [1024, 4096], F32, kind="ExternalInput").ap()
    b2_h = nc.dram_tensor("b2", [1, D], F32, kind="ExternalInput").ap()
    out_h = nc.dram_tensor("out", [SEQ, D], F32, kind="ExternalOutput").ap()

    KS = "ExternalOutput" if dbg else "Internal"
    xbf_scr = nc.dram_tensor("xbf_scr", [SEQ, D], BF16, kind=KS).ap()
    A_scr = nc.dram_tensor("A_scr", [NH, 256, 256], F32, kind=KS)
    Bv_scr = nc.dram_tensor("Bv_scr", [NH, 256, 256], F32, kind=KS)
    xpost_scr = nc.dram_tensor("xpost_scr", [SEQ, D], F32, kind=KS).ap()
    t1_scr = nc.dram_tensor("t1_scr", [4096, 1024], BF16, kind=KS).ap()
    t2_scr = nc.dram_tensor("t2_scr", [1024, 4096], BF16, kind=KS).ap()
    grow_scr = nc.dram_tensor("grow_scr", [1, 64], F32, kind=KS)
    msc_scr = nc.dram_tensor("msc_scr", [1, 4], F32, kind=KS)
    dbg_stt = nc.dram_tensor("dbg_stt", [4, 128, 64], F32, kind=KS)
    dbg_acc = nc.dram_tensor("dbg_acc", [4, 128, 257], F32, kind=KS)
    dbg_po = nc.dram_tensor("dbg_po", [4, 128, 257], F32, kind=KS)
    dbg_gcol = nc.dram_tensor("dbg_gcol", [128, 512], F32, kind=KS)
    dbg_rcol = nc.dram_tensor("dbg_rcol", [128, 64], F32, kind=KS)
    dbg_gw = nc.dram_tensor("dbg_gw", [4, 3 * SEQ], F32, kind=KS)
    dbg_hat = nc.dram_tensor("dbg_hat", [4, 128, 1024], F32, kind=KS)
    dbg_st2 = nc.dram_tensor("dbg_st2", [4, 4, 128, 64], F32, kind=KS)

    ident_bf_h = nc.inline_tensor(
        np.eye(128, dtype=np.float32).astype(ml_dtypes.bfloat16), name="identb_h").ap()
    ident_f_h = nc.inline_tensor(np.eye(128, dtype=np.float32), name="identf_h").ap()
    ones_row_h = nc.inline_tensor(np.ones((1, 128), np.float32), name="onesr_h").ap()
    ii = np.arange(128)[:, None]
    jj = np.arange(512)[None, :]
    masks_np = np.concatenate(
        [(jj - ii - 128 * r >= 0).astype(np.float32) for r in range(4)], axis=1)
    masks_h = nc.inline_tensor(masks_np, name="masks_h").ap()
    cv = np.zeros((128, 12), np.float32)
    cv[:, 0] = 1e-5
    cv[:, 1] = -0.5
    cv[:, 2] = 1e-8
    cv[:, 3] = 1.0 / 256
    cv[:, 4] = 1.0 / 1024
    cv[:, 5] = 1.0 / 4096
    cv[:, 6] = 127.0
    cv[:, 7] = 1.0
    cv[:, 8] = 1e-30
    cv[:, 9] = 0.5
    cv[:, 10] = -34.53877639491068
    cv[:, 11] = 1e-18
    consts_h = nc.inline_tensor(cv, name="consts_h").ap()

    ARENA_BYTES = 212096
    _arena = nc.alloc_sbuf_tensor("arena", [128, ARENA_BYTES], mybir.dt.uint8)
    BASE = nc.lookup_mloc(_arena).addr

    def T(name, shape, dt, offset):
        return nc.alloc_sbuf_tensor_at(name, shape, dt, offset=BASE + offset)
    consts = T("consts", [128, 12], F32, offset=0)
    dummy4 = T("dummy4", [128, 4], F32, offset=64)
    identb = T("identb", [128, 128], BF16, offset=96)
    identf = T("identf", [128, 128], F32, offset=352)
    onesr = T("onesr", [1, 128], F32, offset=864)
    bstats = T("bstats", [128, 64], F32, offset=1376)
    fcol = T("fcol", [128, 8], F32, offset=1632)
    mrep = T("mrep", [128, 2], F32, offset=1664)
    wq_sb = T("wq_sb", [128, 32], F32, offset=1696)
    wk_sb = T("wk_sb", [128, 32], F32, offset=1824)
    wv_sb = T("wv_sb", [128, 32], F32, offset=1952)
    a16 = T("a16", [128, 32], F32, offset=2080)
    bv16 = T("bv16", [128, 32], F32, offset=2208)
    wgf_all = T("wgf_all", [128, 64], BF16, offset=2336)
    biasg_i = T("biasg_i", [4, 1], F32, offset=2464)
    biasg_f = T("biasg_f", [4, 1], F32, offset=2496)
    A_sb = [[T(f"A_sb{h}_{c}", [128, 256], BF16, offset=2528 + (2 * h + c) * 512)
             for c in range(2)] for h in range(NH)]
    Bv_sb = [[T(f"Bv_sb{h}_{c}", [128, 256], BF16, offset=6624 + (2 * h + c) * 512)
              for c in range(2)] for h in range(NH)]
    gbc_mh = T("gbc_mh", [128, D], F32, offset=10720)
    masks = T("masks", [128, 2048], F32, offset=14816)
    gcol = T("gcol", [128, 512], F32, offset=23008)
    stt_sb = T("stt_sb", [128, 64], F32, offset=25120)
    g2bc = T("g2bc", [128, 4096], BF16, offset=1696)
    b2bc_s = T("b2bc_s", [128, 4096], BF16, offset=9888)
    g1bc = T("g1bc", [128, D], BF16, offset=18080)
    b1bc_s = T("b1bc_s", [128, D], BF16, offset=20128)
    bb2 = T("bb2", [128, D], BF16, offset=22176)
    XF = 25376
    x_fm = [T(f"x_fm{c}", [128, SEQ], BF16, offset=XF + c * 8192) for c in range(8)]
    vt = T("vt", [128, NT * 1028], BF16, offset=XF + 65536)
    W1t = [T(f"W1t{c}", [128, 4096], BF16, offset=XF + c * 8192) for c in range(8)]
    W2t = [T(f"W2t{c}", [128, 1024], BF16, offset=XF + 65536 + c * 2048)
           for c in range(32)]
    R = XF + 65536 + 65792
    gw = T("gw", [4, 3 * SEQ], F32, offset=R)
    wif = T("wif", [8, 3072], F32, offset=R)
    wgs = T("wgs", [8, 1024], F32, offset=R + 12288)
    lnw_sb = T("lnw_sb", [1, D], F32, offset=R + 16384)
    gsm = T("gsm", [4, 64], F32, offset=R + 49152)
    grow = T("grow", [1, 64], F32, offset=R + 49408)
    ones4k = T("ones4k", [4, 1024], F32, offset=R + 49536)
    rcol = T("rcol", [128, 32], F32, offset=R + 53632)
    grep = T("grep", [128, 32], F32, offset=R + 53760)
    zrow = T("zrow", [128, 512], F32, offset=R)
    qf = [[T(f"qf{h}_{c}", [128, QB], BF16, offset=R + (2 * h + c) * 1024)
           for c in range(2)] for h in range(NH)]
    s_sb = [T(f"s_sb{i}", [128, QB], BF16, offset=R + 8192 + i * 1024) for i in range(2)]
    acc = [T(f"acc{i}", [128, 257], F32, offset=R + 10240 + i * 1056) for i in range(4)]
    hatt = [T(f"hatt{i}", [128, D], F32, offset=R + 14464 + i * 4096) for i in range(4)]
    xin = [T(f"xin{i}", [128, D], F32, offset=R + 30848 + i * 4096) for i in range(2)]
    sqscr = T("sqscr", [128, D], F32, offset=R + 39040)
    wstage = [T(f"wstage{i}", [128, 1024], F32, offset=R + i * 4096) for i in range(2)]
    wtern = T("wtern", [128, 1024], BF16, offset=R + 8192)
    wint = T("wint", [128, 1024], I32, offset=R + 10240)
    wabs = T("wabs", [128, 1024], F32, offset=R + 10240)
    wsum = T("wsum", [128, 16], F32, offset=R + 43136)
    xp = [T(f"xp{i}", [128, D], F32, offset=R + i * 4096) for i in range(2)]
    q1 = [T(f"q1_{i}", [128, D], BF16, offset=R + 8192 + i * 2048) for i in range(2)]
    h1 = T("h1", [128, 4096], BF16, offset=R + 12288)
    q2 = T("q2", [128, 4096], BF16, offset=R + 20480)
    sqb = T("sqb", [128, 4096], BF16, offset=R + 28672)
    h2b = T("h2b", [128, 4096], BF16, offset=R + 28672)
    scr32 = T("scr32", [128, 1024], F32, offset=R + 36864)
    hln = T("hln", [128, D], F32, offset=R + 36864)
    ost = T("ost", [128, D], F32, offset=R + 36864)
    bb1 = T("bb1", [128, 4096], BF16, offset=R + 40960)
    scri = T("scri", [128, 512], I32, offset=R + 49152)
    yf_s = T("yf_s", [128, 512], F32, offset=R + 51200)
    adj_s = T("adj_s", [128, 512], F32, offset=R + 53248)
    cbf = T("cbf", [128, 512], BF16, offset=24224)
    vecs = T("vecs", [1, 4096], F32, offset=R + 12288)
    assert BASE + ARENA_BYTES <= 229344 - 500, BASE

    P = Prog()
    from contextlib import ExitStack
    with ExitStack() as _stk:
        sL = _stk.enter_context(nc.semaphore("sL"))
        sLS = _stk.enter_context(nc.semaphore("sLS"))
        sLA = _stk.enter_context(nc.semaphore("sLA"))
        sLG = _stk.enter_context(nc.semaphore("sLG"))
        sT = _stk.enter_context(nc.semaphore("sT"))
        sV = _stk.enter_context(nc.semaphore("sV"))
        sA = _stk.enter_context(nc.semaphore("sA"))
        sG = _stk.enter_context(nc.semaphore("sG"))
        sO = _stk.enter_context(nc.semaphore("sO"))
        sW = _stk.enter_context(nc.semaphore("sW"))
        import os as _os
        _emask = int(_os.environ.get("KELIDE", "15"))
        P.elide = {}
        for _bit, (_sem, _eng) in enumerate([(sV, "vector"), (sT, "tensor"),
                                             (sA, "scalar"), (sG, "gpsimd")]):
            if _emask & (1 << _bit):
                P.elide[id(_sem)] = _eng
        pq = _stk.enter_context(nc.psum_tensor("pq", [128, 512], F32))
        ps0 = _stk.enter_context(nc.psum_tensor("ps0", [128, 512], F32))
        ps1 = _stk.enter_context(nc.psum_tensor("ps1", [128, 512], F32))
        po0 = _stk.enter_context(nc.psum_tensor("po0", [128, 512], F32))
        po1 = _stk.enter_context(nc.psum_tensor("po1", [128, 512], F32))
        po2 = _stk.enter_context(nc.psum_tensor("po2", [128, 512], F32))
        po3 = _stk.enter_context(nc.psum_tensor("po3", [128, 512], F32))
        ptr = _stk.enter_context(nc.psum_tensor("ptr", [128, 1024], BF16))
        block = _stk.enter_context(nc.Block())
        ps = [ps0, ps1]
        po = [po0, po1, po2, po3]

        _dma_sem = {"sync": sLS, "scalar": sLA, "gpsimd": sLG}
        _orig_add = P.add

        def _add(eng, fn, waits=(), sig=None, n=1, basic=False):
            if sig is sL:
                sig = _dma_sem[eng]
            return _orig_add(eng, fn, waits, sig, n, basic)
        P.add = _add

        def _cut():
            for s_ in (sLS, sLA, sLG, sT, sV, sA, sG, sO, sW):
                v = P.cnt.get(id(s_), 0)
                if v:
                    P.add("sync", lambda q, s_=s_, v=v: q.wait_ge(s_, v))
            P.emit(block)
            return True

        def uround(src_ap, scale_ap, dst_ap, waits, clamp=False):
            w = src_ap.shape[1]
            e_1 = P.add("vector", lambda q: q.tensor_scalar(
                out=yf_s[:, 0:w], in0=src_ap, scalar1=scale_ap,
                scalar2=1023.5, op0=ALU.mult, op1=ALU.add), waits=waits, sig=sV, basic=True)
            e_2 = P.add("vector", lambda q: q.tensor_scalar(
                out=scri[:, 0:w], in0=yf_s[:, 0:w],
                scalar1=0.5, scalar2=None, op0=ALU.add), waits=[e_1], sig=sV)
            e_3 = P.add("vector", lambda q: q.tensor_tensor(
                out=adj_s[:, 0:w], in0=yf_s[:, 0:w],
                in1=scri[:, 0:w], op=ALU.is_ge), waits=[e_2], sig=sV, basic=True)
            e_4 = P.add("vector", lambda q: q.scalar_tensor_tensor(
                out=dst_ap, in0=scri[:, 0:w], scalar=-1024.0,
                in1=adj_s[:, 0:w], op0=ALU.add, op1=ALU.add),
                waits=[e_3], sig=sV, basic=True)
            if clamp:
                e_4 = P.add("vector", lambda q: q.tensor_scalar(
                    out=dst_ap, in0=dst_ap, scalar1=-1.0, scalar2=1.0,
                    op0=ALU.max, op1=ALU.min), waits=[e_4], sig=sV, basic=True)
            return e_4

        def uround_bs(src_ap, scale_ap, dst_ap, waits):
            w = src_ap.shape[1]
            e_1 = P.add("vector", lambda q: q.tensor_scalar(
                out=yf_s[:, 0:w], in0=src_ap, scalar1=scale_ap,
                scalar2=1024.5, op0=ALU.mult, op1=ALU.add), waits=waits, sig=sV, basic=True)
            e_2 = P.add("vector", lambda q: q.tensor_scalar(
                out=scri[:, 0:w], in0=yf_s[:, 0:w],
                scalar1=1.0, scalar2=None, op0=ALU.mult), waits=[e_1], sig=sV)
            e_3 = P.add("vector", lambda q: q.scalar_tensor_tensor(
                out=cbf[:, 0:w], in0=scri[:, 0:w], scalar=-1.0,
                in1=yf_s[:, 0:w], op0=ALU.mult, op1=ALU.add), waits=[e_2], sig=sV, basic=True)
            ev = P.add("vector", lambda q: q.tensor_scalar(
                out=dst_ap, in0=cbf[:, 0:w], scalar1=1.0,
                scalar2=16.0, op0=ALU.mult, op1=ALU.add), waits=[e_3], sig=sV, basic=True)
            for B in (16.0, 8.0, 4.0, 2.0, 1.0):
                ea = P.add("vector", lambda q, B=B: q.tensor_scalar(
                    out=cbf[:, 0:w], in0=dst_ap, scalar1=B, scalar2=B,
                    op0=ALU.is_ge, op1=ALU.mult), waits=[ev], sig=sV, basic=True)
                ev = P.add("vector", lambda q: q.tensor_tensor(
                    out=dst_ap, in0=dst_ap, in1=cbf[:, 0:w], op=ALU.subtract),
                    waits=[ea], sig=sV, basic=True)
            ev = P.add("vector", lambda q: q.scalar_tensor_tensor(
                out=dst_ap, in0=yf_s[:, 0:w], scalar=-1024.0,
                in1=dst_ap, op0=ALU.add, op1=ALU.subtract), waits=[ev], sig=sV, basic=True)
            return ev

        ldc = P.add("sync", lambda q: q.dma_start(out=consts[:], in_=consts_h), sig=sL, n=16)
        ldib = P.add("sync", lambda q: q.dma_start(out=identb[:], in_=ident_bf_h), sig=sL, n=16)
        ldif = P.add("sync", lambda q: q.dma_start(out=identf[:], in_=ident_f_h), sig=sL, n=16)
        ldor = P.add("sync", lambda q: q.dma_start(out=onesr[:], in_=ones_row_h), sig=sL, n=16)
        ldmk = P.add("sync", lambda q: q.dma_start(out=masks[:], in_=masks_h), sig=sL, n=16)
        for cc in range(2):
            ldwq = P.add("sync", lambda q, cc=cc: q.dma_start(
                out=wq_sb[:, 16 * cc:16 * cc + 16],
                in_=wq_h[128 * cc:128 * cc + 128, :]), sig=sL, n=16)
            ldwk = P.add("sync", lambda q, cc=cc: q.dma_start(
                out=wk_sb[:, 16 * cc:16 * cc + 16],
                in_=wk_h[128 * cc:128 * cc + 128, :]), sig=sL, n=16)
            ldwv = P.add("sync", lambda q, cc=cc: q.dma_start(
                out=wv_sb[:, 16 * cc:16 * cc + 16],
                in_=wv_h[128 * cc:128 * cc + 128, :]), sig=sL, n=16)
        ldwi = P.add("sync", lambda q: q.dma_start(out=wif[0:4, :], in_=wi_h), sig=sL, n=16)
        ldwf = P.add("sync", lambda q: q.dma_start(out=wif[4:8, :], in_=wf_h), sig=sL, n=16)
        ldbi = P.add("sync", lambda q: q.dma_start(out=biasg_i[:], in_=bi_h), sig=sL, n=16)
        ldbf = P.add("sync", lambda q: q.dma_start(out=biasg_f[:], in_=bf_h), sig=sL, n=16)
        ldln = None

        if stage <= 11:
            if _cut():
                return nc
        ldx = [None] * NT
        stxbf = [None] * NT
        for t in range(NT):
            ldx[t] = P.add("sync", lambda q, t=t: q.dma_start(
                out=xin[t % 2][:], in_=x_h[t * 128:(t + 1) * 128, :]),
                waits=[stxbf[t - 2]] if t >= 2 else [], sig=sL, n=16)
            stxbf[t] = P.add("gpsimd", lambda q, t=t: q.dma_start(
                out=xbf_scr[t * 128:(t + 1) * 128, :], in_=xin[t % 2][:]),
                waits=[ldx[t]], sig=sL, n=16)

        if stage <= 12:
            if _cut():
                return nc
        a16_evs = []
        for c in range(2):
            for i in range(4):
                for i2 in range(4):
                    def f(q, c=c, i=i, i2=i2):
                        return q.tensor_tensor(
                            out=dummy4[:], in0=wq_sb[:, 16 * c + i:16 * c + i + 13:4],
                            in1=wk_sb[:, 16 * c + i2:16 * c + i2 + 13:4], op=ALU.mult)
                    ev = P.add("vector", f, waits=[ldwq, ldwk], sig=sV)
                    def f2(q, c=c, i=i, i2=i2):
                        col = 16 * c + 4 * i + i2
                        return q.reduce_sum(a16[:, col:col + 1], dummy4[:], AX.X)
                    a16_evs.append(P.add("vector", f2, waits=[ev], sig=sV))
        a16_evs.append(P.add("vector", lambda q: q.tensor_scalar(
            out=a16[:], in0=a16[:], scalar1=1.0 / 16.0, scalar2=None, op0=ALU.mult),
            waits=[a16_evs[-1]], sig=sV))
        bv_evs = []
        for c in range(2):
            for i in range(4):
                for o in range(4):
                    def f(q, c=c, i=i, o=o):
                        return q.tensor_copy(
                            bv16[:, 16 * c + 4 * i + o:16 * c + 4 * i + o + 1],
                            wv_sb[:, 16 * c + 4 * o + i:16 * c + 4 * o + i + 1])
                    bv_evs.append(P.add("vector", f, waits=[ldwv], sig=sV))

        if stage <= 13:
            if _cut():
                return nc
        zm = P.add("gpsimd", lambda q: q.memset(zrow[:], 0.0), sig=sG)
        zevs = []
        for h in range(NH):
            for dst in (A_scr, Bv_scr):
                def fz(q, h=h, dst=dst):
                    return q.dma_start(
                        out=bass.AP(dst, h * 65536, [[512, 128], [1, 512]]),
                        in_=zrow[:])
                zevs.append(P.add("gpsimd", fz, waits=[zm], sig=sL, n=16))
        scat = []
        for h in range(NH):
            c, poff = h // 2, (h % 2) * 64
            def fa(q, h=h, c=c, poff=poff):
                return q.dma_start(
                    out=bass.AP(A_scr, h * 65536, [[1028, 64], [256, 4], [1, 4]]),
                    in_=a16[poff:poff + 64, 16 * c:16 * c + 16])
            def fb(q, h=h, c=c, poff=poff):
                return q.dma_start(
                    out=bass.AP(Bv_scr, h * 65536, [[1028, 64], [256, 4], [1, 4]]),
                    in_=bv16[poff:poff + 64, 16 * c:16 * c + 16])
            scat.append(P.add("gpsimd", fa, waits=[a16_evs[-1], zevs[-1]], sig=sL, n=16))
            scat.append(P.add("gpsimd", fb, waits=[bv_evs[-1], zevs[-1]], sig=sL, n=16))
        if stage <= 14:
            if _cut():
                return nc
        ldA = []
        for h in range(NH):
            for c in range(2):
                ldA.append(P.add("gpsimd", lambda q, h=h, c=c: q.dma_start(
                    out=A_sb[h][c][:], in_=A_scr.ap()[h, 128 * c:128 * c + 128, :]),
                    waits=[scat[-1]], sig=sL, n=16))
                ldA.append(P.add("gpsimd", lambda q, h=h, c=c: q.dma_start(
                    out=Bv_sb[h][c][:], in_=Bv_scr.ap()[h, 128 * c:128 * c + 128, :]),
                    waits=[scat[-1]], sig=sL, n=16))

        if stage <= 15:
            if _cut():
                return nc
        g1 = P.add("vector", lambda q: q.tensor_tensor(
            out=wgs[:], in0=wif[:, 0:1024], in1=wif[:, 1024:2048], op=ALU.add),
            waits=[ldwi, ldwf], sig=sV)
        g2 = P.add("vector", lambda q: q.tensor_tensor(
            out=wgs[:], in0=wgs[:], in1=wif[:, 2048:3072], op=ALU.add),
            waits=[g1], sig=sV)
        if stage <= 17:
            if _cut():
                return nc
        wg_evs = []
        prev = None
        for c in range(8):
            tev = P.add("tensor", lambda q, c=c: q.transpose(
                pq[:, 0:8], wgs[:, 128 * c:128 * c + 128], identf[0:8, 0:8]),
                waits=[g2, ldif] + ([prev] if prev else []), sig=sT)
            prev = P.add("vector", lambda q, c=c: q.tensor_copy(
                wgf_all[:, 8 * c:8 * c + 8], pq[:, 0:8]), waits=[tev], sig=sV)
            wg_evs.append(prev)

        if stage <= 18:
            if _cut():
                return nc
        gmh_ld = P.add("sync", lambda q: q.dma_start(
            out=gbc_mh[:], in_=lnw_h[0:1, :].to_broadcast((128, D))), sig=sL, n=16)
        gmh = [P.add("vector", lambda q: q.tensor_scalar(
            out=gbc_mh[:], in0=gbc_mh[:], scalar1=1.0, scalar2=None, op0=ALU.add),
            waits=[gmh_ld], sig=sV)]

        if stage <= 16:
            if _cut():
                return nc
        xfm_evs = []
        for c in range(8):
            xfm_evs.append(P.add("sync", lambda q, c=c: q.dma_start_transpose(
                x_fm[c][:], xbf_scr[:, 128 * c:128 * c + 128]),
                waits=[stxbf[NT - 1]], sig=sL, n=16))

        if stage <= 21:
            if _cut():
                return nc
        gate_ev = []
        for blk in range(8):
            last_f = None
            for c in range(8):
                def fi(q, blk=blk, c=c):
                    return q.matmul(pq[0:4, :], wgf_all[:, 8 * c:8 * c + 4],
                                    x_fm[c][:, 512 * blk:512 * blk + 512],
                                    start=(c == 0), stop=(c == 7))
                def ff(q, blk=blk, c=c):
                    return q.matmul(po1[0:4, :], wgf_all[:, 8 * c + 4:8 * c + 8],
                                    x_fm[c][:, 512 * blk:512 * blk + 512],
                                    start=(c == 0), stop=(c == 7))
                w = [xfm_evs[-1], wg_evs[-1], gmh[-1], ldA[-1]] if c == 0 else []
                if c == 0 and gate_ev:
                    w.append(gate_ev[-1])
                P.add("tensor", fi, waits=w, sig=None)
                last_f = P.add("tensor", ff, sig=sT if c == 7 else None)
            gate_ev.append(P.add("vector", lambda q, blk=blk: q.tensor_copy(
                gw[0:4, 512 * blk:512 * blk + 512], pq[0:4, :]),
                waits=[last_f] + ([scat[-1]] if blk == 0 else []), sig=sV))
            gate_ev.append(P.add("vector", lambda q, blk=blk: q.tensor_copy(
                gw[0:4, SEQ + 512 * blk:SEQ + 512 * blk + 512], po1[0:4, :]),
                waits=[last_f], sig=sV))
        gb1 = P.add("vector", lambda q: q.tensor_scalar(
            out=gw[0:4, 0:SEQ], in0=gw[0:4, 0:SEQ], scalar1=biasg_i[:],
            scalar2=None, op0=ALU.add), waits=[gate_ev[-1], ldbi], sig=sV)
        gb2 = P.add("vector", lambda q: q.tensor_scalar(
            out=gw[0:4, SEQ:2 * SEQ], in0=gw[0:4, SEQ:2 * SEQ], scalar1=biasg_f[:],
            scalar2=None, op0=ALU.add), waits=[gate_ev[-1], ldbf], sig=sV)

        on4 = P.add("gpsimd", lambda q: q.memset(ones4k[:], 1.0),
                    waits=[zevs[-1]], sig=sG)
        sp1 = P.add("scalar", lambda q: q.activation(
            gw[0:4, SEQ:2 * SEQ], gw[0:4, SEQ:2 * SEQ], AF.Exp, scale=-1.0),
            waits=[gb2], sig=sA)
        sp2 = P.add("scalar", lambda q: q.activation(
            gw[0:4, SEQ:2 * SEQ], gw[0:4, SEQ:2 * SEQ], AF.Ln, bias=1.0),
            waits=[sp1], sig=sA)
        ev = None
        for ck in range(4):
            ev = P.add("vector", lambda q, ck=ck: q.tensor_tensor_scan(
                out=gw[0:4, 2 * SEQ + 1024 * ck:2 * SEQ + 1024 * ck + 1024],
                data0=ones4k[:], data1=gw[0:4, SEQ + 1024 * ck:SEQ + 1024 * ck + 1024],
                initial=0.0, op0=ALU.mult, op1=ALU.add),
                waits=[sp2, on4] if ck == 0 else [ev], sig=sV)
            if ck > 0:
                ev = P.add("vector", lambda q, ck=ck: q.tensor_scalar(
                    out=gw[0:4, 2 * SEQ + 1024 * ck:2 * SEQ + 1024 * ck + 1024],
                    in0=gw[0:4, 2 * SEQ + 1024 * ck:2 * SEQ + 1024 * ck + 1024],
                    scalar1=gw[0:4, 2 * SEQ + 1024 * ck - 1:2 * SEQ + 1024 * ck],
                    scalar2=None, op0=ALU.add), waits=[ev], sig=sV)
        cs_ev = ev
        u_ev = P.add("vector", lambda q: q.tensor_tensor(
            out=gw[0:4, 0:SEQ], in0=gw[0:4, 0:SEQ], in1=gw[0:4, 2 * SEQ:3 * SEQ],
            op=ALU.add), waits=[cs_ev, gb1], sig=sV)
        ev = None
        for ck in range(4):
            ev = P.add("vector", lambda q, ck=ck: q.tensor_tensor_scan(
                out=gw[0:4, SEQ + 1024 * ck:SEQ + 1024 * ck + 1024],
                data0=ones4k[:], data1=gw[0:4, 1024 * ck:1024 * ck + 1024],
                initial=-3.0e38, op0=ALU.mult, op1=ALU.max),
                waits=[u_ev, on4] if ck == 0 else [ev], sig=sV)
            if ck > 0:
                ev = P.add("vector", lambda q, ck=ck: q.tensor_scalar(
                    out=gw[0:4, SEQ + 1024 * ck:SEQ + 1024 * ck + 1024],
                    in0=gw[0:4, SEQ + 1024 * ck:SEQ + 1024 * ck + 1024],
                    scalar1=gw[0:4, SEQ + 1024 * ck - 1:SEQ + 1024 * ck],
                    scalar2=None, op0=ALU.max), waits=[ev], sig=sV)
        m_ev = ev
        rcop = []
        for g in range(8):
            rcop.append(P.add("vector", lambda q, g=g: q.tensor_copy(
                gsm[0:4, 8 + g:9 + g],
                gw[0:4, SEQ + 512 * g + 511:SEQ + 512 * g + 512]),
                waits=[m_ev], sig=sV))
        eu_evs = []
        for g in range(8):
            eu_evs.append(P.add("vector", lambda q, g=g: q.tensor_scalar(
                out=gw[0:4, 512 * g:512 * g + 512],
                in0=gw[0:4, 512 * g:512 * g + 512],
                scalar1=gsm[0:4, 8 + g:9 + g], scalar2=None, op0=ALU.subtract),
                waits=[rcop[g]] + rcop, sig=sV))
        eu2 = P.add("scalar", lambda q: q.activation(
            gw[0:4, 0:SEQ], gw[0:4, 0:SEQ], AF.Exp), waits=eu_evs, sig=sA)
        rflat = None
        for h_ in range(4):
            rflat = P.add("gpsimd", lambda q, h_=h_: q.dma_start(
                out=grow[0:1, 8 * h_:8 * h_ + 8], in_=gsm[h_:h_ + 1, 8:16]),
                waits=rcop, sig=sL, n=16)
        gd_evs = []
        for g in range(8):
            gp = max(g - 1, 0)
            gd_evs.append(P.add("vector", lambda q, g=g, gp=gp: q.tensor_tensor(
                out=gsm[0:4, 16 + g:17 + g], in0=gsm[0:4, 8 + gp:9 + gp],
                in1=gsm[0:4, 8 + g:9 + g], op=ALU.subtract),
                waits=[rcop[g], rcop[gp]], sig=sV))
        gdflat = None
        for h_ in range(4):
            gdflat = P.add("gpsimd", lambda q, h_=h_: q.dma_start(
                out=grow[0:1, 32 + 8 * h_:32 + 8 * h_ + 8],
                in_=gsm[h_:h_ + 1, 16:24]),
                waits=gd_evs + [rflat], sig=sL, n=16)
        gdexp = P.add("scalar", lambda q: q.activation(
            grow[:, 32:64], grow[:, 32:64], AF.Exp), waits=[gdflat], sig=sA)
        st_grow = P.add("gpsimd", lambda q: q.dma_start(
            out=grow_scr.ap(), in_=grow[:]), waits=[gdexp, rflat], sig=sL, n=16)
        rcol_ev = P.add("sync", lambda q: q.dma_start(
            out=rcol[:], in_=grow_scr.ap()[0:1, 0:32].to_broadcast((128, 32))),
            waits=[st_grow], sig=sL, n=16)
        grep_ev = P.add("sync", lambda q: q.dma_start(
            out=grep[:], in_=grow_scr.ap()[0:1, 32:64].to_broadcast((128, 32))),
            waits=[st_grow], sig=sL, n=16)
        grp_evs = [grep_ev] * 8

        if stage <= 22:
            if _cut():
                return nc
        gcol_evs = [None] * NT
        prevc = [None, None]
        for t in range(NT):
            bank = pq if t % 2 == 0 else po1
            tlast = None
            for qi_ in range(3):
                tlast = P.add("tensor", lambda q, t=t, bank=bank, qi_=qi_: q.transpose(
                    bank[:, 4 * qi_:4 * qi_ + 4],
                    gw[0:4, SEQ * qi_ + 128 * t:SEQ * qi_ + 128 * t + 128],
                    identf[0:4, 0:4]),
                    waits=([eu2, rcol_ev, grep_ev] if qi_ == 0 else []) +
                          ([prevc[t % 2]] if (qi_ == 0 and prevc[t % 2]) else []),
                    sig=sT if qi_ == 2 else None)
            gcol_evs[t] = P.add("vector", lambda q, t=t, bank=bank: q.tensor_copy(
                gcol[:, 16 * t:16 * t + 12], bank[:, 0:12]), waits=[tlast], sig=sV)
            prevc[t % 2] = gcol_evs[t]

        if stage <= 23:
            if _cut():
                return nc
        if dbg and (dmask & 4):
            P.add("sync", lambda q: q.dma_start(out=dbg_gcol.ap(), in_=gcol[:]),
                  waits=[gcol_evs[NT - 1], gcol_evs[NT - 2]], sig=sL, n=16)
            P.add("sync", lambda q: q.dma_start(out=dbg_rcol.ap()[:, 0:32], in_=rcol[:]),
                  waits=[rcol_ev], sig=sL, n=16)
            P.add("sync", lambda q: q.dma_start(out=dbg_rcol.ap()[:, 32:64], in_=grep[:]),
                  waits=[grep_ev], sig=sL, n=16)
            P.add("sync", lambda q: q.dma_start(out=dbg_gw.ap(), in_=gw[:]),
                  waits=[eu2], sig=sL, n=16)
        bank_free = {0: gcol_evs[NT - 2], 1: gcol_evs[NT - 1]}
        vt_last = None
        for t in range(NT):
            for pair in range(2):
                bank = pq if pair == 0 else po1
                last = None
                for hh in range(2):
                    h = 2 * pair + hh
                    for c in range(2):
                        def f(q, h=h, c=c, hh=hh, t=t, bank=bank):
                            return q.matmul(
                                bank[:, 256 * hh:256 * hh + 256],
                                x_fm[2 * h + c][:, 128 * t:128 * t + 128],
                                Bv_sb[h][c][:], start=(c == 0), stop=(c == 1))
                        w = [ldA[-1], xfm_evs[-1], bank_free[pair]] \
                            if (hh == 0 and c == 0) else []
                        last = P.add("tensor", f, waits=w,
                                     sig=sT if (hh == 1 and c == 1) else None)
                ev_sc = None
                for hh in range(2):
                    h = 2 * pair + hh
                    ev_sc = P.add("vector", lambda q, h=h, hh=hh, t=t, bank=bank:
                                  q.tensor_scalar(
                                      out=vt[:, 1028 * t + 257 * h:1028 * t + 257 * h + 256],
                                      in0=bank[:, 256 * hh:256 * hh + 256],
                                      scalar1=gcol[:, 16 * t + h:16 * t + h + 1],
                                      scalar2=None, op0=ALU.mult),
                                  waits=[last, gcol_evs[t]], sig=sV)
                    P.add("vector", lambda q, h=h, t=t: q.tensor_copy(
                        vt[:, 1028 * t + 257 * h + 256:1028 * t + 257 * h + 257],
                        gcol[:, 16 * t + h:16 * t + h + 1]),
                        waits=[gcol_evs[t]], sig=sV)
                bank_free[pair] = ev_sc
                vt_last = ev_sc
        vt_all = vt_last

        if stage <= 24:
            if _cut():
                return nc
        xpost_st = [None] * NT
        ps_free = [None, None]
        po_free = [None] * 4
        q_bank_free = {0: bank_free[0], 1: bank_free[1]}
        q_bank_free = {0: bank_free[0], 1: None}
        sct = 0
        for qb in range(NQB):
            qf_evs = []
            rnd = 0
            for h in range(NH):
                for m in range(2):
                    bank = pq if rnd % 2 == 0 else ps[0]
                    bsel = rnd % 2
                    last = None
                    for c in range(2):
                        def f(q, h=h, m=m, c=c, qb=qb, bank=bank):
                            return q.matmul(
                                bank[:], A_sb[h][c][:, 128 * m:128 * m + 128],
                                x_fm[2 * h + c][:, QB * qb:QB * qb + QB],
                                start=(c == 0), stop=(c == 1))
                        w = []
                        if c == 0:
                            if qb == 0 and rnd == 0:
                                w.append(vt_all)
                            if q_bank_free[bsel] is not None:
                                w.append(q_bank_free[bsel])
                            if bsel == 1 and rnd == 1 and ps_free[0] is not None:
                                w.append(ps_free[0])
                        last = P.add("tensor", f, waits=w, sig=sT if c == 1 else None)
                    ev = P.add("vector", lambda q, h=h, m=m, bank=bank: q.tensor_copy(
                        qf[h][m][:], bank[:]), waits=[last], sig=sV)
                    q_bank_free[bsel] = ev
                    qf_evs.append(ev)
                    rnd += 1
            qf_ready = qf_evs[-1]
            ps_free[0] = q_bank_free[1]
            if dbg and (dmask & 8) and qb == 0:
                P.add("gpsimd", lambda q: q.dma_start(
                    out=dbg_po.ap()[0][:, 0:257],
                    in_=vt[:, 0:257]), waits=[vt_all], sig=sL, n=16)
                P.add("gpsimd", lambda q: q.dma_start(
                    out=dbg_po.ap()[1][:, 0:256],
                    in_=qf[0][0][:, 0:256]), waits=[qf_ready], sig=sL, n=16)

            acc_ev = {h: [None] * 4 for h in range(NH)}

            def emit_out(job, sev, sb):
                h, gi, jt, s_, e_ = job
                grp_g = jt // 4
                last_out = None
                for ic in range(4):
                    def f(q, sb=sb, ic=ic, h=h, jt=jt, s=s_, e=e_):
                        return q.matmul(
                            po[ic][:, 0:257],
                            s_sb[sb][:, 128 * ic:128 * ic + 128],
                            vt[:, 1028 * jt + 257 * h:1028 * jt + 257 * h + 257],
                            start=s, stop=e)
                    w = [sev]
                    if s_ and po_free[ic] is not None:
                        w.append(po_free[ic])
                    last_out = P.add("tensor", f, waits=w,
                                     sig=sT if ic == 3 else None)
                if not e_:
                    return last_out
                for ic in range(4):
                    if gi == 0:
                        ev = P.add("vector", lambda q, ic=ic: q.tensor_copy(
                            acc[ic][:], po[ic][:, 0:257]), waits=[last_out], sig=sV)
                    else:
                        def f2(q, ic=ic, h=h, grp_g=grp_g):
                            return q.scalar_tensor_tensor(
                                out=acc[ic][:], in0=acc[ic][:],
                                scalar=grep[:, 8 * h + grp_g:8 * h + grp_g + 1],
                                in1=po[ic][:, 0:257], op0=ALU.mult, op1=ALU.add)
                        ev = P.add("vector", f2,
                                   waits=[last_out, grp_evs[grp_g], acc_ev[h][ic]],
                                   sig=sV)
                    po_free[ic] = ev
                    acc_ev[h][ic] = ev
                return last_out

            for h in range(NH):
                n_jt = 4 * (qb + 1)
                jobs = [(h, jt // 4, jt, jt % 4 == 0, jt % 4 == 3)
                        for jt in range(n_jt)]
                pend = None
                for job in jobs + [None]:
                    if job is not None:
                        _h, gi, jt, s_, e_ = job
                        sb = sct % 2
                        last = None
                        for c in range(2):
                            def f(q, h=h, c=c, jt=jt, sb=sb):
                                return q.matmul(
                                    ps[sb][:], x_fm[2 * h + c][:, 128 * jt:128 * jt + 128],
                                    qf[h][c][:], start=(c == 0), stop=(c == 1))
                            w = [qf_evs[2 * h + 1]] if c == 0 else []
                            if c == 0 and ps_free[sb] is not None:
                                w.append(ps_free[sb])
                            last = P.add("tensor", f, waits=w,
                                         sig=sT if c == 1 else None)
                    if pend is not None:
                        ps_free[pend[2]] = emit_out(pend[0], pend[1], pend[2])
                    if job is not None:
                        diag = (jt >= 4 * qb)
                        if diag:
                            r = jt - 4 * qb
                            sev = P.add("vector", lambda q, sb=sb, r=r: q.tensor_tensor(
                                out=s_sb[sb][:], in0=ps[sb][:],
                                in1=masks[:, 512 * r:512 * r + 512], op=ALU.mult),
                                waits=[last, ldmk], sig=sV)
                        else:
                            sev = P.add("scalar", lambda q, sb=sb: q.copy(
                                s_sb[sb][:], ps[sb][:]), waits=[last], sig=sA)
                        pend = (job, sev, sb)
                        sct += 1
                    else:
                        pend = None

                ep = []
                for ic in range(4):
                    tg = 4 * qb + ic
                    ep.append(P.add("vector", lambda q, ic=ic: q.reduce_sum(
                        stt_sb[:, ic:ic + 1], acc[ic][:, 0:256], AX.X),
                        waits=[acc_ev[h][ic]], sig=sV))
                    ep.append(P.add("scalar", lambda q, ic=ic: q.activation(
                        sqscr[:, 0:256], acc[ic][:, 0:256], AF.Square,
                        accum_out=stt_sb[:, 4 + ic:5 + ic]),
                        waits=[acc_ev[h][ic]], sig=sA))
                    ep.append(P.add("vector", lambda q, ic=ic: q.tensor_copy(
                        stt_sb[:, 8 + ic:9 + ic], acc[ic][:, 256:257]),
                        waits=[acc_ev[h][ic]], sig=sV))
                    gq = qb
                    ep.append(P.add("vector", lambda q, ic=ic, tg=tg, h=h, gq=gq:
                              q.tensor_tensor(
                                  out=stt_sb[:, 12 + ic:13 + ic],
                                  in0=rcol[:, 8 * h + gq:8 * h + gq + 1],
                                  in1=gcol[:, 16 * tg + 4 + h:16 * tg + 5 + h],
                                  op=ALU.subtract),
                              waits=[gcol_evs[tg], rcol_ev], sig=sV))
                    ep.append(P.add("vector", lambda q, ic=ic, tg=tg, h=h: q.tensor_tensor(
                        out=stt_sb[:, 16 + ic:17 + ic],
                        in0=gcol[:, 16 * tg + 8 + h:16 * tg + 9 + h],
                        in1=gcol[:, 16 * tg + 4 + h:16 * tg + 5 + h], op=ALU.subtract),
                        waits=[gcol_evs[tg]], sig=sV))
                e1 = P.add("scalar", lambda q: q.activation(
                    stt_sb[:, 20:28], stt_sb[:, 12:20], AF.Exp), waits=ep, sig=sA)
                e2 = P.add("scalar", lambda q: q.activation(
                    stt_sb[:, 8:12], stt_sb[:, 8:12], AF.Abs), waits=ep, sig=sA)
                e3 = P.add("vector", lambda q: q.tensor_tensor(
                    out=stt_sb[:, 40:44], in0=stt_sb[:, 20:24], in1=stt_sb[:, 8:12],
                    op=ALU.mult), waits=[e1, e2], sig=sV)
                e4 = P.add("vector", lambda q: q.tensor_tensor(
                    out=stt_sb[:, 40:44], in0=stt_sb[:, 40:44], in1=stt_sb[:, 24:28],
                    op=ALU.max), waits=[e3], sig=sV)
                e5 = P.add("vector", lambda q: q.tensor_scalar(
                    out=stt_sb[:, 40:44], in0=stt_sb[:, 40:44],
                    scalar1=consts[:, 2:3], scalar2=None, op0=ALU.add),
                    waits=[e4, ldc], sig=sV)
                e6 = P.add("vector", lambda q: q.reciprocal(
                    stt_sb[:, 44:48], stt_sb[:, 40:44]), waits=[e5], sig=sV)
                e7 = P.add("vector", lambda q: q.tensor_tensor(
                    out=stt_sb[:, 20:24], in0=stt_sb[:, 20:24], in1=stt_sb[:, 44:48],
                    op=ALU.mult), waits=[e6], sig=sV)
                e8 = P.add("vector", lambda q: q.tensor_scalar(
                    out=stt_sb[:, 28:32], in0=stt_sb[:, 0:4], scalar1=consts[:, 3:4],
                    scalar2=None, op0=ALU.mult), waits=ep, sig=sV)
                e9 = P.add("vector", lambda q: q.tensor_tensor(
                    out=stt_sb[:, 48:52], in0=stt_sb[:, 28:32], in1=stt_sb[:, 28:32],
                    op=ALU.mult), waits=[e8], sig=sV)
                e10 = P.add("vector", lambda q: q.tensor_scalar(
                    out=stt_sb[:, 32:36], in0=stt_sb[:, 4:8], scalar1=consts[:, 3:4],
                    scalar2=None, op0=ALU.mult), waits=ep, sig=sV)
                e11 = P.add("vector", lambda q: q.tensor_tensor(
                    out=stt_sb[:, 32:36], in0=stt_sb[:, 32:36], in1=stt_sb[:, 48:52],
                    op=ALU.subtract), waits=[e9, e10], sig=sV)
                e12 = P.add("vector", lambda q: q.tensor_scalar(
                    out=stt_sb[:, 32:36], in0=stt_sb[:, 32:36], scalar1=0.0,
                    scalar2=None, op0=ALU.max), waits=[e11], sig=sV)
                e12a = P.add("vector", lambda q: q.tensor_scalar(
                    out=stt_sb[:, 48:52], in0=stt_sb[:, 32:36], scalar1=1e14,
                    scalar2=None, op0=ALU.mult), waits=[e12], sig=sV)
                e12b = P.add("scalar", lambda q: q.activation(
                    stt_sb[:, 48:52], stt_sb[:, 48:52], AF.Ln),
                    waits=[e12a], sig=sA)
                e12bb = P.add("vector", lambda q: q.tensor_scalar(
                    out=stt_sb[:, 48:52], in0=stt_sb[:, 48:52], scalar1=0.5,
                    scalar2=-16.11809565095832, op0=ALU.mult, op1=ALU.add),
                    waits=[e12b], sig=sV)
                e12c = P.add("scalar", lambda q: q.activation(
                    stt_sb[:, 48:52], stt_sb[:, 48:52], AF.Exp),
                    waits=[e12bb], sig=sA)
                e13 = P.add("vector", lambda q: q.tensor_tensor(
                    out=stt_sb[:, 48:52], in0=stt_sb[:, 20:24], in1=stt_sb[:, 48:52],
                    op=ALU.mult), waits=[e12c, e7], sig=sV)
                e13b = P.add("vector", lambda q: q.tensor_tensor(
                    out=stt_sb[:, 52:56], in0=stt_sb[:, 48:52], in1=stt_sb[:, 48:52],
                    op=ALU.mult), waits=[e13], sig=sV)
                e13c = P.add("vector", lambda q: q.tensor_scalar(
                    out=stt_sb[:, 52:56], in0=stt_sb[:, 52:56], scalar1=consts[:, 0:1],
                    scalar2=None, op0=ALU.add), waits=[e13b], sig=sV)
                e14 = P.add("scalar", lambda q: q.activation(
                    stt_sb[:, 52:56], stt_sb[:, 52:56], AF.Ln), waits=[e13c], sig=sA)
                e15 = P.add("scalar", lambda q: q.activation(
                    stt_sb[:, 52:56], stt_sb[:, 52:56], AF.Exp, scale=consts[:, 1:2]),
                    waits=[e14], sig=sA)
                e16 = P.add("vector", lambda q: q.tensor_tensor(
                    out=stt_sb[:, 36:40], in0=stt_sb[:, 20:24], in1=stt_sb[:, 52:56],
                    op=ALU.mult), waits=[e15], sig=sV)
                if dbg and (dmask & 8) and qb == 0 and h == 0:
                    for icd in range(4):
                        P.add("sync", lambda q, icd=icd: q.dma_start(
                            out=dbg_acc.ap()[icd], in_=acc[icd][:]),
                            waits=[acc_ev[h][icd]], sig=sL, n=16)
                dmp = None
                if dbg and (dmask & 1) and qb >= 6:
                    dmp = P.add("sync", lambda q, h=h, qb=qb: q.dma_start(
                        out=dbg_st2.ap()[qb - 6, h], in_=stt_sb[:]),
                        waits=[e16], sig=sL, n=16)
                if dbg and (dmask & 16) and qb == 0:
                    dmp = P.add("sync", lambda q, h=h: q.dma_start(
                        out=dbg_st2.ap()[0, h], in_=stt_sb[:]),
                        waits=[e16], sig=sL, n=16)
                last_apply = None
                for ic in range(4):
                    def f(q, ic=ic, h=h):
                        return q.tensor_scalar(
                            out=hatt[ic][:, 256 * h:256 * h + 256],
                            in0=acc[ic][:, 0:256],
                            scalar1=stt_sb[:, 28 + ic:29 + ic],
                            scalar2=stt_sb[:, 36 + ic:37 + ic],
                            op0=ALU.subtract, op1=ALU.mult)
                    w = [e16, dmp if (dwait & 1) else None]
                    if h == 0 and qb > 0:
                        w.append(xpost_st[4 * (qb - 1) + ic])
                    last_apply = P.add("vector", f, waits=w, sig=sV)

            for ic in range(4):
                t = 4 * qb + ic
                w = [stxbf[NT - 2 + (t % 2)], eu2, ldA[-1], gcol_evs[NT - 1],
                     gcol_evs[NT - 2]]
                if t >= 2:
                    w.append(xpost_st[t - 2])
                xold = P.add("sync", lambda q, t=t: q.dma_start(
                    out=xin[t % 2][:], in_=x_h[128 * t:128 * t + 128, :]),
                    waits=w, sig=sL, n=16)
                hdmp = None
                if dbg and (dmask & 2) and qb == 7:
                    hdmp = P.add("sync", lambda q, ic=ic: q.dma_start(
                        out=dbg_hat.ap()[ic], in_=hatt[ic][:]),
                        waits=[last_apply], sig=sL, n=16)
                gam = P.add("gpsimd", lambda q, ic=ic: q.tensor_tensor(
                    out=hatt[ic][:], in0=hatt[ic][:], in1=gbc_mh[:], op=ALU.mult),
                    waits=[last_apply, hdmp if (dwait & 2) else None, gmh[-1]], sig=sG)
                r1 = P.add("vector", lambda q, ic=ic, t=t: q.scalar_tensor_tensor(
                    out=hatt[ic][:], in0=hatt[ic][:], scalar=1.0, in1=xin[t % 2][:],
                    op0=ALU.mult, op1=ALU.add,
                    accum_out=stt_sb[:, 52 + ic:53 + ic]), waits=[gam, xold], sig=sV)
                r2 = P.add("scalar", lambda q, ic=ic: q.activation(
                    sqscr[:], hatt[ic][:], AF.Square,
                    accum_out=stt_sb[:, 56 + ic:57 + ic]), waits=[r1], sig=sA)
                r3 = P.add("vector", lambda q, ic=ic: q.tensor_scalar(
                    out=stt_sb[:, 60 + ic:61 + ic], in0=stt_sb[:, 52 + ic:53 + ic],
                    scalar1=consts[:, 4:5], scalar2=None, op0=ALU.mult),
                    waits=[r1], sig=sV)
                r4 = P.add("vector", lambda q, ic=ic: q.tensor_tensor(
                    out=stt_sb[:, 48 + ic:49 + ic], in0=stt_sb[:, 60 + ic:61 + ic],
                    in1=stt_sb[:, 60 + ic:61 + ic], op=ALU.mult), waits=[r3], sig=sV)
                r5 = P.add("vector", lambda q, ic=ic: q.tensor_scalar(
                    out=stt_sb[:, 44 + ic:45 + ic], in0=stt_sb[:, 56 + ic:57 + ic],
                    scalar1=consts[:, 4:5], scalar2=None, op0=ALU.mult),
                    waits=[r2], sig=sV)
                r6 = P.add("vector", lambda q, ic=ic: q.tensor_tensor(
                    out=stt_sb[:, 44 + ic:45 + ic], in0=stt_sb[:, 44 + ic:45 + ic],
                    in1=stt_sb[:, 48 + ic:49 + ic], op=ALU.subtract),
                    waits=[r4, r5], sig=sV)
                r7 = P.add("scalar", lambda q, ic=ic: q.activation(
                    stt_sb[:, 40 + ic:41 + ic], stt_sb[:, 44 + ic:45 + ic],
                    AF.Ln, bias=consts[:, 0:1]), waits=[r6], sig=sA)
                r8 = P.add("scalar", lambda q, ic=ic: q.activation(
                    stt_sb[:, 40 + ic:41 + ic], stt_sb[:, 40 + ic:41 + ic],
                    AF.Exp, scale=consts[:, 1:2]), waits=[r7], sig=sA)
                r9 = P.add("vector", lambda q, ic=ic: q.tensor_scalar(
                    out=hatt[ic][:], in0=hatt[ic][:],
                    scalar1=stt_sb[:, 60 + ic:61 + ic],
                    scalar2=stt_sb[:, 40 + ic:41 + ic],
                    op0=ALU.subtract, op1=ALU.mult), waits=[r8], sig=sV)
                xpost_st[t] = P.add("sync", lambda q, t=t, ic=ic: q.dma_start(
                    out=xpost_scr[128 * t:128 * t + 128, :], in_=hatt[ic][:]),
                    waits=[r9], sig=sL, n=16)

        if stage <= 25:
            if _cut():
                return nc
        bar0 = P.barrier(sW)

        tern_done = []
        mrep_evs = []
        for wi_, (src, dst, nch, nsub) in enumerate(
                [(w1_h, t1_scr, 32, 1), (w2_h, t2_scr, 8, 4)]):
            prev_use = {}
            last_acc = None
            for ch in range(nch):
                for sb_ in range(nsub):
                    k = ch * nsub + sb_
                    def fl(q, src=src, ch=ch, sb_=sb_, k=k):
                        return q.dma_start(
                            out=wstage[k % 2][:],
                            in_=src[128 * ch:128 * ch + 128, 1024 * sb_:1024 * sb_ + 1024])
                    w = list(bar0) if k == 0 else []
                    if k % 2 in prev_use:
                        w.append(prev_use[k % 2])
                    lde = P.add("scalar", fl, waits=w, sig=sL, n=16)
                    abe = P.add("scalar", lambda q, k=k: q.activation(
                        wabs[:], wstage[k % 2][:], AF.Abs, accum_out=wsum[:, 0:1]),
                        waits=[lde] + ([last_acc] if last_acc else list(bar0)), sig=sA)
                    def fac(q, wi_=wi_, k=k):
                        if k == 0:
                            return q.tensor_copy(wsum[:, 2 + wi_:3 + wi_], wsum[:, 0:1])
                        return q.tensor_tensor(
                            out=wsum[:, 2 + wi_:3 + wi_], in0=wsum[:, 2 + wi_:3 + wi_],
                            in1=wsum[:, 0:1], op=ALU.add)
                    last_acc = P.add("vector", fac, waits=[abe], sig=sV)
                    prev_use[k % 2] = last_acc
            t1e = P.add("tensor", lambda q, wi_=wi_: q.matmul(
                pq[0:1, 0:1], consts[:, 7:8], wsum[:, 2 + wi_:3 + wi_],
                start=True, stop=True), waits=[last_acc] + bar0, sig=sT)
            t2e = P.add("vector", lambda q: q.tensor_copy(
                wsum[0:1, 8:9], pq[0:1, 0:1]), waits=[t1e], sig=sV)
            t3e = P.add("vector", lambda q: q.reciprocal(
                wsum[0:1, 9:10], wsum[0:1, 8:9]), waits=[t2e], sig=sV)
            t4e = P.add("vector", lambda q: q.tensor_scalar(
                out=wsum[0:1, 10:11], in0=wsum[0:1, 9:10],
                scalar1=float(4096 * 1024), scalar2=None, op0=ALU.mult),
                waits=[t3e], sig=sV)
            t4b = P.add("vector", lambda q: q.tensor_scalar(
                out=wsum[0:1, 11:12], in0=wsum[0:1, 8:9],
                scalar1=1.0 / float(4096 * 1024), scalar2=None, op0=ALU.mult),
                waits=[t2e], sig=sV)
            t5e = P.add("gpsimd", lambda q: q.dma_start(
                out=msc_scr.ap()[0:1, 0:2], in_=wsum[0:1, 10:12]),
                waits=[t4e, t4b], sig=sL, n=16)
            t6e = P.add("sync", lambda q: q.dma_start(
                out=wsum[:, 12:13],
                in_=msc_scr.ap()[0:1, 0:1].to_broadcast((128, 1))),
                waits=[t5e], sig=sL, n=16)
            t6b = P.add("sync", lambda q, wi_=wi_: q.dma_start(
                out=mrep[:, wi_:wi_ + 1],
                in_=msc_scr.ap()[0:1, 1:2].to_broadcast((128, 1))),
                waits=[t5e], sig=sL, n=16)
            mrep_evs.append(t6b)
            prev2 = {}
            stv = None
            qe_prev = None
            for ch in range(nch):
                for sb_ in range(nsub):
                    k = ch * nsub + sb_
                    def fl(q, src=src, ch=ch, sb_=sb_, k=k):
                        return q.dma_start(
                            out=wstage[k % 2][:],
                            in_=src[128 * ch:128 * ch + 128, 1024 * sb_:1024 * sb_ + 1024])
                    w = [t6e] + ([prev2[k % 2]] if k % 2 in prev2 else [])
                    lde = P.add("scalar", fl, waits=w, sig=sL, n=16)
                    ce = None
                    for hf in range(2):
                        w0 = [lde] + ([qe_prev] if qe_prev else []) \
                            + ([stv] if (stv and hf == 0) else []) \
                            + ([xpost_st[NT - 1]] if k == 0 and hf == 0 else [])
                        ce = uround(wstage[k % 2][:, 512 * hf:512 * hf + 512],
                                    wsum[:, 12:13],
                                    wtern[:, 512 * hf:512 * hf + 512],
                                    w0, clamp=True)
                    qe_prev = ce
                    prev2[k % 2] = ce
                    stv = P.add("scalar", lambda q, dst=dst, ch=ch, sb_=sb_: q.dma_start(
                        out=dst[128 * ch:128 * ch + 128, 1024 * sb_:1024 * sb_ + 1024],
                        in_=wtern[:]), waits=[ce], sig=sL, n=16)
            tern_done.append(stv)

        if stage <= 26:
            if _cut():
                return nc
        barA = P.barrier(sW, waits=[xpost_st[NT - 1], tern_done[-1]])

        wt_evs = []
        for c in range(8):
            wt_evs.append(P.add("sync", lambda q, c=c: q.dma_start_transpose(
                W1t[c][:], t1_scr[:, 128 * c:128 * c + 128]), waits=barA, sig=sL, n=16))
        for c in range(32):
            wt_evs.append(P.add("sync", lambda q, c=c: q.dma_start_transpose(
                W2t[c][:], t2_scr[:, 128 * c:128 * c + 128]), waits=barA, sig=sL, n=16))

        bc_evs = []
        for (hsrc, dstt, width) in [
                (l1w_h, g1bc, 1024), (l1b_h, b1bc_s, 1024), (b2_h, bb2, 1024),
                (l2w_h, g2bc, 4096), (l2b_h, b2bc_s, 4096), (b1_h, bb1, 4096)]:
            bc_evs.append(P.add("gpsimd", lambda q, hsrc=hsrc, dstt=dstt, width=width:
                           q.dma_start(out=dstt[:],
                                       in_=hsrc[0:1, :].to_broadcast((128, width))),
                           waits=barA, sig=sL, n=16))
        bc_all = bc_evs[-1]

        def heron(col_t, col_s, iters, waits, seed_half):
            if seed_half:
                ev = P.add("vector", lambda q, a=col_t, b=col_s: q.tensor_scalar(
                    out=bstats[:, b:b + 1], in0=bstats[:, a:a + 1],
                    scalar1=0.5, scalar2=0.5, op0=ALU.mult, op1=ALU.add),
                    waits=waits, sig=sV)
            else:
                ev = P.add("vector", lambda q, a=col_t, b=col_s: q.tensor_scalar(
                    out=bstats[:, b:b + 1], in0=bstats[:, a:a + 1],
                    scalar1=0.0, scalar2=1.0, op0=ALU.mult, op1=ALU.add),
                    waits=waits, sig=sV)
            for _ in range(iters):
                ev = P.add("vector", lambda q, b=col_s: q.reciprocal(
                    bstats[:, b + 1:b + 2], bstats[:, b:b + 1]), waits=[ev], sig=sV)
                ev = P.add("vector", lambda q, a=col_t, b=col_s: q.tensor_tensor(
                    out=bstats[:, b + 1:b + 2], in0=bstats[:, a:a + 1],
                    in1=bstats[:, b + 1:b + 2], op=ALU.mult), waits=[ev], sig=sV)
                ev = P.add("vector", lambda q, b=col_s: q.tensor_tensor(
                    out=bstats[:, b:b + 1], in0=bstats[:, b:b + 1],
                    in1=bstats[:, b + 1:b + 2], op=ALU.add), waits=[ev], sig=sV)
                ev = P.add("vector", lambda q, b=col_s: q.tensor_scalar(
                    out=bstats[:, b:b + 1], in0=bstats[:, b:b + 1],
                    scalar1=0.5, scalar2=None, op0=ALU.mult), waits=[ev], sig=sV)
            ev = P.add("vector", lambda q, b=col_s: q.reciprocal(
                bstats[:, b:b + 1], bstats[:, b:b + 1]), waits=[ev], sig=sV)
            return ev

        def rsqrt_nr(col_t, col_s, waits):
            ct, cs = col_t, col_s
            ev = P.add("vector", lambda q: q.tensor_scalar(
                out=scri[:, 0:1],
                in0=bstats[:, ct:ct + 1].bitcast(I32),
                scalar1=1, scalar2=None, op0=ALU.logical_shift_right),
                waits=waits, sig=sV)
            ev = P.add("vector", lambda q: q.tensor_scalar(
                out=bstats[:, cs:cs + 1], in0=scri[:, 0:1].bitcast(F32),
                scalar1=1.2665333395337613e19, scalar2=None, op0=ALU.mult),
                waits=[ev], sig=sV)
            ev = P.add("vector", lambda q: q.reciprocal(
                bstats[:, cs + 1:cs + 2], bstats[:, cs:cs + 1]),
                waits=[ev], sig=sV)
            for _ in range(4):
                ev = P.add("vector", lambda q: q.tensor_tensor(
                    out=bstats[:, cs:cs + 1], in0=bstats[:, cs + 1:cs + 2],
                    in1=bstats[:, cs + 1:cs + 2], op=ALU.mult), waits=[ev], sig=sV)
                ev = P.add("vector", lambda q: q.tensor_tensor(
                    out=bstats[:, cs:cs + 1], in0=bstats[:, cs:cs + 1],
                    in1=bstats[:, ct:ct + 1], op=ALU.mult), waits=[ev], sig=sV)
                ev = P.add("vector", lambda q: q.tensor_scalar(
                    out=bstats[:, cs:cs + 1], in0=bstats[:, cs:cs + 1],
                    scalar1=-0.5, scalar2=1.5, op0=ALU.mult, op1=ALU.add),
                    waits=[ev], sig=sV)
                ev = P.add("vector", lambda q: q.tensor_tensor(
                    out=bstats[:, cs + 1:cs + 2], in0=bstats[:, cs + 1:cs + 2],
                    in1=bstats[:, cs:cs + 1], op=ALU.mult), waits=[ev], sig=sV)
            ev = P.add("vector", lambda q: q.tensor_scalar(
                out=bstats[:, cs:cs + 1], in0=bstats[:, cs + 1:cs + 2],
                scalar1=1.0, scalar2=None, op0=ALU.mult), waits=[ev], sig=sV)
            return ev

        xp_free = [None, None]
        q1_free = [None, None]
        h1_free = None
        q2_free = None
        scr_free = None
        ptr_free = None
        pb_po_free = list(po_free)
        pb_ps_free = list(ps_free)
        out_st = [None] * NT

        st_fap = [None] * NT
        st_s2 = [None] * NT
        st_mmh1 = [None] * NT
        st_q1c = [None] * NT
        st_gel = [None] * NT
        st_mmh0 = [None] * NT
        st_p1 = [None] * NT
        st_q2e = [None] * NT
        st_q2c = [None] * NT
        st_mm2 = [None] * NT
        st_f1e = [None] * NT
        st_f2e = [None] * NT
        st_fr2 = [None] * NT
        st_qb16 = [None] * NT
        out_st = [None] * NT

        def S1(t):
            pr = t % 2
            lxp = P.add("sync", lambda q, t=t, pr=pr: q.dma_start(
                out=xp[pr][:], in_=xpost_scr[128 * t:128 * t + 128, :]),
                waits=[xpost_st[t]] + barA +
                      ([st_fap[t - 2]] if t >= 2 else []), sig=sL, n=16)
            b1e = P.add("vector", lambda q, pr=pr: q.reduce_sum(
                bstats[:, 0:1], xp[pr][:], AX.X), waits=[lxp] + barA, sig=sV)
            b2e = P.add("scalar", lambda q, pr=pr: q.activation(
                scr32[:], xp[pr][:], AF.Square, accum_out=bstats[:, 1:2]),
                waits=[lxp], sig=sA)
            b3e = P.add("vector", lambda q: q.tensor_scalar(
                out=bstats[:, 2:3], in0=bstats[:, 0:1], scalar1=consts[:, 4:5],
                scalar2=None, op0=ALU.mult), waits=[b1e], sig=sV)
            b4e = P.add("vector", lambda q: q.tensor_tensor(
                out=bstats[:, 3:4], in0=bstats[:, 2:3], in1=bstats[:, 2:3],
                op=ALU.mult), waits=[b3e], sig=sV)
            b5e = P.add("vector", lambda q: q.tensor_scalar(
                out=bstats[:, 4:5], in0=bstats[:, 1:2], scalar1=consts[:, 4:5],
                scalar2=None, op0=ALU.mult), waits=[b2e], sig=sV)
            b6e = P.add("vector", lambda q: q.tensor_tensor(
                out=bstats[:, 4:5], in0=bstats[:, 4:5], in1=bstats[:, 3:4],
                op=ALU.subtract), waits=[b4e, b5e], sig=sV)
            b7e = P.add("vector", lambda q: q.tensor_scalar(
                out=bstats[:, 5:6], in0=bstats[:, 4:5], scalar1=consts[:, 0:1],
                scalar2=None, op0=ALU.add), waits=[b6e], sig=sV)
            rstd1 = heron(5, 6, 2, [b7e], seed_half=False)
            a1 = P.add("vector", lambda q, pr=pr: q.tensor_scalar(
                out=hln[:], in0=xp[pr][:], scalar1=bstats[:, 2:3],
                scalar2=bstats[:, 6:7], op0=ALU.subtract, op1=ALU.mult),
                waits=[rstd1, b2e] +
                      ([st_gel[t - 1][-1]] if t >= 1 else []), sig=sV, basic=True)
            a2 = P.add("vector", lambda q: q.tensor_tensor(
                out=hln[:], in0=hln[:], in1=g1bc[:], op=ALU.mult),
                waits=[a1, bc_all], sig=sV, basic=True)
            a3 = P.add("vector", lambda q: q.tensor_tensor(
                out=hln[:], in0=hln[:], in1=b1bc_s[:], op=ALU.add),
                waits=[a2], sig=sV, basic=True)
            qm = P.add("vector", lambda q: q.tensor_reduce(
                bstats[:, 8:9], hln[:], AX.X, ALU.max, apply_absolute_value=True),
                waits=[a3], sig=sV)
            qs1 = P.add("vector", lambda q: q.tensor_scalar(
                out=bstats[:, 9:10], in0=bstats[:, 8:9], scalar1=consts[:, 0:1],
                scalar2=None, op0=ALU.max), waits=[qm], sig=sV)
            qs2 = P.add("vector", lambda q: q.reciprocal(
                bstats[:, 10:11], bstats[:, 9:10]), waits=[qs1], sig=sV)
            qs3 = P.add("vector", lambda q: q.tensor_scalar(
                out=bstats[:, 11:12], in0=bstats[:, 10:11], scalar1=consts[:, 6:7],
                scalar2=None, op0=ALU.mult), waits=[qs2], sig=sV)
            f1a = P.add("vector", lambda q, pr=pr: q.tensor_scalar(
                out=fcol[:, pr:pr + 1], in0=bstats[:, 9:10], scalar1=1.0 / 127.0,
                scalar2=None, op0=ALU.mult), waits=[qs1], sig=sV)
            f1e = P.add("vector", lambda q, pr=pr: q.tensor_tensor(
                out=fcol[:, pr:pr + 1], in0=fcol[:, pr:pr + 1], in1=mrep[:, 0:1],
                op=ALU.mult), waits=[f1a, mrep_evs[0]], sig=sV)
            st_f1e[t] = f1e
            qprev = None
            for cc in range(2):
                w0 = [qs3] + ([qprev] if qprev else [])
                if cc == 0 and t >= 2 and out_st[t - 2] is not None:
                    w0.append(out_st[t - 2])
                qprev = uround_bs(hln[:, 512 * cc:512 * cc + 512], bstats[:, 11:12],
                                  q1[pr][:, 512 * cc:512 * cc + 512], w0)
            st_qb16[t] = qprev
            tr_evs = []
            for g4 in range(2):
                tes = []
                for cc in range(4):
                    c = 4 * g4 + cc
                    te = P.add("tensor", lambda q, c=c, cc=cc, pr=pr: q.transpose(
                        ptr[:, 128 * cc:128 * cc + 128],
                        q1[pr][:, 128 * c:128 * c + 128], identb[:]),
                        waits=[qprev, ldib] + (tr_evs[-1:] if g4 > 0 else
                              ([st_q1c[t - 1]] if t >= 1 else [])),
                        sig=sT)
                    tes.append(te)
                tr_evs.append(P.add("scalar", lambda q, g4=g4, pr=pr: q.activation(
                    q1[pr][:, 512 * g4:512 * g4 + 512], ptr[:, 0:512], AF.Copy),
                    waits=tes, sig=sA))
            st_q1c[t] = tr_evs[-1]

        def MM1H0(t):
            pr = t % 2
            mm_last = {}
            for k in range(8):
                for hc2 in range(4):
                    def f(q, k=k, hc2=hc2, pr=pr):
                        return q.matmul(
                            po[hc2][:], q1[pr][:, 128 * k:128 * k + 128],
                            W1t[k][:, 512 * hc2:512 * hc2 + 512],
                            start=(k == 0), stop=(k == 7))
                    w = []
                    if k == 0:
                        w = [st_q1c[t]]
                        if t == 0:
                            w.append(wt_evs[7])
                        if pb_po_free[hc2] is not None:
                            w.append(pb_po_free[hc2])
                    ev = P.add("tensor", f, waits=w, sig=sT if k == 7 else None)
                    if k == 7:
                        mm_last[hc2] = ev
            st_mmh0[t] = mm_last

        def S2(t):
            pr = t % 2
            gels = []

            def evac_gelu(hc2, hc, mm_ev):
                sl = 512 * (hc % 2)
                w = [mm_ev, st_f1e[t], bc_all]
                if hc == 0:
                    w.append(st_qb16[t])
                    if t >= 1:
                        w.append(st_gel[t - 1][6])
                elif hc == 1 and t >= 1:
                    w.append(st_gel[t - 1][7])
                if len(gels) >= 2:
                    w.append(gels[-2])
                e1 = P.add("vector", lambda q, hc2=hc2, hc=hc, sl=sl, pr=pr:
                           q.scalar_tensor_tensor(
                               out=scr32[:, sl:sl + 512], in0=po[hc2][:],
                               scalar=fcol[:, pr:pr + 1],
                               in1=bb1[:, 512 * hc:512 * hc + 512],
                               op0=ALU.mult, op1=ALU.add), waits=w, sig=sV)
                pb_po_free[hc2] = e1
                gw_ = [e1]
                if hc == 0 and t >= 1 and st_p1[t - 1] is not None:
                    gw_.append(st_p1[t - 1])
                ge = P.add("scalar", lambda q, hc=hc, sl=sl: q.activation(
                    h1[:, 512 * hc:512 * hc + 512], scr32[:, sl:sl + 512],
                    AF.Gelu, accum_out=bstats[:, 16 + hc:17 + hc]),
                    waits=gw_, sig=sA)
                gels.append(ge)

            for hc2 in range(4):
                evac_gelu(hc2, hc2, st_mmh0[t][hc2])
            st_s2[t] = (gels, evac_gelu)

        def MM1H1(t):
            pr = t % 2
            mm_last = {}
            for k in range(8):
                for hc2 in range(4):
                    hc = 4 + hc2
                    def f(q, k=k, hc2=hc2, hc=hc, pr=pr):
                        return q.matmul(
                            po[hc2][:], q1[pr][:, 128 * k:128 * k + 128],
                            W1t[k][:, 512 * hc:512 * hc + 512],
                            start=(k == 0), stop=(k == 7))
                    w = []
                    if k == 0:
                        w = [pb_po_free[hc2]]
                    ev = P.add("tensor", f, waits=w, sig=sT if k == 7 else None)
                    if k == 7:
                        mm_last[hc2] = ev
            st_mmh1[t] = mm_last

        def S2b(t):
            gels, evac_gelu = st_s2[t]
            for hc2 in range(4):
                evac_gelu(hc2, 4 + hc2, st_mmh1[t][hc2])
            st_gel[t] = gels

        def S3(t):
            pr = t % 2
            l2s1 = P.add("vector", lambda q: q.reduce_sum(
                bstats[:, 24:25], bstats[:, 16:24], AX.X), waits=st_gel[t], sig=sV)
            l2s2 = P.add("scalar", lambda q: q.activation(
                sqb[:], h1[:], AF.Square, accum_out=bstats[:, 25:26]),
                waits=st_gel[t] + ([st_q2e[t - 1]] if t >= 1 else []), sig=sA)
            l2s3 = P.add("vector", lambda q: q.tensor_scalar(
                out=bstats[:, 26:27], in0=bstats[:, 24:25], scalar1=consts[:, 5:6],
                scalar2=None, op0=ALU.mult), waits=[l2s1], sig=sV)
            l2s4 = P.add("vector", lambda q: q.tensor_tensor(
                out=bstats[:, 27:28], in0=bstats[:, 26:27], in1=bstats[:, 26:27],
                op=ALU.mult), waits=[l2s3], sig=sV)
            l2s5 = P.add("vector", lambda q: q.tensor_scalar(
                out=bstats[:, 28:29], in0=bstats[:, 25:26], scalar1=consts[:, 5:6],
                scalar2=None, op0=ALU.mult), waits=[l2s2], sig=sV)
            l2s6 = P.add("vector", lambda q: q.tensor_tensor(
                out=bstats[:, 28:29], in0=bstats[:, 28:29], in1=bstats[:, 27:28],
                op=ALU.subtract), waits=[l2s4, l2s5], sig=sV)
            l2s7 = P.add("vector", lambda q: q.tensor_scalar(
                out=bstats[:, 29:30], in0=bstats[:, 28:29], scalar1=consts[:, 0:1],
                scalar2=1e-6, op0=ALU.add, op1=ALU.max), waits=[l2s6], sig=sV)
            rstd2 = rsqrt_nr(29, 30, [l2s7])
            p1 = P.add("vector", lambda q: q.tensor_scalar(
                out=h2b[:], in0=h1[:], scalar1=bstats[:, 26:27],
                scalar2=bstats[:, 30:31], op0=ALU.subtract, op1=ALU.mult),
                waits=[rstd2, l2s2] +
                      ([st_fr2[t - 1]] if t >= 1 and st_fr2[t - 1] else []), sig=sV, basic=True)
            st_p1[t] = p1
            p2 = P.add("vector", lambda q: q.tensor_tensor(
                out=h2b[:], in0=h2b[:], in1=g2bc[:], op=ALU.mult),
                waits=[p1, bc_all], sig=sV, basic=True)
            p3 = P.add("vector", lambda q: q.tensor_tensor(
                out=h2b[:], in0=h2b[:], in1=b2bc_s[:], op=ALU.add),
                waits=[p2], sig=sV, basic=True)
            q2m = P.add("vector", lambda q: q.tensor_reduce(
                bstats[:, 32:33], h2b[:], AX.X, ALU.max, apply_absolute_value=True),
                waits=[p3], sig=sV)
            q2s1 = P.add("vector", lambda q: q.tensor_scalar(
                out=bstats[:, 33:34], in0=bstats[:, 32:33], scalar1=consts[:, 0:1],
                scalar2=None, op0=ALU.max), waits=[q2m], sig=sV)
            q2s2 = P.add("vector", lambda q: q.reciprocal(
                bstats[:, 34:35], bstats[:, 33:34]), waits=[q2s1], sig=sV)
            q2s3 = P.add("vector", lambda q: q.tensor_scalar(
                out=bstats[:, 35:36], in0=bstats[:, 34:35], scalar1=consts[:, 6:7],
                scalar2=None, op0=ALU.mult), waits=[q2s2], sig=sV)
            f2a = P.add("vector", lambda q, pr=pr: q.tensor_scalar(
                out=fcol[:, 2 + pr:3 + pr], in0=bstats[:, 33:34], scalar1=1.0 / 127.0,
                scalar2=None, op0=ALU.mult), waits=[q2s1], sig=sV)
            f2e = P.add("vector", lambda q, pr=pr: q.tensor_tensor(
                out=fcol[:, 2 + pr:3 + pr], in0=fcol[:, 2 + pr:3 + pr],
                in1=mrep[:, 1:2], op=ALU.mult), waits=[f2a, mrep_evs[1]], sig=sV)
            st_f2e[t] = f2e
            qprev = None
            for cc in range(8):
                w0 = [q2s3] + ([qprev] if qprev else [])
                if cc == 0 and t >= 1:
                    w0.append(st_mm2[t - 1][1])
                qprev = uround_bs(h2b[:, 512 * cc:512 * cc + 512], bstats[:, 35:36],
                                  q2[:, 512 * cc:512 * cc + 512], w0)
            st_q2e[t] = qprev
            q2f_evs = []
            for g4 in range(8):
                tes = []
                for cc in range(4):
                    c = 4 * g4 + cc
                    te = P.add("tensor", lambda q, c=c, cc=cc: q.transpose(
                        ptr[:, 512 + 128 * cc:512 + 128 * cc + 128],
                        q2[:, 128 * c:128 * c + 128], identb[:]),
                        waits=[qprev, ldib] + (q2f_evs[-1:] if g4 > 0 else
                              ([st_q2c[t - 1][-1]] if t >= 1 else [])),
                        sig=sT)
                    tes.append(te)
                q2f_evs.append(P.add("scalar", lambda q, g4=g4: q.activation(
                    q2[:, 512 * g4:512 * g4 + 512], ptr[:, 512:1024], AF.Copy),
                    waits=tes, sig=sA))
            st_q2c[t] = q2f_evs

        def S4a(t):
            mm_last = {}
            for c in range(32):
                for nn in range(2):
                    def f(q, c=c, nn=nn):
                        return q.matmul(
                            ps[nn][:], q2[:, 128 * c:128 * c + 128],
                            W2t[c][:, 512 * nn:512 * nn + 512],
                            start=(c == 0), stop=(c == 31))
                    w = [st_q2c[t][c // 4]]
                    if c == 0:
                        if t == 0:
                            w.append(wt_evs[-1])
                        if pb_ps_free[nn] is not None:
                            w.append(pb_ps_free[nn])
                    ev = P.add("tensor", f, waits=w, sig=sT if c == 31 else None)
                    if c == 31:
                        mm_last[nn] = ev
            st_mm2[t] = mm_last

        def S4b(t):
            pr = t % 2
            ev1 = {}
            for nn in range(2):
                ev1[nn] = P.add("vector", lambda q, nn=nn, pr=pr:
                                q.scalar_tensor_tensor(
                                    out=xp[pr][:, 512 * nn:512 * nn + 512],
                                    in0=ps[nn][:], scalar=fcol[:, 2 + pr:3 + pr],
                                    in1=xp[pr][:, 512 * nn:512 * nn + 512],
                                    op0=ALU.mult, op1=ALU.add),
                                waits=[st_mm2[t][nn], st_f2e[t]], sig=sV)
                pb_ps_free[nn] = ev1[nn]
            frb = P.add("vector", lambda q, pr=pr: q.scalar_tensor_tensor(
                out=xp[pr][:], in0=xp[pr][:], scalar=1.0, in1=bb2[:],
                op0=ALU.mult, op1=ALU.add, accum_out=bstats[:, 40:41]),
                waits=[ev1[0], ev1[1], bc_all], sig=sV)
            fr2 = P.add("scalar", lambda q, pr=pr: q.activation(
                sqb[:, 0:1024], xp[pr][:], AF.Square, accum_out=bstats[:, 41:42]),
                waits=[frb], sig=sA)
            st_fr2[t] = fr2
            fr3 = P.add("vector", lambda q: q.tensor_scalar(
                out=bstats[:, 42:43], in0=bstats[:, 40:41], scalar1=consts[:, 4:5],
                scalar2=None, op0=ALU.mult), waits=[frb], sig=sV)
            fr4 = P.add("vector", lambda q: q.tensor_tensor(
                out=bstats[:, 43:44], in0=bstats[:, 42:43], in1=bstats[:, 42:43],
                op=ALU.mult), waits=[fr3], sig=sV)
            fr5 = P.add("vector", lambda q: q.tensor_scalar(
                out=bstats[:, 44:45], in0=bstats[:, 41:42], scalar1=consts[:, 4:5],
                scalar2=None, op0=ALU.mult), waits=[fr2], sig=sV)
            fr6 = P.add("vector", lambda q: q.tensor_tensor(
                out=bstats[:, 44:45], in0=bstats[:, 44:45], in1=bstats[:, 43:44],
                op=ALU.subtract), waits=[fr4, fr5], sig=sV)
            fr7 = P.add("vector", lambda q: q.tensor_scalar(
                out=bstats[:, 45:46], in0=bstats[:, 44:45], scalar1=consts[:, 0:1],
                scalar2=1e-6, op0=ALU.add, op1=ALU.max), waits=[fr6], sig=sV)
            rstdf = rsqrt_nr(45, 46, [fr7])
            fap = P.add("vector", lambda q, pr=pr: q.tensor_scalar(
                out=q1[pr][:], in0=xp[pr][:], scalar1=bstats[:, 42:43],
                scalar2=bstats[:, 46:47], op0=ALU.subtract, op1=ALU.mult),
                waits=[rstdf, fr2], sig=sV)
            st_fap[t] = fap
            out_st[t] = P.add("gpsimd", lambda q, t=t, pr=pr: q.dma_start(
                out=out_h[128 * t:128 * t + 128, :], in_=q1[pr][:]),
                waits=[fap], sig=sO, n=16)

        S1(0)
        MM1H0(0)
        for t in range(NT):
            S2(t)
            MM1H1(t)
            if t >= 1:
                S4b(t - 1)
            S2b(t)
            if t + 1 < NT:
                S1(t + 1)
                MM1H0(t + 1)
            S3(t)
            S4a(t)
        S4b(NT - 1)

        fin = P.cnt[id(sO)]
        P.add("sync", lambda q: q.wait_ge(sO, fin))
        for s_ in (sLS, sLA, sLG):
            v = P.cnt.get(id(s_), 0)
            if v:
                P.add("sync", lambda q, s_=s_, v=v: q.wait_ge(s_, v))
        P.emit(block)
    return nc


_NC_CACHE = {}


def kernel(L0_attn_x, L0_attn_wq, L0_attn_wk, L0_attn_wv, L0_attn_wi, L0_attn_bi,
           L0_attn_wf, L0_attn_bf, L0_attn_ln_w, L0_ffn_ln1_w, L0_ffn_ln1_b,
           L0_ffn_w1, L0_ffn_b1, L0_ffn_ln2_w, L0_ffn_ln2_b, L0_ffn_w2, L0_ffn_b2):
    if "nc" not in _NC_CACHE:
        _NC_CACHE["nc"] = build_program()
    nc = _NC_CACHE["nc"]
    x = np.ascontiguousarray(np.asarray(L0_attn_x, dtype=np.float32))
    com = {
        "wq": np.ascontiguousarray(np.asarray(L0_attn_wq, np.float32).reshape(256, 16)),
        "wk": np.ascontiguousarray(np.asarray(L0_attn_wk, np.float32).reshape(256, 16)),
        "wv": np.ascontiguousarray(np.asarray(L0_attn_wv, np.float32).reshape(256, 16)),
        "wi": np.ascontiguousarray(np.asarray(L0_attn_wi, np.float32)),
        "bi": np.ascontiguousarray(np.asarray(L0_attn_bi, np.float32).reshape(4, 1)),
        "wf": np.ascontiguousarray(np.asarray(L0_attn_wf, np.float32)),
        "bf": np.ascontiguousarray(np.asarray(L0_attn_bf, np.float32).reshape(4, 1)),
        "lnw": np.ascontiguousarray(np.asarray(L0_attn_ln_w, np.float32).reshape(1, D)),
        "l1w": np.ascontiguousarray(np.asarray(L0_ffn_ln1_w, np.float32).reshape(1, D)),
        "l1b": np.ascontiguousarray(np.asarray(L0_ffn_ln1_b, np.float32).reshape(1, D)),
        "w1": np.ascontiguousarray(np.asarray(L0_ffn_w1, np.float32)),
        "b1": np.ascontiguousarray(np.asarray(L0_ffn_b1, np.float32).reshape(1, 4096)),
        "l2w": np.ascontiguousarray(np.asarray(L0_ffn_ln2_w, np.float32).reshape(1, 4096)),
        "l2b": np.ascontiguousarray(np.asarray(L0_ffn_ln2_b, np.float32).reshape(1, 4096)),
        "w2": np.ascontiguousarray(np.asarray(L0_ffn_w2, np.float32)),
        "b2": np.ascontiguousarray(np.asarray(L0_ffn_b2, np.float32).reshape(1, D)),
    }
    in_maps = [dict(com, x=np.ascontiguousarray(x[b])) for b in range(8)]
    res = run_bass_kernel_spmd(nc, in_maps, core_ids=list(range(8)))
    out = np.stack([res.results[b]["out"] for b in range(8)], axis=0)
    return out


# version 18
# speedup vs baseline: 1.0664x; 1.0155x over previous
import numpy as np
import ml_dtypes
import concourse.bass as bass
import concourse.mybir as mybir
from concourse.bass_utils import run_bass_kernel_spmd

F32 = mybir.dt.float32
BF16 = mybir.dt.bfloat16
I32 = mybir.dt.int32
AF = mybir.ActivationFunctionType
ALU = mybir.AluOpType
AX = mybir.AxisListType

SEQ, D, NH = 4096, 1024, 4
QB = 512
NQB = SEQ // QB
NT = SEQ // 128
NGRP = SEQ // 1024

ENGINES = ("sync", "tensor", "vector", "scalar", "gpsimd")


class Prog:
    def __init__(self):
        self.steps = {e: [] for e in ENGINES}
        self.cnt = {}
        self.basic_ev = set()

    def add(self, eng, fn, waits=(), sig=None, n=1, basic=False):
        ev = None
        if sig is not None:
            c = self.cnt.get(id(sig), 0) + n
            self.cnt[id(sig)] = c
            ev = (sig, c)
            if basic:
                self.basic_ev.add((id(sig), c))
        self.steps[eng].append((fn, [w for w in waits if w is not None],
                                (sig, n) if sig is not None else None))
        return ev

    def barrier(self, sig, waits=()):
        return [self.add(e, lambda q: q.nop(), waits=waits, sig=sig)
                for e in ENGINES]

    def emit(self, block):
        def mk(eng_name):
            def body(q):
                maxw = {}
                own = {}
                el = getattr(self, "elide", {})
                for fn, waits, sig in self.steps[eng_name]:
                    for (sem, val) in waits:
                        if (el.get(id(sem)) == eng_name
                                and own.get(id(sem), 0) >= val
                                and (eng_name != "vector"
                                     or (id(sem), val) in self.basic_ev)):
                            continue
                        if maxw.get(id(sem), -1) >= val:
                            continue
                        q.wait_ge(sem, val)
                        maxw[id(sem)] = val
                    instr = fn(q)
                    if sig is not None:
                        if instr is None:
                            instr = q.nop()
                        instr.then_inc(sig[0], sig[1])
                        own[id(sig[0])] = own.get(id(sig[0]), 0) + sig[1]
            return body
        for e in ENGINES:
            getattr(block, e)(mk(e))


def build_program(stage=99, dbg=False, dwait=3, dmask=7):
    nc = bass.Bass("TRN2", target_bir_lowering=False, detect_race_conditions=False)

    x_h = nc.dram_tensor("x", [SEQ, D], F32, kind="ExternalInput").ap()
    wq_h = nc.dram_tensor("wq", [256, 16], F32, kind="ExternalInput").ap()
    wk_h = nc.dram_tensor("wk", [256, 16], F32, kind="ExternalInput").ap()
    wv_h = nc.dram_tensor("wv", [256, 16], F32, kind="ExternalInput").ap()
    wi_h = nc.dram_tensor("wi", [4, 3072], F32, kind="ExternalInput").ap()
    bi_h = nc.dram_tensor("bi", [4, 1], F32, kind="ExternalInput").ap()
    wf_h = nc.dram_tensor("wf", [4, 3072], F32, kind="ExternalInput").ap()
    bf_h = nc.dram_tensor("bf", [4, 1], F32, kind="ExternalInput").ap()
    lnw_h = nc.dram_tensor("lnw", [1, D], F32, kind="ExternalInput").ap()
    l1w_h = nc.dram_tensor("l1w", [1, D], F32, kind="ExternalInput").ap()
    l1b_h = nc.dram_tensor("l1b", [1, D], F32, kind="ExternalInput").ap()
    w1_h = nc.dram_tensor("w1", [4096, 1024], F32, kind="ExternalInput").ap()
    b1_h = nc.dram_tensor("b1", [1, 4096], F32, kind="ExternalInput").ap()
    l2w_h = nc.dram_tensor("l2w", [1, 4096], F32, kind="ExternalInput").ap()
    l2b_h = nc.dram_tensor("l2b", [1, 4096], F32, kind="ExternalInput").ap()
    w2_h = nc.dram_tensor("w2", [1024, 4096], F32, kind="ExternalInput").ap()
    b2_h = nc.dram_tensor("b2", [1, D], F32, kind="ExternalInput").ap()
    out_h = nc.dram_tensor("out", [SEQ, D], F32, kind="ExternalOutput").ap()

    KS = "ExternalOutput" if dbg else "Internal"
    xbf_scr = nc.dram_tensor("xbf_scr", [SEQ, D], BF16, kind=KS).ap()
    A_scr = nc.dram_tensor("A_scr", [NH, 256, 256], F32, kind=KS)
    Bv_scr = nc.dram_tensor("Bv_scr", [NH, 256, 256], F32, kind=KS)
    xpost_scr = nc.dram_tensor("xpost_scr", [SEQ, D], F32, kind=KS).ap()
    t1_scr = nc.dram_tensor("t1_scr", [4096, 1024], BF16, kind=KS).ap()
    t2_scr = nc.dram_tensor("t2_scr", [1024, 4096], BF16, kind=KS).ap()
    grow_scr = nc.dram_tensor("grow_scr", [1, 64], F32, kind=KS)
    msc_scr = nc.dram_tensor("msc_scr", [1, 4], F32, kind=KS)
    dbg_stt = nc.dram_tensor("dbg_stt", [4, 128, 64], F32, kind=KS)
    dbg_acc = nc.dram_tensor("dbg_acc", [4, 128, 257], F32, kind=KS)
    dbg_po = nc.dram_tensor("dbg_po", [4, 128, 257], F32, kind=KS)
    dbg_gcol = nc.dram_tensor("dbg_gcol", [128, 512], F32, kind=KS)
    dbg_rcol = nc.dram_tensor("dbg_rcol", [128, 64], F32, kind=KS)
    dbg_gw = nc.dram_tensor("dbg_gw", [4, 3 * SEQ], F32, kind=KS)
    dbg_hat = nc.dram_tensor("dbg_hat", [4, 128, 1024], F32, kind=KS)
    dbg_st2 = nc.dram_tensor("dbg_st2", [4, 4, 128, 64], F32, kind=KS)

    ident_bf_h = nc.inline_tensor(
        np.eye(128, dtype=np.float32).astype(ml_dtypes.bfloat16), name="identb_h").ap()
    ident_f_h = nc.inline_tensor(np.eye(128, dtype=np.float32), name="identf_h").ap()
    ones_row_h = nc.inline_tensor(np.ones((1, 128), np.float32), name="onesr_h").ap()
    ii = np.arange(128)[:, None]
    jj = np.arange(512)[None, :]
    masks_np = np.concatenate(
        [(jj - ii - 128 * r >= 0).astype(np.float32) for r in range(4)], axis=1)
    masks_h = nc.inline_tensor(masks_np, name="masks_h").ap()
    cv = np.zeros((128, 12), np.float32)
    cv[:, 0] = 1e-5
    cv[:, 1] = -0.5
    cv[:, 2] = 1e-8
    cv[:, 3] = 1.0 / 256
    cv[:, 4] = 1.0 / 1024
    cv[:, 5] = 1.0 / 4096
    cv[:, 6] = 127.0
    cv[:, 7] = 1.0
    cv[:, 8] = 1e-30
    cv[:, 9] = 0.5
    cv[:, 10] = -34.53877639491068
    cv[:, 11] = 1e-18
    consts_h = nc.inline_tensor(cv, name="consts_h").ap()

    ARENA_BYTES = 212096
    _arena = nc.alloc_sbuf_tensor("arena", [128, ARENA_BYTES], mybir.dt.uint8)
    BASE = nc.lookup_mloc(_arena).addr

    def T(name, shape, dt, offset):
        return nc.alloc_sbuf_tensor_at(name, shape, dt, offset=BASE + offset)
    consts = T("consts", [128, 12], F32, offset=0)
    dummy4 = T("dummy4", [128, 4], F32, offset=64)
    identb = T("identb", [128, 128], BF16, offset=96)
    identf = T("identf", [128, 128], F32, offset=352)
    onesr = T("onesr", [1, 128], F32, offset=864)
    bstats = T("bstats", [128, 64], F32, offset=1376)
    fcol = T("fcol", [128, 8], F32, offset=1632)
    mrep = T("mrep", [128, 2], F32, offset=1664)
    wq_sb = T("wq_sb", [128, 32], F32, offset=1696)
    wk_sb = T("wk_sb", [128, 32], F32, offset=1824)
    wv_sb = T("wv_sb", [128, 32], F32, offset=1952)
    a16 = T("a16", [128, 32], F32, offset=2080)
    bv16 = T("bv16", [128, 32], F32, offset=2208)
    wgf_all = T("wgf_all", [128, 64], BF16, offset=2336)
    biasg_i = T("biasg_i", [4, 1], F32, offset=2464)
    biasg_f = T("biasg_f", [4, 1], F32, offset=2496)
    A_sb = [[T(f"A_sb{h}_{c}", [128, 256], BF16, offset=2528 + (2 * h + c) * 512)
             for c in range(2)] for h in range(NH)]
    Bv_sb = [[T(f"Bv_sb{h}_{c}", [128, 256], BF16, offset=6624 + (2 * h + c) * 512)
              for c in range(2)] for h in range(NH)]
    gbc_mh = T("gbc_mh", [128, D], F32, offset=10720)
    masks = T("masks", [128, 2048], F32, offset=14816)
    gcol = T("gcol", [128, 512], F32, offset=23008)
    stt_sb = T("stt_sb", [128, 64], F32, offset=25120)
    g2bc = T("g2bc", [128, 4096], BF16, offset=1696)
    b2bc_s = T("b2bc_s", [128, 4096], BF16, offset=9888)
    g1bc = T("g1bc", [128, D], BF16, offset=18080)
    b1bc_s = T("b1bc_s", [128, D], BF16, offset=20128)
    bb2 = T("bb2", [128, D], BF16, offset=22176)
    XF = 25376
    x_fm = [T(f"x_fm{c}", [128, SEQ], BF16, offset=XF + c * 8192) for c in range(8)]
    vt = T("vt", [128, NT * 1028], BF16, offset=XF + 65536)
    W1t = [T(f"W1t{c}", [128, 4096], BF16, offset=XF + c * 8192) for c in range(8)]
    W2t = [T(f"W2t{c}", [128, 1024], BF16, offset=XF + 65536 + c * 2048)
           for c in range(32)]
    R = XF + 65536 + 65792
    gw = T("gw", [4, 3 * SEQ], F32, offset=R)
    wif = T("wif", [8, 3072], F32, offset=R)
    wgs = T("wgs", [8, 1024], F32, offset=R + 12288)
    lnw_sb = T("lnw_sb", [1, D], F32, offset=R + 16384)
    gsm = T("gsm", [4, 64], F32, offset=R + 49152)
    grow = T("grow", [1, 64], F32, offset=R + 49408)
    ones4k = T("ones4k", [4, 1024], F32, offset=R + 49536)
    rcol = T("rcol", [128, 32], F32, offset=R + 53632)
    grep = T("grep", [128, 32], F32, offset=R + 53760)
    zrow = T("zrow", [128, 512], F32, offset=R)
    qf = [[T(f"qf{h}_{c}", [128, QB], BF16, offset=R + (2 * h + c) * 1024)
           for c in range(2)] for h in range(NH)]
    s_sb = [T(f"s_sb{i}", [128, QB], BF16, offset=R + 8192 + i * 1024) for i in range(2)]
    acc = [T(f"acc{i}", [128, 257], F32, offset=R + 10240 + i * 1056) for i in range(4)]
    hatt = [T(f"hatt{i}", [128, D], F32, offset=R + 14464 + i * 4096) for i in range(4)]
    xin = [T(f"xin{i}", [128, D], F32, offset=R + 30848 + i * 4096) for i in range(2)]
    sqscr = T("sqscr", [128, D], F32, offset=R + 39040)
    wstage = [T(f"wstage{i}", [128, 1024], F32, offset=R + i * 4096) for i in range(2)]
    wtern = T("wtern", [128, 1024], BF16, offset=R + 8192)
    wint = T("wint", [128, 1024], I32, offset=R + 10240)
    wabs = T("wabs", [128, 1024], F32, offset=R + 10240)
    wsum = T("wsum", [128, 16], F32, offset=R + 43136)
    xp = [T(f"xp{i}", [128, D], F32, offset=R + i * 4096) for i in range(2)]
    q1 = [T(f"q1_{i}", [128, D], BF16, offset=R + 8192 + i * 2048) for i in range(2)]
    h1 = T("h1", [128, 4096], BF16, offset=R + 12288)
    q2 = T("q2", [128, 4096], BF16, offset=R + 20480)
    sqb = T("sqb", [128, 4096], BF16, offset=R + 28672)
    h2b = T("h2b", [128, 4096], BF16, offset=R + 28672)
    scr32 = T("scr32", [128, 1024], F32, offset=R + 36864)
    hln = T("hln", [128, D], F32, offset=R + 36864)
    ost = T("ost", [128, D], F32, offset=R + 36864)
    bb1 = T("bb1", [128, 4096], BF16, offset=R + 40960)
    scri = T("scri", [128, 512], I32, offset=R + 49152)
    yf_s = T("yf_s", [128, 512], F32, offset=R + 51200)
    adj_s = T("adj_s", [128, 512], F32, offset=R + 53248)
    cbf = T("cbf", [128, 512], BF16, offset=24224)
    vecs = T("vecs", [1, 4096], F32, offset=R + 12288)
    assert BASE + ARENA_BYTES <= 229344 - 500, BASE

    P = Prog()
    from contextlib import ExitStack
    with ExitStack() as _stk:
        sL = _stk.enter_context(nc.semaphore("sL"))
        sLS = _stk.enter_context(nc.semaphore("sLS"))
        sLA = _stk.enter_context(nc.semaphore("sLA"))
        sLG = _stk.enter_context(nc.semaphore("sLG"))
        sT = _stk.enter_context(nc.semaphore("sT"))
        sV = _stk.enter_context(nc.semaphore("sV"))
        sA = _stk.enter_context(nc.semaphore("sA"))
        sG = _stk.enter_context(nc.semaphore("sG"))
        sO = _stk.enter_context(nc.semaphore("sO"))
        sW = _stk.enter_context(nc.semaphore("sW"))
        import os as _os
        _emask = int(_os.environ.get("KELIDE", "15"))
        P.elide = {}
        for _bit, (_sem, _eng) in enumerate([(sV, "vector"), (sT, "tensor"),
                                             (sA, "scalar"), (sG, "gpsimd")]):
            if _emask & (1 << _bit):
                P.elide[id(_sem)] = _eng
        pq = _stk.enter_context(nc.psum_tensor("pq", [128, 512], F32))
        ps0 = _stk.enter_context(nc.psum_tensor("ps0", [128, 512], F32))
        ps1 = _stk.enter_context(nc.psum_tensor("ps1", [128, 512], F32))
        po0 = _stk.enter_context(nc.psum_tensor("po0", [128, 512], F32))
        po1 = _stk.enter_context(nc.psum_tensor("po1", [128, 512], F32))
        po2 = _stk.enter_context(nc.psum_tensor("po2", [128, 512], F32))
        po3 = _stk.enter_context(nc.psum_tensor("po3", [128, 512], F32))
        ptr = _stk.enter_context(nc.psum_tensor("ptr", [128, 1024], BF16))
        block = _stk.enter_context(nc.Block())
        ps = [ps0, ps1]
        po = [po0, po1, po2, po3]

        _dma_sem = {"sync": sLS, "scalar": sLA, "gpsimd": sLG}
        _orig_add = P.add

        def _add(eng, fn, waits=(), sig=None, n=1, basic=False):
            if sig is sL:
                sig = _dma_sem[eng]
            return _orig_add(eng, fn, waits, sig, n, basic)
        P.add = _add

        def _cut():
            for s_ in (sLS, sLA, sLG, sT, sV, sA, sG, sO, sW):
                v = P.cnt.get(id(s_), 0)
                if v:
                    P.add("sync", lambda q, s_=s_, v=v: q.wait_ge(s_, v))
            P.emit(block)
            return True

        def uround(src_ap, scale_ap, dst_ap, waits, clamp=False):
            w = src_ap.shape[1]
            e_1 = P.add("vector", lambda q: q.tensor_scalar(
                out=yf_s[:, 0:w], in0=src_ap, scalar1=scale_ap,
                scalar2=1023.5, op0=ALU.mult, op1=ALU.add), waits=waits, sig=sV, basic=True)
            e_2 = P.add("vector", lambda q: q.tensor_scalar(
                out=scri[:, 0:w], in0=yf_s[:, 0:w],
                scalar1=0.5, scalar2=None, op0=ALU.add), waits=[e_1], sig=sV)
            e_3 = P.add("vector", lambda q: q.tensor_tensor(
                out=adj_s[:, 0:w], in0=yf_s[:, 0:w],
                in1=scri[:, 0:w], op=ALU.is_ge), waits=[e_2], sig=sV, basic=True)
            e_4 = P.add("vector", lambda q: q.scalar_tensor_tensor(
                out=dst_ap, in0=scri[:, 0:w], scalar=-1024.0,
                in1=adj_s[:, 0:w], op0=ALU.add, op1=ALU.add),
                waits=[e_3], sig=sV, basic=True)
            if clamp:
                e_4 = P.add("vector", lambda q: q.tensor_scalar(
                    out=dst_ap, in0=dst_ap, scalar1=-1.0, scalar2=1.0,
                    op0=ALU.max, op1=ALU.min), waits=[e_4], sig=sV, basic=True)
            return e_4

        def uround_bs(src_ap, scale_ap, dst_ap, waits):
            w = src_ap.shape[1]
            e_1 = P.add("vector", lambda q: q.tensor_scalar(
                out=yf_s[:, 0:w], in0=src_ap, scalar1=scale_ap,
                scalar2=1024.5, op0=ALU.mult, op1=ALU.add), waits=waits, sig=sV, basic=True)
            e_2 = P.add("vector", lambda q: q.tensor_scalar(
                out=scri[:, 0:w], in0=yf_s[:, 0:w],
                scalar1=1.0, scalar2=None, op0=ALU.mult), waits=[e_1], sig=sV)
            e_3 = P.add("vector", lambda q: q.scalar_tensor_tensor(
                out=cbf[:, 0:w], in0=scri[:, 0:w], scalar=-1.0,
                in1=yf_s[:, 0:w], op0=ALU.mult, op1=ALU.add), waits=[e_2], sig=sV, basic=True)
            ev = P.add("vector", lambda q: q.tensor_scalar(
                out=dst_ap, in0=cbf[:, 0:w], scalar1=1.0,
                scalar2=16.0, op0=ALU.mult, op1=ALU.add), waits=[e_3], sig=sV, basic=True)
            for B in (16.0, 8.0, 4.0, 2.0, 1.0):
                ea = P.add("vector", lambda q, B=B: q.tensor_scalar(
                    out=cbf[:, 0:w], in0=dst_ap, scalar1=B, scalar2=B,
                    op0=ALU.is_ge, op1=ALU.mult), waits=[ev], sig=sV, basic=True)
                ev = P.add("vector", lambda q: q.tensor_tensor(
                    out=dst_ap, in0=dst_ap, in1=cbf[:, 0:w], op=ALU.subtract),
                    waits=[ea], sig=sV, basic=True)
            ev = P.add("vector", lambda q: q.scalar_tensor_tensor(
                out=dst_ap, in0=yf_s[:, 0:w], scalar=-1024.0,
                in1=dst_ap, op0=ALU.add, op1=ALU.subtract), waits=[ev], sig=sV, basic=True)
            return ev

        ldc = P.add("sync", lambda q: q.dma_start(out=consts[:], in_=consts_h), sig=sL, n=16)
        ldib = P.add("sync", lambda q: q.dma_start(out=identb[:], in_=ident_bf_h), sig=sL, n=16)
        ldif = P.add("sync", lambda q: q.dma_start(out=identf[:], in_=ident_f_h), sig=sL, n=16)
        ldor = P.add("sync", lambda q: q.dma_start(out=onesr[:], in_=ones_row_h), sig=sL, n=16)
        ldmk = P.add("sync", lambda q: q.dma_start(out=masks[:], in_=masks_h), sig=sL, n=16)
        for cc in range(2):
            ldwq = P.add("sync", lambda q, cc=cc: q.dma_start(
                out=wq_sb[:, 16 * cc:16 * cc + 16],
                in_=wq_h[128 * cc:128 * cc + 128, :]), sig=sL, n=16)
            ldwk = P.add("sync", lambda q, cc=cc: q.dma_start(
                out=wk_sb[:, 16 * cc:16 * cc + 16],
                in_=wk_h[128 * cc:128 * cc + 128, :]), sig=sL, n=16)
            ldwv = P.add("sync", lambda q, cc=cc: q.dma_start(
                out=wv_sb[:, 16 * cc:16 * cc + 16],
                in_=wv_h[128 * cc:128 * cc + 128, :]), sig=sL, n=16)
        ldwi = P.add("sync", lambda q: q.dma_start(out=wif[0:4, :], in_=wi_h), sig=sL, n=16)
        ldwf = P.add("sync", lambda q: q.dma_start(out=wif[4:8, :], in_=wf_h), sig=sL, n=16)
        ldbi = P.add("sync", lambda q: q.dma_start(out=biasg_i[:], in_=bi_h), sig=sL, n=16)
        ldbf = P.add("sync", lambda q: q.dma_start(out=biasg_f[:], in_=bf_h), sig=sL, n=16)
        ldln = None

        if stage <= 11:
            if _cut():
                return nc
        ldx = [None] * NT
        stxbf = [None] * NT
        for t in range(NT):
            ldx[t] = P.add("sync", lambda q, t=t: q.dma_start(
                out=xin[t % 2][:], in_=x_h[t * 128:(t + 1) * 128, :]),
                waits=[stxbf[t - 2]] if t >= 2 else [], sig=sL, n=16)
            stxbf[t] = P.add("gpsimd", lambda q, t=t: q.dma_start(
                out=xbf_scr[t * 128:(t + 1) * 128, :], in_=xin[t % 2][:]),
                waits=[ldx[t]], sig=sL, n=16)

        if stage <= 12:
            if _cut():
                return nc
        a16_evs = []
        for c in range(2):
            for i in range(4):
                for i2 in range(4):
                    def f(q, c=c, i=i, i2=i2):
                        return q.tensor_tensor(
                            out=dummy4[:], in0=wq_sb[:, 16 * c + i:16 * c + i + 13:4],
                            in1=wk_sb[:, 16 * c + i2:16 * c + i2 + 13:4], op=ALU.mult)
                    ev = P.add("vector", f, waits=[ldwq, ldwk], sig=sV)
                    def f2(q, c=c, i=i, i2=i2):
                        col = 16 * c + 4 * i + i2
                        return q.reduce_sum(a16[:, col:col + 1], dummy4[:], AX.X)
                    a16_evs.append(P.add("vector", f2, waits=[ev], sig=sV))
        a16_evs.append(P.add("vector", lambda q: q.tensor_scalar(
            out=a16[:], in0=a16[:], scalar1=1.0 / 16.0, scalar2=None, op0=ALU.mult),
            waits=[a16_evs[-1]], sig=sV))
        bv_evs = []
        for c in range(2):
            for i in range(4):
                for o in range(4):
                    def f(q, c=c, i=i, o=o):
                        return q.tensor_copy(
                            bv16[:, 16 * c + 4 * i + o:16 * c + 4 * i + o + 1],
                            wv_sb[:, 16 * c + 4 * o + i:16 * c + 4 * o + i + 1])
                    bv_evs.append(P.add("vector", f, waits=[ldwv], sig=sV))

        if stage <= 13:
            if _cut():
                return nc
        zm = P.add("gpsimd", lambda q: q.memset(zrow[:], 0.0), sig=sG)
        zevs = []
        for h in range(NH):
            for dst in (A_scr, Bv_scr):
                def fz(q, h=h, dst=dst):
                    return q.dma_start(
                        out=bass.AP(dst, h * 65536, [[512, 128], [1, 512]]),
                        in_=zrow[:])
                zevs.append(P.add("gpsimd", fz, waits=[zm], sig=sL, n=16))
        scat = []
        for h in range(NH):
            c, poff = h // 2, (h % 2) * 64
            def fa(q, h=h, c=c, poff=poff):
                return q.dma_start(
                    out=bass.AP(A_scr, h * 65536, [[1028, 64], [256, 4], [1, 4]]),
                    in_=a16[poff:poff + 64, 16 * c:16 * c + 16])
            def fb(q, h=h, c=c, poff=poff):
                return q.dma_start(
                    out=bass.AP(Bv_scr, h * 65536, [[1028, 64], [256, 4], [1, 4]]),
                    in_=bv16[poff:poff + 64, 16 * c:16 * c + 16])
            scat.append(P.add("gpsimd", fa, waits=[a16_evs[-1], zevs[-1]], sig=sL, n=16))
            scat.append(P.add("gpsimd", fb, waits=[bv_evs[-1], zevs[-1]], sig=sL, n=16))
        if stage <= 14:
            if _cut():
                return nc
        ldA = []
        for h in range(NH):
            for c in range(2):
                ldA.append(P.add("gpsimd", lambda q, h=h, c=c: q.dma_start(
                    out=A_sb[h][c][:], in_=A_scr.ap()[h, 128 * c:128 * c + 128, :]),
                    waits=[scat[-1]], sig=sL, n=16))
                ldA.append(P.add("gpsimd", lambda q, h=h, c=c: q.dma_start(
                    out=Bv_sb[h][c][:], in_=Bv_scr.ap()[h, 128 * c:128 * c + 128, :]),
                    waits=[scat[-1]], sig=sL, n=16))

        if stage <= 15:
            if _cut():
                return nc
        g1 = P.add("vector", lambda q: q.tensor_tensor(
            out=wgs[:], in0=wif[:, 0:1024], in1=wif[:, 1024:2048], op=ALU.add),
            waits=[ldwi, ldwf], sig=sV)
        g2 = P.add("vector", lambda q: q.tensor_tensor(
            out=wgs[:], in0=wgs[:], in1=wif[:, 2048:3072], op=ALU.add),
            waits=[g1], sig=sV)
        if stage <= 17:
            if _cut():
                return nc
        wg_evs = []
        prev = None
        for c in range(8):
            tev = P.add("tensor", lambda q, c=c: q.transpose(
                pq[:, 0:8], wgs[:, 128 * c:128 * c + 128], identf[0:8, 0:8]),
                waits=[g2, ldif] + ([prev] if prev else []), sig=sT)
            prev = P.add("vector", lambda q, c=c: q.tensor_copy(
                wgf_all[:, 8 * c:8 * c + 8], pq[:, 0:8]), waits=[tev], sig=sV)
            wg_evs.append(prev)

        if stage <= 18:
            if _cut():
                return nc
        gmh_ld = P.add("sync", lambda q: q.dma_start(
            out=gbc_mh[:], in_=lnw_h[0:1, :].to_broadcast((128, D))), sig=sL, n=16)
        gmh = [P.add("vector", lambda q: q.tensor_scalar(
            out=gbc_mh[:], in0=gbc_mh[:], scalar1=1.0, scalar2=None, op0=ALU.add),
            waits=[gmh_ld], sig=sV)]

        if stage <= 16:
            if _cut():
                return nc
        xfm_evs = []
        for c in range(8):
            xfm_evs.append(P.add("sync", lambda q, c=c: q.dma_start_transpose(
                x_fm[c][:], xbf_scr[:, 128 * c:128 * c + 128]),
                waits=[stxbf[NT - 1]], sig=sL, n=16))

        if stage <= 21:
            if _cut():
                return nc
        gate_ev = []
        for blk in range(8):
            last_f = None
            for c in range(8):
                def fi(q, blk=blk, c=c):
                    return q.matmul(pq[0:4, :], wgf_all[:, 8 * c:8 * c + 4],
                                    x_fm[c][:, 512 * blk:512 * blk + 512],
                                    start=(c == 0), stop=(c == 7))
                def ff(q, blk=blk, c=c):
                    return q.matmul(po1[0:4, :], wgf_all[:, 8 * c + 4:8 * c + 8],
                                    x_fm[c][:, 512 * blk:512 * blk + 512],
                                    start=(c == 0), stop=(c == 7))
                w = [xfm_evs[-1], wg_evs[-1], gmh[-1], ldA[-1]] if c == 0 else []
                if c == 0 and gate_ev:
                    w.append(gate_ev[-1])
                P.add("tensor", fi, waits=w, sig=None)
                last_f = P.add("tensor", ff, sig=sT if c == 7 else None)
            gate_ev.append(P.add("vector", lambda q, blk=blk: q.tensor_copy(
                gw[0:4, 512 * blk:512 * blk + 512], pq[0:4, :]),
                waits=[last_f] + ([scat[-1]] if blk == 0 else []), sig=sV))
            gate_ev.append(P.add("vector", lambda q, blk=blk: q.tensor_copy(
                gw[0:4, SEQ + 512 * blk:SEQ + 512 * blk + 512], po1[0:4, :]),
                waits=[last_f], sig=sV))
        gb1 = P.add("vector", lambda q: q.tensor_scalar(
            out=gw[0:4, 0:SEQ], in0=gw[0:4, 0:SEQ], scalar1=biasg_i[:],
            scalar2=None, op0=ALU.add), waits=[gate_ev[-1], ldbi], sig=sV)
        gb2 = P.add("vector", lambda q: q.tensor_scalar(
            out=gw[0:4, SEQ:2 * SEQ], in0=gw[0:4, SEQ:2 * SEQ], scalar1=biasg_f[:],
            scalar2=None, op0=ALU.add), waits=[gate_ev[-1], ldbf], sig=sV)

        on4 = P.add("gpsimd", lambda q: q.memset(ones4k[:], 1.0),
                    waits=[zevs[-1]], sig=sG)
        sp1 = P.add("scalar", lambda q: q.activation(
            gw[0:4, SEQ:2 * SEQ], gw[0:4, SEQ:2 * SEQ], AF.Exp, scale=-1.0),
            waits=[gb2], sig=sA)
        sp2 = P.add("scalar", lambda q: q.activation(
            gw[0:4, SEQ:2 * SEQ], gw[0:4, SEQ:2 * SEQ], AF.Ln, bias=1.0),
            waits=[sp1], sig=sA)
        ev = None
        for ck in range(4):
            ev = P.add("vector", lambda q, ck=ck: q.tensor_tensor_scan(
                out=gw[0:4, 2 * SEQ + 1024 * ck:2 * SEQ + 1024 * ck + 1024],
                data0=ones4k[:], data1=gw[0:4, SEQ + 1024 * ck:SEQ + 1024 * ck + 1024],
                initial=0.0, op0=ALU.mult, op1=ALU.add),
                waits=[sp2, on4] if ck == 0 else [ev], sig=sV)
            if ck > 0:
                ev = P.add("vector", lambda q, ck=ck: q.tensor_scalar(
                    out=gw[0:4, 2 * SEQ + 1024 * ck:2 * SEQ + 1024 * ck + 1024],
                    in0=gw[0:4, 2 * SEQ + 1024 * ck:2 * SEQ + 1024 * ck + 1024],
                    scalar1=gw[0:4, 2 * SEQ + 1024 * ck - 1:2 * SEQ + 1024 * ck],
                    scalar2=None, op0=ALU.add), waits=[ev], sig=sV)
        cs_ev = ev
        u_ev = P.add("vector", lambda q: q.tensor_tensor(
            out=gw[0:4, 0:SEQ], in0=gw[0:4, 0:SEQ], in1=gw[0:4, 2 * SEQ:3 * SEQ],
            op=ALU.add), waits=[cs_ev, gb1], sig=sV)
        ev = None
        for ck in range(4):
            ev = P.add("vector", lambda q, ck=ck: q.tensor_tensor_scan(
                out=gw[0:4, SEQ + 1024 * ck:SEQ + 1024 * ck + 1024],
                data0=ones4k[:], data1=gw[0:4, 1024 * ck:1024 * ck + 1024],
                initial=-3.0e38, op0=ALU.mult, op1=ALU.max),
                waits=[u_ev, on4] if ck == 0 else [ev], sig=sV)
            if ck > 0:
                ev = P.add("vector", lambda q, ck=ck: q.tensor_scalar(
                    out=gw[0:4, SEQ + 1024 * ck:SEQ + 1024 * ck + 1024],
                    in0=gw[0:4, SEQ + 1024 * ck:SEQ + 1024 * ck + 1024],
                    scalar1=gw[0:4, SEQ + 1024 * ck - 1:SEQ + 1024 * ck],
                    scalar2=None, op0=ALU.max), waits=[ev], sig=sV)
        m_ev = ev
        rcop = []
        for g in range(8):
            rcop.append(P.add("vector", lambda q, g=g: q.tensor_copy(
                gsm[0:4, 8 + g:9 + g],
                gw[0:4, SEQ + 512 * g + 511:SEQ + 512 * g + 512]),
                waits=[m_ev], sig=sV))
        eu_evs = []
        for g in range(8):
            eu_evs.append(P.add("vector", lambda q, g=g: q.tensor_scalar(
                out=gw[0:4, 512 * g:512 * g + 512],
                in0=gw[0:4, 512 * g:512 * g + 512],
                scalar1=gsm[0:4, 8 + g:9 + g], scalar2=None, op0=ALU.subtract),
                waits=[rcop[g]] + rcop, sig=sV))
        eu2 = P.add("scalar", lambda q: q.activation(
            gw[0:4, 0:SEQ], gw[0:4, 0:SEQ], AF.Exp), waits=eu_evs, sig=sA)
        rflat = None
        for h_ in range(4):
            rflat = P.add("gpsimd", lambda q, h_=h_: q.dma_start(
                out=grow[0:1, 8 * h_:8 * h_ + 8], in_=gsm[h_:h_ + 1, 8:16]),
                waits=rcop, sig=sL, n=16)
        gd_evs = []
        for g in range(8):
            gp = max(g - 1, 0)
            gd_evs.append(P.add("vector", lambda q, g=g, gp=gp: q.tensor_tensor(
                out=gsm[0:4, 16 + g:17 + g], in0=gsm[0:4, 8 + gp:9 + gp],
                in1=gsm[0:4, 8 + g:9 + g], op=ALU.subtract),
                waits=[rcop[g], rcop[gp]], sig=sV))
        gdflat = None
        for h_ in range(4):
            gdflat = P.add("gpsimd", lambda q, h_=h_: q.dma_start(
                out=grow[0:1, 32 + 8 * h_:32 + 8 * h_ + 8],
                in_=gsm[h_:h_ + 1, 16:24]),
                waits=gd_evs + [rflat], sig=sL, n=16)
        gdexp = P.add("scalar", lambda q: q.activation(
            grow[:, 32:64], grow[:, 32:64], AF.Exp), waits=[gdflat], sig=sA)
        st_grow = P.add("gpsimd", lambda q: q.dma_start(
            out=grow_scr.ap(), in_=grow[:]), waits=[gdexp, rflat], sig=sL, n=16)
        rcol_ev = P.add("sync", lambda q: q.dma_start(
            out=rcol[:], in_=grow_scr.ap()[0:1, 0:32].to_broadcast((128, 32))),
            waits=[st_grow], sig=sL, n=16)
        grep_ev = P.add("sync", lambda q: q.dma_start(
            out=grep[:], in_=grow_scr.ap()[0:1, 32:64].to_broadcast((128, 32))),
            waits=[st_grow], sig=sL, n=16)
        grp_evs = [grep_ev] * 8

        if stage <= 22:
            if _cut():
                return nc
        gcol_evs = [None] * NT
        prevc = [None, None]
        for t in range(NT):
            bank = pq if t % 2 == 0 else po1
            tlast = None
            for qi_ in range(3):
                tlast = P.add("tensor", lambda q, t=t, bank=bank, qi_=qi_: q.transpose(
                    bank[:, 4 * qi_:4 * qi_ + 4],
                    gw[0:4, SEQ * qi_ + 128 * t:SEQ * qi_ + 128 * t + 128],
                    identf[0:4, 0:4]),
                    waits=([eu2, rcol_ev, grep_ev] if qi_ == 0 else []) +
                          ([prevc[t % 2]] if (qi_ == 0 and prevc[t % 2]) else []),
                    sig=sT if qi_ == 2 else None)
            gcol_evs[t] = P.add("vector", lambda q, t=t, bank=bank: q.tensor_copy(
                gcol[:, 16 * t:16 * t + 12], bank[:, 0:12]), waits=[tlast], sig=sV)
            prevc[t % 2] = gcol_evs[t]

        if stage <= 23:
            if _cut():
                return nc
        if dbg and (dmask & 4):
            P.add("sync", lambda q: q.dma_start(out=dbg_gcol.ap(), in_=gcol[:]),
                  waits=[gcol_evs[NT - 1], gcol_evs[NT - 2]], sig=sL, n=16)
            P.add("sync", lambda q: q.dma_start(out=dbg_rcol.ap()[:, 0:32], in_=rcol[:]),
                  waits=[rcol_ev], sig=sL, n=16)
            P.add("sync", lambda q: q.dma_start(out=dbg_rcol.ap()[:, 32:64], in_=grep[:]),
                  waits=[grep_ev], sig=sL, n=16)
            P.add("sync", lambda q: q.dma_start(out=dbg_gw.ap(), in_=gw[:]),
                  waits=[eu2], sig=sL, n=16)
        bank_free = {0: gcol_evs[NT - 2], 1: gcol_evs[NT - 1]}
        vt_last = None
        for t in range(NT):
            for pair in range(2):
                bank = pq if pair == 0 else po1
                last = None
                for hh in range(2):
                    h = 2 * pair + hh
                    for c in range(2):
                        def f(q, h=h, c=c, hh=hh, t=t, bank=bank):
                            return q.matmul(
                                bank[:, 256 * hh:256 * hh + 256],
                                x_fm[2 * h + c][:, 128 * t:128 * t + 128],
                                Bv_sb[h][c][:], start=(c == 0), stop=(c == 1))
                        w = [ldA[-1], xfm_evs[-1], bank_free[pair]] \
                            if (hh == 0 and c == 0) else []
                        last = P.add("tensor", f, waits=w,
                                     sig=sT if (hh == 1 and c == 1) else None)
                ev_sc = None
                for hh in range(2):
                    h = 2 * pair + hh
                    ev_sc = P.add("vector", lambda q, h=h, hh=hh, t=t, bank=bank:
                                  q.tensor_scalar(
                                      out=vt[:, 1028 * t + 257 * h:1028 * t + 257 * h + 256],
                                      in0=bank[:, 256 * hh:256 * hh + 256],
                                      scalar1=gcol[:, 16 * t + h:16 * t + h + 1],
                                      scalar2=None, op0=ALU.mult),
                                  waits=[last, gcol_evs[t]], sig=sV)
                    P.add("vector", lambda q, h=h, t=t: q.tensor_copy(
                        vt[:, 1028 * t + 257 * h + 256:1028 * t + 257 * h + 257],
                        gcol[:, 16 * t + h:16 * t + h + 1]),
                        waits=[gcol_evs[t]], sig=sV)
                bank_free[pair] = ev_sc
                vt_last = ev_sc
        vt_all = vt_last

        if stage <= 24:
            if _cut():
                return nc
        xpost_st = [None] * NT
        ps_free = [None, None]
        po_free = [None] * 4
        q_bank_free = {0: bank_free[0], 1: bank_free[1]}
        q_bank_free = {0: bank_free[0], 1: None}
        sct = 0
        for qb in range(NQB):
            qf_evs = []
            rnd = 0
            for h in range(NH):
                for m in range(2):
                    bank = pq if rnd % 2 == 0 else po0
                    bsel = rnd % 2
                    last = None
                    for c in range(2):
                        def f(q, h=h, m=m, c=c, qb=qb, bank=bank):
                            return q.matmul(
                                bank[:], A_sb[h][c][:, 128 * m:128 * m + 128],
                                x_fm[2 * h + c][:, QB * qb:QB * qb + QB],
                                start=(c == 0), stop=(c == 1))
                        w = []
                        if c == 0:
                            if qb == 0 and rnd == 0:
                                w.append(vt_all)
                            if q_bank_free[bsel] is not None:
                                w.append(q_bank_free[bsel])
                            if bsel == 1 and po_free[0] is not None:
                                w.append(po_free[0])
                        last = P.add("tensor", f, waits=w, sig=sT if c == 1 else None)
                    ev = P.add("vector", lambda q, h=h, m=m, bank=bank: q.tensor_copy(
                        qf[h][m][:], bank[:]), waits=[last], sig=sV)
                    q_bank_free[bsel] = ev
                    qf_evs.append(ev)
                    rnd += 1
            qf_ready = qf_evs[-1]
            if dbg and (dmask & 8) and qb == 0:
                P.add("gpsimd", lambda q: q.dma_start(
                    out=dbg_po.ap()[0][:, 0:257],
                    in_=vt[:, 0:257]), waits=[vt_all], sig=sL, n=16)
                P.add("gpsimd", lambda q: q.dma_start(
                    out=dbg_po.ap()[1][:, 0:256],
                    in_=qf[0][0][:, 0:256]), waits=[qf_ready], sig=sL, n=16)
            po_free[0] = q_bank_free[1]

            acc_ev = {h: [None] * 4 for h in range(NH)}

            def emit_out(job, sev, sb):
                h, gi, jt, s_, e_ = job
                grp_g = jt // 4
                last_out = None
                for ic in range(4):
                    def f(q, sb=sb, ic=ic, h=h, jt=jt, s=s_, e=e_):
                        return q.matmul(
                            po[ic][:, 0:257],
                            s_sb[sb][:, 128 * ic:128 * ic + 128],
                            vt[:, 1028 * jt + 257 * h:1028 * jt + 257 * h + 257],
                            start=s, stop=e)
                    w = [sev]
                    if s_ and po_free[ic] is not None:
                        w.append(po_free[ic])
                    last_out = P.add("tensor", f, waits=w,
                                     sig=sT if ic == 3 else None)
                if not e_:
                    return last_out
                for ic in range(4):
                    if gi == 0:
                        ev = P.add("vector", lambda q, ic=ic: q.tensor_copy(
                            acc[ic][:], po[ic][:, 0:257]), waits=[last_out], sig=sV)
                    else:
                        def f2(q, ic=ic, h=h, grp_g=grp_g):
                            return q.scalar_tensor_tensor(
                                out=acc[ic][:], in0=acc[ic][:],
                                scalar=grep[:, 8 * h + grp_g:8 * h + grp_g + 1],
                                in1=po[ic][:, 0:257], op0=ALU.mult, op1=ALU.add)
                        ev = P.add("vector", f2,
                                   waits=[last_out, grp_evs[grp_g], acc_ev[h][ic]],
                                   sig=sV)
                    po_free[ic] = ev
                    acc_ev[h][ic] = ev
                return last_out

            for h in range(NH):
                n_jt = 4 * (qb + 1)
                jobs = [(h, jt // 4, jt, jt % 4 == 0, jt % 4 == 3)
                        for jt in range(n_jt)]
                pend = None
                for job in jobs + [None]:
                    if job is not None:
                        _h, gi, jt, s_, e_ = job
                        sb = sct % 2
                        last = None
                        for c in range(2):
                            def f(q, h=h, c=c, jt=jt, sb=sb):
                                return q.matmul(
                                    ps[sb][:], x_fm[2 * h + c][:, 128 * jt:128 * jt + 128],
                                    qf[h][c][:], start=(c == 0), stop=(c == 1))
                            w = [qf_ready] if c == 0 else []
                            if c == 0 and ps_free[sb] is not None:
                                w.append(ps_free[sb])
                            last = P.add("tensor", f, waits=w,
                                         sig=sT if c == 1 else None)
                    if pend is not None:
                        ps_free[pend[2]] = emit_out(pend[0], pend[1], pend[2])
                    if job is not None:
                        diag = (jt >= 4 * qb)
                        if diag:
                            r = jt - 4 * qb
                            sev = P.add("vector", lambda q, sb=sb, r=r: q.tensor_tensor(
                                out=s_sb[sb][:], in0=ps[sb][:],
                                in1=masks[:, 512 * r:512 * r + 512], op=ALU.mult),
                                waits=[last, ldmk], sig=sV)
                        else:
                            sev = P.add("scalar", lambda q, sb=sb: q.copy(
                                s_sb[sb][:], ps[sb][:]), waits=[last], sig=sA)
                        pend = (job, sev, sb)
                        sct += 1
                    else:
                        pend = None

                ep = []
                for ic in range(4):
                    tg = 4 * qb + ic
                    ep.append(P.add("vector", lambda q, ic=ic: q.reduce_sum(
                        stt_sb[:, ic:ic + 1], acc[ic][:, 0:256], AX.X),
                        waits=[acc_ev[h][ic]], sig=sV))
                    ep.append(P.add("scalar", lambda q, ic=ic: q.activation(
                        sqscr[:, 0:256], acc[ic][:, 0:256], AF.Square,
                        accum_out=stt_sb[:, 4 + ic:5 + ic]),
                        waits=[acc_ev[h][ic]], sig=sA))
                    ep.append(P.add("vector", lambda q, ic=ic: q.tensor_copy(
                        stt_sb[:, 8 + ic:9 + ic], acc[ic][:, 256:257]),
                        waits=[acc_ev[h][ic]], sig=sV))
                    gq = qb
                    ep.append(P.add("vector", lambda q, ic=ic, tg=tg, h=h, gq=gq:
                              q.tensor_tensor(
                                  out=stt_sb[:, 12 + ic:13 + ic],
                                  in0=rcol[:, 8 * h + gq:8 * h + gq + 1],
                                  in1=gcol[:, 16 * tg + 4 + h:16 * tg + 5 + h],
                                  op=ALU.subtract),
                              waits=[gcol_evs[tg], rcol_ev], sig=sV))
                    ep.append(P.add("vector", lambda q, ic=ic, tg=tg, h=h: q.tensor_tensor(
                        out=stt_sb[:, 16 + ic:17 + ic],
                        in0=gcol[:, 16 * tg + 8 + h:16 * tg + 9 + h],
                        in1=gcol[:, 16 * tg + 4 + h:16 * tg + 5 + h], op=ALU.subtract),
                        waits=[gcol_evs[tg]], sig=sV))
                e1 = P.add("scalar", lambda q: q.activation(
                    stt_sb[:, 20:28], stt_sb[:, 12:20], AF.Exp), waits=ep, sig=sA)
                e2 = P.add("scalar", lambda q: q.activation(
                    stt_sb[:, 8:12], stt_sb[:, 8:12], AF.Abs), waits=ep, sig=sA)
                e3 = P.add("vector", lambda q: q.tensor_tensor(
                    out=stt_sb[:, 40:44], in0=stt_sb[:, 20:24], in1=stt_sb[:, 8:12],
                    op=ALU.mult), waits=[e1, e2], sig=sV)
                e4 = P.add("vector", lambda q: q.tensor_tensor(
                    out=stt_sb[:, 40:44], in0=stt_sb[:, 40:44], in1=stt_sb[:, 24:28],
                    op=ALU.max), waits=[e3], sig=sV)
                e5 = P.add("vector", lambda q: q.tensor_scalar(
                    out=stt_sb[:, 40:44], in0=stt_sb[:, 40:44],
                    scalar1=consts[:, 2:3], scalar2=None, op0=ALU.add),
                    waits=[e4, ldc], sig=sV)
                e6 = P.add("vector", lambda q: q.reciprocal(
                    stt_sb[:, 44:48], stt_sb[:, 40:44]), waits=[e5], sig=sV)
                e7 = P.add("vector", lambda q: q.tensor_tensor(
                    out=stt_sb[:, 20:24], in0=stt_sb[:, 20:24], in1=stt_sb[:, 44:48],
                    op=ALU.mult), waits=[e6], sig=sV)
                e8 = P.add("vector", lambda q: q.tensor_scalar(
                    out=stt_sb[:, 28:32], in0=stt_sb[:, 0:4], scalar1=consts[:, 3:4],
                    scalar2=None, op0=ALU.mult), waits=ep, sig=sV)
                e9 = P.add("vector", lambda q: q.tensor_tensor(
                    out=stt_sb[:, 48:52], in0=stt_sb[:, 28:32], in1=stt_sb[:, 28:32],
                    op=ALU.mult), waits=[e8], sig=sV)
                e10 = P.add("vector", lambda q: q.tensor_scalar(
                    out=stt_sb[:, 32:36], in0=stt_sb[:, 4:8], scalar1=consts[:, 3:4],
                    scalar2=None, op0=ALU.mult), waits=ep, sig=sV)
                e11 = P.add("vector", lambda q: q.tensor_tensor(
                    out=stt_sb[:, 32:36], in0=stt_sb[:, 32:36], in1=stt_sb[:, 48:52],
                    op=ALU.subtract), waits=[e9, e10], sig=sV)
                e12 = P.add("vector", lambda q: q.tensor_scalar(
                    out=stt_sb[:, 32:36], in0=stt_sb[:, 32:36], scalar1=0.0,
                    scalar2=None, op0=ALU.max), waits=[e11], sig=sV)
                e12a = P.add("vector", lambda q: q.tensor_scalar(
                    out=stt_sb[:, 48:52], in0=stt_sb[:, 32:36], scalar1=1e14,
                    scalar2=None, op0=ALU.mult), waits=[e12], sig=sV)
                e12b = P.add("scalar", lambda q: q.activation(
                    stt_sb[:, 48:52], stt_sb[:, 48:52], AF.Ln),
                    waits=[e12a], sig=sA)
                e12bb = P.add("vector", lambda q: q.tensor_scalar(
                    out=stt_sb[:, 48:52], in0=stt_sb[:, 48:52], scalar1=0.5,
                    scalar2=-16.11809565095832, op0=ALU.mult, op1=ALU.add),
                    waits=[e12b], sig=sV)
                e12c = P.add("scalar", lambda q: q.activation(
                    stt_sb[:, 48:52], stt_sb[:, 48:52], AF.Exp),
                    waits=[e12bb], sig=sA)
                e13 = P.add("vector", lambda q: q.tensor_tensor(
                    out=stt_sb[:, 48:52], in0=stt_sb[:, 20:24], in1=stt_sb[:, 48:52],
                    op=ALU.mult), waits=[e12c, e7], sig=sV)
                e13b = P.add("vector", lambda q: q.tensor_tensor(
                    out=stt_sb[:, 52:56], in0=stt_sb[:, 48:52], in1=stt_sb[:, 48:52],
                    op=ALU.mult), waits=[e13], sig=sV)
                e13c = P.add("vector", lambda q: q.tensor_scalar(
                    out=stt_sb[:, 52:56], in0=stt_sb[:, 52:56], scalar1=consts[:, 0:1],
                    scalar2=None, op0=ALU.add), waits=[e13b], sig=sV)
                e14 = P.add("scalar", lambda q: q.activation(
                    stt_sb[:, 52:56], stt_sb[:, 52:56], AF.Ln), waits=[e13c], sig=sA)
                e15 = P.add("scalar", lambda q: q.activation(
                    stt_sb[:, 52:56], stt_sb[:, 52:56], AF.Exp, scale=consts[:, 1:2]),
                    waits=[e14], sig=sA)
                e16 = P.add("vector", lambda q: q.tensor_tensor(
                    out=stt_sb[:, 36:40], in0=stt_sb[:, 20:24], in1=stt_sb[:, 52:56],
                    op=ALU.mult), waits=[e15], sig=sV)
                if dbg and (dmask & 8) and qb == 0 and h == 0:
                    for icd in range(4):
                        P.add("sync", lambda q, icd=icd: q.dma_start(
                            out=dbg_acc.ap()[icd], in_=acc[icd][:]),
                            waits=[acc_ev[h][icd]], sig=sL, n=16)
                dmp = None
                if dbg and (dmask & 1) and qb >= 6:
                    dmp = P.add("sync", lambda q, h=h, qb=qb: q.dma_start(
                        out=dbg_st2.ap()[qb - 6, h], in_=stt_sb[:]),
                        waits=[e16], sig=sL, n=16)
                if dbg and (dmask & 16) and qb == 0:
                    dmp = P.add("sync", lambda q, h=h: q.dma_start(
                        out=dbg_st2.ap()[0, h], in_=stt_sb[:]),
                        waits=[e16], sig=sL, n=16)
                last_apply = None
                for ic in range(4):
                    def f(q, ic=ic, h=h):
                        return q.tensor_scalar(
                            out=hatt[ic][:, 256 * h:256 * h + 256],
                            in0=acc[ic][:, 0:256],
                            scalar1=stt_sb[:, 28 + ic:29 + ic],
                            scalar2=stt_sb[:, 36 + ic:37 + ic],
                            op0=ALU.subtract, op1=ALU.mult)
                    w = [e16, dmp if (dwait & 1) else None]
                    if h == 0 and qb > 0:
                        w.append(xpost_st[4 * (qb - 1) + ic])
                    last_apply = P.add("vector", f, waits=w, sig=sV)

            for ic in range(4):
                t = 4 * qb + ic
                w = [stxbf[NT - 2 + (t % 2)], eu2, ldA[-1], gcol_evs[NT - 1],
                     gcol_evs[NT - 2]]
                if t >= 2:
                    w.append(xpost_st[t - 2])
                xold = P.add("sync", lambda q, t=t: q.dma_start(
                    out=xin[t % 2][:], in_=x_h[128 * t:128 * t + 128, :]),
                    waits=w, sig=sL, n=16)
                hdmp = None
                if dbg and (dmask & 2) and qb == 7:
                    hdmp = P.add("sync", lambda q, ic=ic: q.dma_start(
                        out=dbg_hat.ap()[ic], in_=hatt[ic][:]),
                        waits=[last_apply], sig=sL, n=16)
                gam = P.add("gpsimd", lambda q, ic=ic: q.tensor_tensor(
                    out=hatt[ic][:], in0=hatt[ic][:], in1=gbc_mh[:], op=ALU.mult),
                    waits=[last_apply, hdmp if (dwait & 2) else None, gmh[-1]], sig=sG)
                r1 = P.add("vector", lambda q, ic=ic, t=t: q.scalar_tensor_tensor(
                    out=hatt[ic][:], in0=hatt[ic][:], scalar=1.0, in1=xin[t % 2][:],
                    op0=ALU.mult, op1=ALU.add,
                    accum_out=stt_sb[:, 52 + ic:53 + ic]), waits=[gam, xold], sig=sV)
                r2 = P.add("scalar", lambda q, ic=ic: q.activation(
                    sqscr[:], hatt[ic][:], AF.Square,
                    accum_out=stt_sb[:, 56 + ic:57 + ic]), waits=[r1], sig=sA)
                r3 = P.add("vector", lambda q, ic=ic: q.tensor_scalar(
                    out=stt_sb[:, 60 + ic:61 + ic], in0=stt_sb[:, 52 + ic:53 + ic],
                    scalar1=consts[:, 4:5], scalar2=None, op0=ALU.mult),
                    waits=[r1], sig=sV)
                r4 = P.add("vector", lambda q, ic=ic: q.tensor_tensor(
                    out=stt_sb[:, 48 + ic:49 + ic], in0=stt_sb[:, 60 + ic:61 + ic],
                    in1=stt_sb[:, 60 + ic:61 + ic], op=ALU.mult), waits=[r3], sig=sV)
                r5 = P.add("vector", lambda q, ic=ic: q.tensor_scalar(
                    out=stt_sb[:, 44 + ic:45 + ic], in0=stt_sb[:, 56 + ic:57 + ic],
                    scalar1=consts[:, 4:5], scalar2=None, op0=ALU.mult),
                    waits=[r2], sig=sV)
                r6 = P.add("vector", lambda q, ic=ic: q.tensor_tensor(
                    out=stt_sb[:, 44 + ic:45 + ic], in0=stt_sb[:, 44 + ic:45 + ic],
                    in1=stt_sb[:, 48 + ic:49 + ic], op=ALU.subtract),
                    waits=[r4, r5], sig=sV)
                r7 = P.add("scalar", lambda q, ic=ic: q.activation(
                    stt_sb[:, 40 + ic:41 + ic], stt_sb[:, 44 + ic:45 + ic],
                    AF.Ln, bias=consts[:, 0:1]), waits=[r6], sig=sA)
                r8 = P.add("scalar", lambda q, ic=ic: q.activation(
                    stt_sb[:, 40 + ic:41 + ic], stt_sb[:, 40 + ic:41 + ic],
                    AF.Exp, scale=consts[:, 1:2]), waits=[r7], sig=sA)
                r9 = P.add("vector", lambda q, ic=ic: q.tensor_scalar(
                    out=hatt[ic][:], in0=hatt[ic][:],
                    scalar1=stt_sb[:, 60 + ic:61 + ic],
                    scalar2=stt_sb[:, 40 + ic:41 + ic],
                    op0=ALU.subtract, op1=ALU.mult), waits=[r8], sig=sV)
                xpost_st[t] = P.add("sync", lambda q, t=t, ic=ic: q.dma_start(
                    out=xpost_scr[128 * t:128 * t + 128, :], in_=hatt[ic][:]),
                    waits=[r9], sig=sL, n=16)

        if stage <= 25:
            if _cut():
                return nc
        bar0 = P.barrier(sW)

        tern_done = []
        mrep_evs = []
        for wi_, (src, dst, nch, nsub) in enumerate(
                [(w1_h, t1_scr, 32, 1), (w2_h, t2_scr, 8, 4)]):
            prev_use = {}
            last_acc = None
            for ch in range(nch):
                for sb_ in range(nsub):
                    k = ch * nsub + sb_
                    def fl(q, src=src, ch=ch, sb_=sb_, k=k):
                        return q.dma_start(
                            out=wstage[k % 2][:],
                            in_=src[128 * ch:128 * ch + 128, 1024 * sb_:1024 * sb_ + 1024])
                    w = list(bar0) if k == 0 else []
                    if k % 2 in prev_use:
                        w.append(prev_use[k % 2])
                    lde = P.add("scalar", fl, waits=w, sig=sL, n=16)
                    abe = P.add("scalar", lambda q, k=k: q.activation(
                        wabs[:], wstage[k % 2][:], AF.Abs, accum_out=wsum[:, 0:1]),
                        waits=[lde] + ([last_acc] if last_acc else list(bar0)), sig=sA)
                    def fac(q, wi_=wi_, k=k):
                        if k == 0:
                            return q.tensor_copy(wsum[:, 2 + wi_:3 + wi_], wsum[:, 0:1])
                        return q.tensor_tensor(
                            out=wsum[:, 2 + wi_:3 + wi_], in0=wsum[:, 2 + wi_:3 + wi_],
                            in1=wsum[:, 0:1], op=ALU.add)
                    last_acc = P.add("vector", fac, waits=[abe], sig=sV)
                    prev_use[k % 2] = last_acc
            t1e = P.add("tensor", lambda q, wi_=wi_: q.matmul(
                pq[0:1, 0:1], consts[:, 7:8], wsum[:, 2 + wi_:3 + wi_],
                start=True, stop=True), waits=[last_acc] + bar0, sig=sT)
            t2e = P.add("vector", lambda q: q.tensor_copy(
                wsum[0:1, 8:9], pq[0:1, 0:1]), waits=[t1e], sig=sV)
            t3e = P.add("vector", lambda q: q.reciprocal(
                wsum[0:1, 9:10], wsum[0:1, 8:9]), waits=[t2e], sig=sV)
            t4e = P.add("vector", lambda q: q.tensor_scalar(
                out=wsum[0:1, 10:11], in0=wsum[0:1, 9:10],
                scalar1=float(4096 * 1024), scalar2=None, op0=ALU.mult),
                waits=[t3e], sig=sV)
            t4b = P.add("vector", lambda q: q.tensor_scalar(
                out=wsum[0:1, 11:12], in0=wsum[0:1, 8:9],
                scalar1=1.0 / float(4096 * 1024), scalar2=None, op0=ALU.mult),
                waits=[t2e], sig=sV)
            t5e = P.add("gpsimd", lambda q: q.dma_start(
                out=msc_scr.ap()[0:1, 0:2], in_=wsum[0:1, 10:12]),
                waits=[t4e, t4b], sig=sL, n=16)
            t6e = P.add("sync", lambda q: q.dma_start(
                out=wsum[:, 12:13],
                in_=msc_scr.ap()[0:1, 0:1].to_broadcast((128, 1))),
                waits=[t5e], sig=sL, n=16)
            t6b = P.add("sync", lambda q, wi_=wi_: q.dma_start(
                out=mrep[:, wi_:wi_ + 1],
                in_=msc_scr.ap()[0:1, 1:2].to_broadcast((128, 1))),
                waits=[t5e], sig=sL, n=16)
            mrep_evs.append(t6b)
            prev2 = {}
            stv = None
            qe_prev = None
            for ch in range(nch):
                for sb_ in range(nsub):
                    k = ch * nsub + sb_
                    def fl(q, src=src, ch=ch, sb_=sb_, k=k):
                        return q.dma_start(
                            out=wstage[k % 2][:],
                            in_=src[128 * ch:128 * ch + 128, 1024 * sb_:1024 * sb_ + 1024])
                    w = [t6e] + ([prev2[k % 2]] if k % 2 in prev2 else [])
                    lde = P.add("scalar", fl, waits=w, sig=sL, n=16)
                    ce = None
                    for hf in range(2):
                        w0 = [lde] + ([qe_prev] if qe_prev else []) \
                            + ([stv] if (stv and hf == 0) else []) \
                            + ([xpost_st[NT - 1]] if k == 0 and hf == 0 else [])
                        ce = uround(wstage[k % 2][:, 512 * hf:512 * hf + 512],
                                    wsum[:, 12:13],
                                    wtern[:, 512 * hf:512 * hf + 512],
                                    w0, clamp=True)
                    qe_prev = ce
                    prev2[k % 2] = ce
                    stv = P.add("scalar", lambda q, dst=dst, ch=ch, sb_=sb_: q.dma_start(
                        out=dst[128 * ch:128 * ch + 128, 1024 * sb_:1024 * sb_ + 1024],
                        in_=wtern[:]), waits=[ce], sig=sL, n=16)
            tern_done.append(stv)

        if stage <= 26:
            if _cut():
                return nc
        barA = P.barrier(sW, waits=[xpost_st[NT - 1], tern_done[-1]])

        wt_evs = []
        for c in range(8):
            wt_evs.append(P.add("sync", lambda q, c=c: q.dma_start_transpose(
                W1t[c][:], t1_scr[:, 128 * c:128 * c + 128]), waits=barA, sig=sL, n=16))
        for c in range(32):
            wt_evs.append(P.add("sync", lambda q, c=c: q.dma_start_transpose(
                W2t[c][:], t2_scr[:, 128 * c:128 * c + 128]), waits=barA, sig=sL, n=16))

        bc_evs = []
        for (hsrc, dstt, width) in [
                (l1w_h, g1bc, 1024), (l1b_h, b1bc_s, 1024), (b2_h, bb2, 1024),
                (l2w_h, g2bc, 4096), (l2b_h, b2bc_s, 4096), (b1_h, bb1, 4096)]:
            bc_evs.append(P.add("gpsimd", lambda q, hsrc=hsrc, dstt=dstt, width=width:
                           q.dma_start(out=dstt[:],
                                       in_=hsrc[0:1, :].to_broadcast((128, width))),
                           waits=barA, sig=sL, n=16))
        bc_all = bc_evs[-1]

        def heron(col_t, col_s, iters, waits, seed_half):
            if seed_half:
                ev = P.add("vector", lambda q, a=col_t, b=col_s: q.tensor_scalar(
                    out=bstats[:, b:b + 1], in0=bstats[:, a:a + 1],
                    scalar1=0.5, scalar2=0.5, op0=ALU.mult, op1=ALU.add),
                    waits=waits, sig=sV)
            else:
                ev = P.add("vector", lambda q, a=col_t, b=col_s: q.tensor_scalar(
                    out=bstats[:, b:b + 1], in0=bstats[:, a:a + 1],
                    scalar1=0.0, scalar2=1.0, op0=ALU.mult, op1=ALU.add),
                    waits=waits, sig=sV)
            for _ in range(iters):
                ev = P.add("vector", lambda q, b=col_s: q.reciprocal(
                    bstats[:, b + 1:b + 2], bstats[:, b:b + 1]), waits=[ev], sig=sV)
                ev = P.add("vector", lambda q, a=col_t, b=col_s: q.tensor_tensor(
                    out=bstats[:, b + 1:b + 2], in0=bstats[:, a:a + 1],
                    in1=bstats[:, b + 1:b + 2], op=ALU.mult), waits=[ev], sig=sV)
                ev = P.add("vector", lambda q, b=col_s: q.tensor_tensor(
                    out=bstats[:, b:b + 1], in0=bstats[:, b:b + 1],
                    in1=bstats[:, b + 1:b + 2], op=ALU.add), waits=[ev], sig=sV)
                ev = P.add("vector", lambda q, b=col_s: q.tensor_scalar(
                    out=bstats[:, b:b + 1], in0=bstats[:, b:b + 1],
                    scalar1=0.5, scalar2=None, op0=ALU.mult), waits=[ev], sig=sV)
            ev = P.add("vector", lambda q, b=col_s: q.reciprocal(
                bstats[:, b:b + 1], bstats[:, b:b + 1]), waits=[ev], sig=sV)
            return ev

        def rsqrt_nr(col_t, col_s, waits):
            ct, cs = col_t, col_s
            ev = P.add("vector", lambda q: q.tensor_scalar(
                out=scri[:, 0:1],
                in0=bstats[:, ct:ct + 1].bitcast(I32),
                scalar1=1, scalar2=None, op0=ALU.logical_shift_right),
                waits=waits, sig=sV)
            ev = P.add("vector", lambda q: q.tensor_scalar(
                out=bstats[:, cs:cs + 1], in0=scri[:, 0:1].bitcast(F32),
                scalar1=1.2665333395337613e19, scalar2=None, op0=ALU.mult),
                waits=[ev], sig=sV)
            ev = P.add("vector", lambda q: q.reciprocal(
                bstats[:, cs + 1:cs + 2], bstats[:, cs:cs + 1]),
                waits=[ev], sig=sV)
            for _ in range(4):
                ev = P.add("vector", lambda q: q.tensor_tensor(
                    out=bstats[:, cs:cs + 1], in0=bstats[:, cs + 1:cs + 2],
                    in1=bstats[:, cs + 1:cs + 2], op=ALU.mult), waits=[ev], sig=sV)
                ev = P.add("vector", lambda q: q.tensor_tensor(
                    out=bstats[:, cs:cs + 1], in0=bstats[:, cs:cs + 1],
                    in1=bstats[:, ct:ct + 1], op=ALU.mult), waits=[ev], sig=sV)
                ev = P.add("vector", lambda q: q.tensor_scalar(
                    out=bstats[:, cs:cs + 1], in0=bstats[:, cs:cs + 1],
                    scalar1=-0.5, scalar2=1.5, op0=ALU.mult, op1=ALU.add),
                    waits=[ev], sig=sV)
                ev = P.add("vector", lambda q: q.tensor_tensor(
                    out=bstats[:, cs + 1:cs + 2], in0=bstats[:, cs + 1:cs + 2],
                    in1=bstats[:, cs:cs + 1], op=ALU.mult), waits=[ev], sig=sV)
            ev = P.add("vector", lambda q: q.tensor_scalar(
                out=bstats[:, cs:cs + 1], in0=bstats[:, cs + 1:cs + 2],
                scalar1=1.0, scalar2=None, op0=ALU.mult), waits=[ev], sig=sV)
            return ev

        xp_free = [None, None]
        q1_free = [None, None]
        h1_free = None
        q2_free = None
        scr_free = None
        ptr_free = None
        pb_po_free = list(po_free)
        pb_ps_free = list(ps_free)
        out_st = [None] * NT

        st_fap = [None] * NT
        st_s2 = [None] * NT
        st_mmh1 = [None] * NT
        st_q1c = [None] * NT
        st_gel = [None] * NT
        st_mmh0 = [None] * NT
        st_p1 = [None] * NT
        st_q2e = [None] * NT
        st_q2c = [None] * NT
        st_mm2 = [None] * NT
        st_f1e = [None] * NT
        st_f2e = [None] * NT
        st_fr2 = [None] * NT
        st_qb16 = [None] * NT
        out_st = [None] * NT

        def S1(t):
            pr = t % 2
            lxp = P.add("sync", lambda q, t=t, pr=pr: q.dma_start(
                out=xp[pr][:], in_=xpost_scr[128 * t:128 * t + 128, :]),
                waits=[xpost_st[t]] + barA +
                      ([st_fap[t - 2]] if t >= 2 else []), sig=sL, n=16)
            b1e = P.add("vector", lambda q, pr=pr: q.reduce_sum(
                bstats[:, 0:1], xp[pr][:], AX.X), waits=[lxp] + barA, sig=sV)
            b2e = P.add("scalar", lambda q, pr=pr: q.activation(
                scr32[:], xp[pr][:], AF.Square, accum_out=bstats[:, 1:2]),
                waits=[lxp], sig=sA)
            b3e = P.add("vector", lambda q: q.tensor_scalar(
                out=bstats[:, 2:3], in0=bstats[:, 0:1], scalar1=consts[:, 4:5],
                scalar2=None, op0=ALU.mult), waits=[b1e], sig=sV)
            b4e = P.add("vector", lambda q: q.tensor_tensor(
                out=bstats[:, 3:4], in0=bstats[:, 2:3], in1=bstats[:, 2:3],
                op=ALU.mult), waits=[b3e], sig=sV)
            b5e = P.add("vector", lambda q: q.tensor_scalar(
                out=bstats[:, 4:5], in0=bstats[:, 1:2], scalar1=consts[:, 4:5],
                scalar2=None, op0=ALU.mult), waits=[b2e], sig=sV)
            b6e = P.add("vector", lambda q: q.tensor_tensor(
                out=bstats[:, 4:5], in0=bstats[:, 4:5], in1=bstats[:, 3:4],
                op=ALU.subtract), waits=[b4e, b5e], sig=sV)
            b7e = P.add("vector", lambda q: q.tensor_scalar(
                out=bstats[:, 5:6], in0=bstats[:, 4:5], scalar1=consts[:, 0:1],
                scalar2=None, op0=ALU.add), waits=[b6e], sig=sV)
            rstd1 = heron(5, 6, 2, [b7e], seed_half=False)
            a1 = P.add("vector", lambda q, pr=pr: q.tensor_scalar(
                out=hln[:], in0=xp[pr][:], scalar1=bstats[:, 2:3],
                scalar2=bstats[:, 6:7], op0=ALU.subtract, op1=ALU.mult),
                waits=[rstd1, b2e] +
                      ([st_gel[t - 1][-1]] if t >= 1 else []), sig=sV, basic=True)
            a2 = P.add("vector", lambda q: q.tensor_tensor(
                out=hln[:], in0=hln[:], in1=g1bc[:], op=ALU.mult),
                waits=[a1, bc_all], sig=sV, basic=True)
            a3 = P.add("vector", lambda q: q.tensor_tensor(
                out=hln[:], in0=hln[:], in1=b1bc_s[:], op=ALU.add),
                waits=[a2], sig=sV, basic=True)
            qm = P.add("vector", lambda q: q.tensor_reduce(
                bstats[:, 8:9], hln[:], AX.X, ALU.max, apply_absolute_value=True),
                waits=[a3], sig=sV)
            qs1 = P.add("vector", lambda q: q.tensor_scalar(
                out=bstats[:, 9:10], in0=bstats[:, 8:9], scalar1=consts[:, 0:1],
                scalar2=None, op0=ALU.max), waits=[qm], sig=sV)
            qs2 = P.add("vector", lambda q: q.reciprocal(
                bstats[:, 10:11], bstats[:, 9:10]), waits=[qs1], sig=sV)
            qs3 = P.add("vector", lambda q: q.tensor_scalar(
                out=bstats[:, 11:12], in0=bstats[:, 10:11], scalar1=consts[:, 6:7],
                scalar2=None, op0=ALU.mult), waits=[qs2], sig=sV)
            f1a = P.add("vector", lambda q, pr=pr: q.tensor_scalar(
                out=fcol[:, pr:pr + 1], in0=bstats[:, 9:10], scalar1=1.0 / 127.0,
                scalar2=None, op0=ALU.mult), waits=[qs1], sig=sV)
            f1e = P.add("vector", lambda q, pr=pr: q.tensor_tensor(
                out=fcol[:, pr:pr + 1], in0=fcol[:, pr:pr + 1], in1=mrep[:, 0:1],
                op=ALU.mult), waits=[f1a, mrep_evs[0]], sig=sV)
            st_f1e[t] = f1e
            qprev = None
            for cc in range(2):
                w0 = [qs3] + ([qprev] if qprev else [])
                if cc == 0 and t >= 2 and out_st[t - 2] is not None:
                    w0.append(out_st[t - 2])
                qprev = uround_bs(hln[:, 512 * cc:512 * cc + 512], bstats[:, 11:12],
                                  q1[pr][:, 512 * cc:512 * cc + 512], w0)
            st_qb16[t] = qprev
            tr_evs = []
            for g4 in range(2):
                tes = []
                for cc in range(4):
                    c = 4 * g4 + cc
                    te = P.add("tensor", lambda q, c=c, cc=cc, pr=pr: q.transpose(
                        ptr[:, 128 * cc:128 * cc + 128],
                        q1[pr][:, 128 * c:128 * c + 128], identb[:]),
                        waits=[qprev, ldib] + (tr_evs[-1:] if g4 > 0 else
                              ([st_q1c[t - 1]] if t >= 1 else [])),
                        sig=sT)
                    tes.append(te)
                tr_evs.append(P.add("scalar", lambda q, g4=g4, pr=pr: q.activation(
                    q1[pr][:, 512 * g4:512 * g4 + 512], ptr[:, 0:512], AF.Copy),
                    waits=tes, sig=sA))
            st_q1c[t] = tr_evs[-1]

        def MM1H0(t):
            pr = t % 2
            mm_last = {}
            for k in range(8):
                for hc2 in range(4):
                    def f(q, k=k, hc2=hc2, pr=pr):
                        return q.matmul(
                            po[hc2][:], q1[pr][:, 128 * k:128 * k + 128],
                            W1t[k][:, 512 * hc2:512 * hc2 + 512],
                            start=(k == 0), stop=(k == 7))
                    w = []
                    if k == 0:
                        w = [st_q1c[t]]
                        if t == 0:
                            w.append(wt_evs[7])
                        if pb_po_free[hc2] is not None:
                            w.append(pb_po_free[hc2])
                    ev = P.add("tensor", f, waits=w, sig=sT if k == 7 else None)
                    if k == 7:
                        mm_last[hc2] = ev
            st_mmh0[t] = mm_last

        def S2(t):
            pr = t % 2
            gels = []

            def evac_gelu(hc2, hc, mm_ev):
                sl = 512 * (hc % 2)
                w = [mm_ev, st_f1e[t], bc_all]
                if hc == 0:
                    w.append(st_qb16[t])
                    if t >= 1:
                        w.append(st_gel[t - 1][6])
                elif hc == 1 and t >= 1:
                    w.append(st_gel[t - 1][7])
                if len(gels) >= 2:
                    w.append(gels[-2])
                e1 = P.add("vector", lambda q, hc2=hc2, hc=hc, sl=sl, pr=pr:
                           q.scalar_tensor_tensor(
                               out=scr32[:, sl:sl + 512], in0=po[hc2][:],
                               scalar=fcol[:, pr:pr + 1],
                               in1=bb1[:, 512 * hc:512 * hc + 512],
                               op0=ALU.mult, op1=ALU.add), waits=w, sig=sV)
                pb_po_free[hc2] = e1
                gw_ = [e1]
                if hc == 0 and t >= 1 and st_p1[t - 1] is not None:
                    gw_.append(st_p1[t - 1])
                ge = P.add("scalar", lambda q, hc=hc, sl=sl: q.activation(
                    h1[:, 512 * hc:512 * hc + 512], scr32[:, sl:sl + 512],
                    AF.Gelu, accum_out=bstats[:, 16 + hc:17 + hc]),
                    waits=gw_, sig=sA)
                sq = P.add("scalar", lambda q, hc=hc: q.activation(
                    sqb[:, 512 * hc:512 * hc + 512], h1[:, 512 * hc:512 * hc + 512],
                    AF.Square, accum_out=bstats[:, 48 + hc:49 + hc]),
                    waits=[ge], sig=sA)
                gels.append(sq)

            for hc2 in range(4):
                evac_gelu(hc2, hc2, st_mmh0[t][hc2])
            st_s2[t] = (gels, evac_gelu)

        def MM1H1(t):
            pr = t % 2
            mm_last = {}
            for k in range(8):
                for hc2 in range(4):
                    hc = 4 + hc2
                    def f(q, k=k, hc2=hc2, hc=hc, pr=pr):
                        return q.matmul(
                            po[hc2][:], q1[pr][:, 128 * k:128 * k + 128],
                            W1t[k][:, 512 * hc:512 * hc + 512],
                            start=(k == 0), stop=(k == 7))
                    w = []
                    if k == 0:
                        w = [pb_po_free[hc2]]
                    ev = P.add("tensor", f, waits=w, sig=sT if k == 7 else None)
                    if k == 7:
                        mm_last[hc2] = ev
            st_mmh1[t] = mm_last

        def S2b(t):
            gels, evac_gelu = st_s2[t]
            for hc2 in range(4):
                evac_gelu(hc2, 4 + hc2, st_mmh1[t][hc2])
            st_gel[t] = gels

        def S3(t):
            pr = t % 2
            l2s1 = P.add("vector", lambda q: q.reduce_sum(
                bstats[:, 24:25], bstats[:, 16:24], AX.X), waits=st_gel[t], sig=sV)
            l2s2 = P.add("vector", lambda q: q.reduce_sum(
                bstats[:, 25:26], bstats[:, 48:56], AX.X),
                waits=st_gel[t], sig=sV)
            l2s3 = P.add("vector", lambda q: q.tensor_scalar(
                out=bstats[:, 26:27], in0=bstats[:, 24:25], scalar1=consts[:, 5:6],
                scalar2=None, op0=ALU.mult), waits=[l2s1], sig=sV)
            l2s4 = P.add("vector", lambda q: q.tensor_tensor(
                out=bstats[:, 27:28], in0=bstats[:, 26:27], in1=bstats[:, 26:27],
                op=ALU.mult), waits=[l2s3], sig=sV)
            l2s5 = P.add("vector", lambda q: q.tensor_scalar(
                out=bstats[:, 28:29], in0=bstats[:, 25:26], scalar1=consts[:, 5:6],
                scalar2=None, op0=ALU.mult), waits=[l2s2], sig=sV)
            l2s6 = P.add("vector", lambda q: q.tensor_tensor(
                out=bstats[:, 28:29], in0=bstats[:, 28:29], in1=bstats[:, 27:28],
                op=ALU.subtract), waits=[l2s4, l2s5], sig=sV)
            l2s7 = P.add("vector", lambda q: q.tensor_scalar(
                out=bstats[:, 29:30], in0=bstats[:, 28:29], scalar1=consts[:, 0:1],
                scalar2=1e-6, op0=ALU.add, op1=ALU.max), waits=[l2s6], sig=sV)
            rstd2 = rsqrt_nr(29, 30, [l2s7])
            p1 = P.add("vector", lambda q: q.tensor_scalar(
                out=h2b[:], in0=h1[:], scalar1=bstats[:, 26:27],
                scalar2=bstats[:, 30:31], op0=ALU.subtract, op1=ALU.mult),
                waits=[rstd2, l2s2] +
                      ([st_fr2[t - 1]] if t >= 1 and st_fr2[t - 1] else []), sig=sV, basic=True)
            st_p1[t] = p1
            p2 = P.add("vector", lambda q: q.tensor_tensor(
                out=h2b[:], in0=h2b[:], in1=g2bc[:], op=ALU.mult),
                waits=[p1, bc_all], sig=sV, basic=True)
            p3 = P.add("vector", lambda q: q.tensor_tensor(
                out=h2b[:], in0=h2b[:], in1=b2bc_s[:], op=ALU.add),
                waits=[p2], sig=sV, basic=True)
            q2m = P.add("vector", lambda q: q.tensor_reduce(
                bstats[:, 32:33], h2b[:], AX.X, ALU.max, apply_absolute_value=True),
                waits=[p3], sig=sV)
            q2s1 = P.add("vector", lambda q: q.tensor_scalar(
                out=bstats[:, 33:34], in0=bstats[:, 32:33], scalar1=consts[:, 0:1],
                scalar2=None, op0=ALU.max), waits=[q2m], sig=sV)
            q2s2 = P.add("vector", lambda q: q.reciprocal(
                bstats[:, 34:35], bstats[:, 33:34]), waits=[q2s1], sig=sV)
            q2s3 = P.add("vector", lambda q: q.tensor_scalar(
                out=bstats[:, 35:36], in0=bstats[:, 34:35], scalar1=consts[:, 6:7],
                scalar2=None, op0=ALU.mult), waits=[q2s2], sig=sV)
            f2a = P.add("vector", lambda q, pr=pr: q.tensor_scalar(
                out=fcol[:, 2 + pr:3 + pr], in0=bstats[:, 33:34], scalar1=1.0 / 127.0,
                scalar2=None, op0=ALU.mult), waits=[q2s1], sig=sV)
            f2e = P.add("vector", lambda q, pr=pr: q.tensor_tensor(
                out=fcol[:, 2 + pr:3 + pr], in0=fcol[:, 2 + pr:3 + pr],
                in1=mrep[:, 1:2], op=ALU.mult), waits=[f2a, mrep_evs[1]], sig=sV)
            st_f2e[t] = f2e
            qprev = None
            for cc in range(8):
                w0 = [q2s3] + ([qprev] if qprev else [])
                if cc == 0 and t >= 1:
                    w0.append(st_mm2[t - 1][1])
                qprev = uround_bs(h2b[:, 512 * cc:512 * cc + 512], bstats[:, 35:36],
                                  q2[:, 512 * cc:512 * cc + 512], w0)
            st_q2e[t] = qprev
            q2f_evs = []
            for g4 in range(8):
                tes = []
                for cc in range(4):
                    c = 4 * g4 + cc
                    te = P.add("tensor", lambda q, c=c, cc=cc: q.transpose(
                        ptr[:, 512 + 128 * cc:512 + 128 * cc + 128],
                        q2[:, 128 * c:128 * c + 128], identb[:]),
                        waits=[qprev, ldib] + (q2f_evs[-1:] if g4 > 0 else
                              ([st_q2c[t - 1][-1]] if t >= 1 else [])),
                        sig=sT)
                    tes.append(te)
                q2f_evs.append(P.add("scalar", lambda q, g4=g4: q.activation(
                    q2[:, 512 * g4:512 * g4 + 512], ptr[:, 512:1024], AF.Copy),
                    waits=tes, sig=sA))
            st_q2c[t] = q2f_evs

        def S4a(t):
            mm_last = {}
            for c in range(32):
                for nn in range(2):
                    def f(q, c=c, nn=nn):
                        return q.matmul(
                            ps[nn][:], q2[:, 128 * c:128 * c + 128],
                            W2t[c][:, 512 * nn:512 * nn + 512],
                            start=(c == 0), stop=(c == 31))
                    w = [st_q2c[t][c // 4]]
                    if c == 0:
                        if t == 0:
                            w.append(wt_evs[-1])
                        if pb_ps_free[nn] is not None:
                            w.append(pb_ps_free[nn])
                    ev = P.add("tensor", f, waits=w, sig=sT if c == 31 else None)
                    if c == 31:
                        mm_last[nn] = ev
            st_mm2[t] = mm_last

        def S4b(t):
            pr = t % 2
            ev1 = {}
            for nn in range(2):
                ev1[nn] = P.add("vector", lambda q, nn=nn, pr=pr:
                                q.scalar_tensor_tensor(
                                    out=xp[pr][:, 512 * nn:512 * nn + 512],
                                    in0=ps[nn][:], scalar=fcol[:, 2 + pr:3 + pr],
                                    in1=xp[pr][:, 512 * nn:512 * nn + 512],
                                    op0=ALU.mult, op1=ALU.add),
                                waits=[st_mm2[t][nn], st_f2e[t]], sig=sV)
                pb_ps_free[nn] = ev1[nn]
            frb = P.add("vector", lambda q, pr=pr: q.scalar_tensor_tensor(
                out=xp[pr][:], in0=xp[pr][:], scalar=1.0, in1=bb2[:],
                op0=ALU.mult, op1=ALU.add, accum_out=bstats[:, 40:41]),
                waits=[ev1[0], ev1[1], bc_all], sig=sV)
            fr2 = P.add("scalar", lambda q, pr=pr: q.activation(
                sqb[:, 0:1024], xp[pr][:], AF.Square, accum_out=bstats[:, 41:42]),
                waits=[frb], sig=sA)
            st_fr2[t] = fr2
            fr3 = P.add("vector", lambda q: q.tensor_scalar(
                out=bstats[:, 42:43], in0=bstats[:, 40:41], scalar1=consts[:, 4:5],
                scalar2=None, op0=ALU.mult), waits=[frb], sig=sV)
            fr4 = P.add("vector", lambda q: q.tensor_tensor(
                out=bstats[:, 43:44], in0=bstats[:, 42:43], in1=bstats[:, 42:43],
                op=ALU.mult), waits=[fr3], sig=sV)
            fr5 = P.add("vector", lambda q: q.tensor_scalar(
                out=bstats[:, 44:45], in0=bstats[:, 41:42], scalar1=consts[:, 4:5],
                scalar2=None, op0=ALU.mult), waits=[fr2], sig=sV)
            fr6 = P.add("vector", lambda q: q.tensor_tensor(
                out=bstats[:, 44:45], in0=bstats[:, 44:45], in1=bstats[:, 43:44],
                op=ALU.subtract), waits=[fr4, fr5], sig=sV)
            fr7 = P.add("vector", lambda q: q.tensor_scalar(
                out=bstats[:, 45:46], in0=bstats[:, 44:45], scalar1=consts[:, 0:1],
                scalar2=1e-6, op0=ALU.add, op1=ALU.max), waits=[fr6], sig=sV)
            rstdf = rsqrt_nr(45, 46, [fr7])
            fap = P.add("vector", lambda q, pr=pr: q.tensor_scalar(
                out=q1[pr][:], in0=xp[pr][:], scalar1=bstats[:, 42:43],
                scalar2=bstats[:, 46:47], op0=ALU.subtract, op1=ALU.mult),
                waits=[rstdf, fr2], sig=sV)
            st_fap[t] = fap
            out_st[t] = P.add("gpsimd", lambda q, t=t, pr=pr: q.dma_start(
                out=out_h[128 * t:128 * t + 128, :], in_=q1[pr][:]),
                waits=[fap], sig=sO, n=16)

        S1(0)
        MM1H0(0)
        for t in range(NT):
            S2(t)
            MM1H1(t)
            if t >= 1:
                S4b(t - 1)
            S2b(t)
            if t + 1 < NT:
                S1(t + 1)
                MM1H0(t + 1)
            S3(t)
            S4a(t)
        S4b(NT - 1)

        fin = P.cnt[id(sO)]
        P.add("sync", lambda q: q.wait_ge(sO, fin))
        for s_ in (sLS, sLA, sLG):
            v = P.cnt.get(id(s_), 0)
            if v:
                P.add("sync", lambda q, s_=s_, v=v: q.wait_ge(s_, v))
        P.emit(block)
    return nc


_NC_CACHE = {}


def kernel(L0_attn_x, L0_attn_wq, L0_attn_wk, L0_attn_wv, L0_attn_wi, L0_attn_bi,
           L0_attn_wf, L0_attn_bf, L0_attn_ln_w, L0_ffn_ln1_w, L0_ffn_ln1_b,
           L0_ffn_w1, L0_ffn_b1, L0_ffn_ln2_w, L0_ffn_ln2_b, L0_ffn_w2, L0_ffn_b2):
    if "nc" not in _NC_CACHE:
        _NC_CACHE["nc"] = build_program()
    nc = _NC_CACHE["nc"]
    x = np.ascontiguousarray(np.asarray(L0_attn_x, dtype=np.float32))
    com = {
        "wq": np.ascontiguousarray(np.asarray(L0_attn_wq, np.float32).reshape(256, 16)),
        "wk": np.ascontiguousarray(np.asarray(L0_attn_wk, np.float32).reshape(256, 16)),
        "wv": np.ascontiguousarray(np.asarray(L0_attn_wv, np.float32).reshape(256, 16)),
        "wi": np.ascontiguousarray(np.asarray(L0_attn_wi, np.float32)),
        "bi": np.ascontiguousarray(np.asarray(L0_attn_bi, np.float32).reshape(4, 1)),
        "wf": np.ascontiguousarray(np.asarray(L0_attn_wf, np.float32)),
        "bf": np.ascontiguousarray(np.asarray(L0_attn_bf, np.float32).reshape(4, 1)),
        "lnw": np.ascontiguousarray(np.asarray(L0_attn_ln_w, np.float32).reshape(1, D)),
        "l1w": np.ascontiguousarray(np.asarray(L0_ffn_ln1_w, np.float32).reshape(1, D)),
        "l1b": np.ascontiguousarray(np.asarray(L0_ffn_ln1_b, np.float32).reshape(1, D)),
        "w1": np.ascontiguousarray(np.asarray(L0_ffn_w1, np.float32)),
        "b1": np.ascontiguousarray(np.asarray(L0_ffn_b1, np.float32).reshape(1, 4096)),
        "l2w": np.ascontiguousarray(np.asarray(L0_ffn_ln2_w, np.float32).reshape(1, 4096)),
        "l2b": np.ascontiguousarray(np.asarray(L0_ffn_ln2_b, np.float32).reshape(1, 4096)),
        "w2": np.ascontiguousarray(np.asarray(L0_ffn_w2, np.float32)),
        "b2": np.ascontiguousarray(np.asarray(L0_ffn_b2, np.float32).reshape(1, D)),
    }
    in_maps = [dict(com, x=np.ascontiguousarray(x[b])) for b in range(8)]
    res = run_bass_kernel_spmd(nc, in_maps, core_ids=list(range(8)))
    out = np.stack([res.results[b]["out"] for b in range(8)], axis=0)
    return out
